# version 05
# speedup vs baseline: 1.2656x; 1.0101x over previous
import contextlib
import numpy as np
import ml_dtypes
import concourse.bass as bass
import concourse.mybir as mybir
from concourse.bass_utils import run_bass_kernel_spmd
from concourse.alu_op_type import AluOpType as aop

F32 = mybir.dt.float32
BF16 = mybir.dt.bfloat16
AF = mybir.ActivationFunctionType
AX = mybir.AxisListType

NCORES = 8
T = 2048
D = 1024
DKT = D // 128
BF = ml_dtypes.bfloat16

LAYERS = ["l0a", "l0f", "l1a", "l1f", "l2a", "l2f", "l3a", "l3f"]
FUSE = True
DEBUG_BARRIER = False


class Buf:
    __slots__ = ("name", "w", "readers", "guard")

    def __init__(self, name, guard=()):
        self.name = name
        self.w = None
        self.readers = []
        self.guard = list(guard)


def toks_of(bufs):
    out = []
    for b in bufs:
        if b.w is not None:
            out.append(b.w)
        out.extend(b.readers)
        out.extend(b.guard)
    return out


class K:
    ENGS = ("t", "v", "s", "p", "y")

    def __init__(self, nc, es):
        self.nc = nc
        self.sem = {e: es.enter_context(nc.semaphore(f"sem_{e}")) for e in self.ENGS}
        self.cnt = {e: 0 for e in self.ENGS}
        self.dsem = {q: es.enter_context(nc.semaphore(f"dsem_{q}")) for q in ("y", "p", "s")}
        self.dcnt = {q: 0 for q in ("y", "p", "s")}
        self.waited = {e: {} for e in self.ENGS}
        self.prog = {e: [] for e in self.ENGS}

    def _wait(self, e, tok):
        key, handle, val = tok
        if self.waited[e].get(key, 0) >= val:
            return
        self.waited[e][key] = val
        self.prog[e].append(lambda eng, h=handle, v=val: eng.wait_ge(h, v))

    def _deps(self, e, reads, writes, extra):
        deps = []
        for b in reads:
            if b.w is not None:
                deps.append(b.w)
            deps.extend(b.guard)
        deps.extend(toks_of(writes))
        deps.extend(extra)
        best = {}
        for tok in deps:
            key = tok[0]
            if key not in best or best[key][2] < tok[2]:
                best[key] = tok
        for tok in best.values():
            self._wait(e, tok)

    def _post(self, tok, reads, writes):
        for b in reads:
            b.readers.append(tok)
        for b in writes:
            b.w = tok
            b.readers = []
            b.guard = []

    def op(self, e, emit, reads=(), writes=(), extra=()):
        if DEBUG_BARRIER:
            extra = list(extra) + self.barrier_tokens()
        self._deps(e, reads, writes, extra)
        self.cnt[e] += 1
        sem = self.sem[e]
        self.prog[e].append(lambda eng, f=emit, s=sem: f(eng).then_inc(s, 1))
        tok = (("e", e), sem, self.cnt[e])
        self._post(tok, reads, writes)
        return tok

    def group(self, e, emits, reads=(), writes=(), extra=()):
        if DEBUG_BARRIER:
            extra = list(extra) + self.barrier_tokens()
        self._deps(e, reads, writes, extra)
        for f in emits[:-1]:
            self.prog[e].append(lambda eng, f=f: f(eng))
        self.cnt[e] += 1
        sem = self.sem[e]
        last = emits[-1]
        self.prog[e].append(lambda eng, f=last, s=sem: f(eng).then_inc(s, 1))
        tok = (("e", e), sem, self.cnt[e])
        self._post(tok, reads, writes)
        return tok

    def dma(self, q, out_ap, in_ap, reads=(), writes=(), extra=()):
        if DEBUG_BARRIER:
            extra = list(extra) + self.barrier_tokens()
        self._deps(q, reads, writes, extra)
        self.dcnt[q] += 1
        dsem = self.dsem[q]
        self.prog[q].append(
            lambda eng, o=out_ap, i=in_ap, s=dsem: eng.dma_start(out=o, in_=i).then_inc(s, 16))
        tok = (("d", q), dsem, 16 * self.dcnt[q])
        self._post(tok, reads, writes)
        return tok

    def barrier_tokens(self):
        toks = []
        for e in self.ENGS:
            if self.cnt[e]:
                toks.append((("e", e), self.sem[e], self.cnt[e]))
        for q in self.dsem:
            if self.dcnt[q]:
                toks.append((("d", q), self.dsem[q], 16 * self.dcnt[q]))
        return toks

    def finalize(self):
        nc = self.nc
        with nc.Block() as block:
            for e in self.ENGS:
                items = self.prog[e]
                if not items:
                    continue
                dec = {"t": block.tensor, "v": block.vector, "s": block.scalar,
                       "p": block.gpsimd, "y": block.sync}[e]

                def body(eng, items=items):
                    for f in items:
                        f(eng)
                dec(body)


def _t(a):
    return np.ascontiguousarray(np.asarray(a).T)


def _cols(vec, ntiles):
    return np.ascontiguousarray(np.asarray(vec).reshape(ntiles, 128).T)


def _sm(WT, nkt, not_):
    WT = np.asarray(WT)
    return np.ascontiguousarray(
        WT.reshape(nkt, 128, not_, 128).transpose(2, 1, 0, 3)
        .reshape(not_, 128, nkt * 128))


def _pk(WT, nkt):
    WT = np.asarray(WT)
    K, O = WT.shape
    return np.ascontiguousarray(WT.reshape(nkt, 128, O).transpose(1, 0, 2).reshape(128, nkt * O))


def _slabcols(mat, nslab, nn):
    m = np.asarray(mat).reshape(nslab, 128, nn)
    return np.ascontiguousarray(m.transpose(1, 0, 2).reshape(128, nslab * nn))


def prep_weights(inp):
    w = {}
    w["ident_b"] = np.eye(128, dtype=BF)
    w["ident_f"] = np.eye(128, dtype=np.float32)
    w["ones_b"] = np.ones((128, 128), dtype=BF)
    w["ones_f"] = np.ones((128, 128), dtype=np.float32)
    w["l0a_wgT"] = _sm(_t(inp["L0_attn_W_gate"]), 8, 16)
    w["l0a_winT"] = _sm(_t(inp["L0_attn_W_in"]), 8, 32)
    w["l0a_woT"] = _sm(_t(inp["L0_attn_W_out"]), 16, 8)
    w["l0a_bg"] = _cols(inp["L0_attn_b_gate"], 16)
    w["l0a_bin"] = _cols(inp["L0_attn_b_in"], 32)
    w["l0a_bo"] = _cols(inp["L0_attn_b_out"], 8)
    w["l0a_alog"] = _slabcols(inp["L0_attn_A_log"], 16, 16)
    w["l0a_bmat"] = _slabcols(inp["L0_attn_B_mat"], 16, 16)
    w["l0a_c0"] = np.ascontiguousarray(np.asarray(inp["L0_attn_C_mat"])[0:1, :])
    w["l0a_dv"] = _cols(inp["L0_attn_D_vec"], 16)
    w["l0f_w1"] = _sm(inp["L0_ffn_W1"], 8, 32)
    w["l0f_b1"] = _cols(inp["L0_ffn_b1"], 32)
    w["l0f_w2"] = _sm(inp["L0_ffn_W2"], 32, 8)
    w["l0f_b2"] = _cols(inp["L0_ffn_b2"], 8)
    w["l1a_wq"] = _pk(inp["L1_attn_wq"], 8)
    w["l1a_wk"] = _pk(inp["L1_attn_wk"], 8)
    w["l1a_wv"] = _pk(inp["L1_attn_wv"], 8)
    w["l1a_wo"] = np.ascontiguousarray(inp["L1_attn_wo"])
    w["l1a_bq"] = np.asarray(inp["L1_attn_bq"]).reshape(64, 1)
    w["l1a_bk"] = np.asarray(inp["L1_attn_bk"]).reshape(64, 1)
    w["l1a_bv"] = np.asarray(inp["L1_attn_bv"]).reshape(1, 64)
    w["l1a_bo"] = _cols(inp["L1_attn_bo"], 8)
    wg = np.concatenate([np.asarray(inp["L1_ffn_Wg_s"])[None], np.asarray(inp["L1_ffn_Wg"])], 0)
    wu = np.concatenate([np.asarray(inp["L1_ffn_Wu_s"])[None], np.asarray(inp["L1_ffn_Wu"])], 0)
    wd = np.concatenate([np.asarray(inp["L1_ffn_Wd_s"])[None], np.asarray(inp["L1_ffn_Wd"])], 0)
    w["l1f_wgT"] = np.stack([_sm(wg[e].T, 8, 16) for e in range(9)])
    w["l1f_wuT"] = np.stack([_sm(wu[e].T, 8, 16) for e in range(9)])
    w["l1f_wdT"] = np.stack([_sm(wd[e].T, 16, 8) for e in range(9)])
    w["l1f_wrT"] = _pk(_t(inp["L1_ffn_Wr"]), 8)
    w["l1f_rb"] = np.asarray(inp["L1_ffn_rb"]).reshape(8, 1)
    w["l2a_tsg"] = _cols(inp["L2_attn_time_shift_gates"], 8)
    w["l2a_lnw"] = _cols(inp["L2_attn_ln_w"], 8)
    w["l2a_lnb"] = _cols(inp["L2_attn_ln_b"], 8)
    w["l2a_kwT"] = _pk(_t(inp["L2_attn_key_w"]), 8)
    w["l2a_vwT"] = _pk(_t(inp["L2_attn_value_w"]), 8)
    w["l2a_owT"] = _t(inp["L2_attn_output_w"])
    w["l2a_td"] = np.asarray(inp["L2_attn_time_decay"]).reshape(64, 1)
    w["l2a_tf"] = np.asarray(inp["L2_attn_time_first"]).reshape(64, 1)
    H2, H2P = 2736, 2816
    wg2 = np.zeros((8, 1024, H2P), np.float32)
    w12 = np.zeros((8, 1024, H2P), np.float32)
    w22 = np.zeros((8, H2P, 1024), np.float32)
    bg2 = np.zeros((8, H2P), np.float32)
    b12 = np.zeros((8, H2P), np.float32)
    wg2[:, :, :H2] = inp["L2_ffn_Wg"]
    w12[:, :, :H2] = inp["L2_ffn_W1"]
    w22[:, :H2, :] = inp["L2_ffn_W2"]
    bg2[:, :H2] = inp["L2_ffn_bg"]
    b12[:, :H2] = inp["L2_ffn_b1"]
    w["l2f_wg"] = np.stack([_sm(wg2[e], 8, 22) for e in range(8)])
    w["l2f_w1"] = np.stack([_sm(w12[e], 8, 22) for e in range(8)])
    w["l2f_w2"] = np.stack([_sm(w22[e], 22, 8) for e in range(8)])
    w["l2f_bg"] = np.ascontiguousarray(bg2.reshape(8, 22, 128).transpose(0, 2, 1))
    w["l2f_b1"] = np.ascontiguousarray(b12.reshape(8, 22, 128).transpose(0, 2, 1))
    w["l2f_b2"] = np.ascontiguousarray(
        np.asarray(inp["L2_ffn_b2"]).reshape(8, 8, 128).transpose(0, 2, 1))
    w["l2f_centT"] = _pk(_t(inp["L2_ffn_centroid"]), 8)
    w["l3a_normw"] = _cols(inp["L3_attn_norm_weight"], 8)
    w["l3a_winT"] = _sm(_t(inp["L3_attn_in_proj_w"]), 8, 32)
    w["l3a_cw"] = np.ascontiguousarray(
        np.asarray(inp["L3_attn_conv_w"]).reshape(16, 128, 4).transpose(1, 0, 2).reshape(128, 64))
    w["l3a_cb"] = _cols(inp["L3_attn_conv_b"], 16)
    w["l3a_xpT"] = _pk(_t(inp["L3_attn_x_proj_w"]), 16)
    w["l3a_dtwT"] = _t(inp["L3_attn_dt_proj_w"])
    w["l3a_dtb"] = _cols(inp["L3_attn_dt_proj_b"], 16)
    w["l3a_alog"] = _slabcols(inp["L3_attn_A_log"], 16, 16)
    w["l3a_dv"] = _cols(inp["L3_attn_D"], 16)
    w["l3a_woT"] = _sm(_t(inp["L3_attn_out_proj_w"]), 16, 8)
    w["l3f_w1T"] = _sm(_t(inp["L3_ffn_w1_w"]), 8, 32)
    w["l3f_w2T"] = _sm(_t(inp["L3_ffn_w2_w"]), 8, 32)
    w["l3f_h1T"] = _sm(_t(inp["L3_ffn_h1_w"]), 32, 32)
    w["l3f_h2T"] = _sm(_t(inp["L3_ffn_h2_w"]), 32, 32)
    w["l3f_w3T"] = _sm(_t(inp["L3_ffn_w3_w"]), 32, 8)
    w["l3f_s1"] = _cols(inp["L3_ffn_w1_s"], 32)
    w["l3f_s2"] = _cols(inp["L3_ffn_w2_s"], 32)
    w["l3f_h1s"] = _cols(inp["L3_ffn_h1_s"], 32)
    w["l3f_h2s"] = _cols(inp["L3_ffn_h2_s"], 32)
    w["l3f_s3"] = _cols(inp["L3_ffn_w3_s"], 8)
    return w


ARENA_BYTES = 200 * 1024
OFF_XSB = 0
OFF_XBF = 65536
OFF_WB = 98304
OFF_SMALL = 114688
OFF_ROWS = 118784
OFF_CONST = 135168
OFF_BIG = 137216
BIG_BYTES = ARENA_BYTES - OFF_BIG
BIGC = BIG_BYTES // 2


class Ctx:
    pass


def build_ctx(nc, es, weight_names, first, last):
    c = Ctx()
    c.nc = nc
    k = K(nc, es)
    c.k = k
    c.hbm = {}
    for name, (shape, dt) in weight_names.items():
        c.hbm[name] = nc.dram_tensor(name, list(shape), dt, kind="ExternalInput").ap()
    c.x_in = nc.dram_tensor("x_in", [T, D] if first else [D, T], F32,
                            kind="ExternalInput").ap()
    c.x_out = nc.dram_tensor("x_out", [T, D] if last else [D, T], F32,
                             kind="ExternalOutput").ap()
    c.hrow = nc.dram_tensor("hrow", [64, T], F32, kind="Internal").ap()

    c.arena_h = es.enter_context(nc.sbuf_tensor("arena", [128, ARENA_BYTES], mybir.dt.uint8))
    c.psum_h = es.enter_context(nc.psum_tensor("ps", [128, 4096], F32))
    c.aap = c.arena_h.ap()
    c.psum = c.psum_h

    def reg(off, nbytes, dt):
        return c.aap[:, off:off + nbytes].bitcast(dt)

    c.reg = reg
    c.x_sb = reg(OFF_XSB, 65536, F32)
    c.x_bf = reg(OFF_XBF, 32768, BF16)
    c.wbr = reg(OFF_WB, 16384, BF16)
    c.smallr = reg(OFF_SMALL, 4096, F32)
    c.rowsr = reg(OFF_ROWS, 16384, F32)
    c.big_b = reg(OFF_BIG, BIG_BYTES, BF16)
    c.big_f = reg(OFF_BIG, BIG_BYTES, F32)
    c.xbf_borrow_b = reg(OFF_XBF, 32768, BF16)
    c.xbf_borrow_f = reg(OFF_XBF, 32768, F32)

    c.ones_f = reg(OFF_CONST, 512, F32)
    c.ones_b = reg(OFF_CONST + 512, 256, BF16)
    c.ident_f = reg(OFF_CONST + 768, 512, F32)
    c.ident_b = reg(OFF_CONST + 1280, 256, BF16)

    c.bx = [Buf(f"x{i}") for i in range(DKT)]
    c.bxf = [Buf(f"xf{i}") for i in range(DKT)]
    c.bps = [Buf(f"pb{i}") for i in range(8)]
    c.brow = [Buf(f"row{i}") for i in range(3)]
    c.bsmall = [Buf(f"sm{i}") for i in range(4)]
    c.bwb = [Buf(f"wb{i}") for i in range(8)]
    c.bconst = Buf("const")
    c.bhrow = Buf("hrow")
    c.spans = []

    c.eps_col = c.smallr[:, 1020:1021]
    c.beps = Buf("eps")
    k.op("v", lambda eng: eng.memset(c.eps_col, 1e-5), writes=(c.beps,))
    k.dma("y", c.ones_f[:, :], c.hbm["ones_f"], writes=(c.bconst,))
    k.dma("y", c.ones_b[:, :], c.hbm["ones_b"], extra=toks_of([c.bconst]))
    k.dma("y", c.ident_f[:, :], c.hbm["ident_f"], extra=toks_of([c.bconst]))
    k.dma("y", c.ident_b[:, :], c.hbm["ident_b"], extra=toks_of([c.bconst]))
    return c


def take_span(c, arena, lo, hi, nbufs):
    guard = []
    for (a, l, h, bufs) in c.spans:
        if a == arena and not (h <= lo or hi <= l):
            guard.extend(toks_of(bufs))
    out = [Buf(f"{arena}:{lo}:{i}", guard=guard) for i in range(nbufs)]
    c.spans.append((arena, lo, hi, out))
    return out


def row_slot(c, i):
    return c.rowsr[0:1, i * 2048:(i + 1) * 2048]


def pbank(c, b, off=0, n=512):
    return c.psum[:, b * 512 + off: b * 512 + off + n]


def small_c(c, col0, ncols):
    return c.smallr[:, col0:col0 + ncols]


def xt(c, kt):
    return c.x_sb[:, kt * T:(kt + 1) * T]


def xbft(c, kt):
    return c.x_bf[:, kt * T:(kt + 1) * T]


def cast_x_bf(c):
    k = c.k
    nb = take_span(c, "xbf", 0, 32768, DKT)
    for kt in range(DKT):
        k.op("s", lambda eng, kt=kt: eng.copy(out=xbft(c, kt), in_=xt(c, kt)),
             reads=(c.bx[kt],), writes=(c.bxf[kt], nb[kt]))


def mm_acc(c, bank_bufs, emits, reads, extra=()):
    return c.k.group("t", emits, reads=tuple(reads), writes=tuple(bank_bufs), extra=extra)


def post_ln(c):
    k = c.k
    onecol_b = c.ones_b[:, 0:1]
    cast_x_bf(c)
    sqb = take_span(c, "wb", 0, 1024, 1)[0]
    sq = c.reg(OFF_WB, 1024, BF16)
    for ci in range(4):
        emits = []
        reads = [c.bconst]
        for kt in range(DKT):
            emits.append(lambda eng, ci=ci, kt=kt: eng.matmul(
                out=pbank(c, 4 + ci)[0:1, :],
                lhsT=onecol_b, rhs=xbft(c, kt)[:, ci * 512:(ci + 1) * 512],
                start=(kt == 0), stop=(kt == DKT - 1)))
            reads.append(c.bxf[kt])
        mm_acc(c, [c.bps[4 + ci]], emits, reads)
    for ci in range(4):
        for kt in range(DKT):
            k.op("s", lambda eng, ci=ci, kt=kt: eng.activation(
                out=sq, in_=xbft(c, kt)[:, ci * 512:(ci + 1) * 512], func=AF.Square),
                reads=(c.bxf[kt],), writes=(sqb,))
            k.op("t", lambda eng, ci=ci, kt=kt: eng.matmul(
                out=pbank(c, ci)[0:1, :], lhsT=onecol_b, rhs=sq,
                start=(kt == 0), stop=(kt == DKT - 1)),
                reads=(sqb, c.bconst), writes=(c.bps[ci],))
    r_mu, r_aux = row_slot(c, 0), row_slot(c, 1)
    rb_ = take_span(c, "rows", 0, 16384, 2)
    c.brow[0], c.brow[1] = rb_[0], rb_[1]
    for ci in range(4):
        k.op("s", lambda eng, ci=ci: eng.mul(
            out=r_mu[:, ci * 512:(ci + 1) * 512], in_=pbank(c, 4 + ci)[0:1, :], mul=1.0 / D),
            reads=(c.bps[4 + ci],), writes=(c.brow[0],))
        k.op("s", lambda eng, ci=ci: eng.mul(
            out=r_aux[:, ci * 512:(ci + 1) * 512], in_=pbank(c, ci)[0:1, :], mul=1.0 / D),
            reads=(c.bps[ci],), writes=(c.brow[1],))
    msq = c.psum[0:1, 0:2048]
    k.op("v", lambda eng: eng.tensor_tensor(out=msq, in0=r_mu, in1=r_mu, op=aop.mult),
         reads=(c.brow[0],), writes=(c.bps[0], c.bps[1], c.bps[2], c.bps[3]))
    k.op("v", lambda eng: eng.tensor_tensor(out=r_aux, in0=r_aux, in1=msq, op=aop.subtract),
         reads=(c.brow[1], c.bps[0], c.bps[1], c.bps[2], c.bps[3]), writes=(c.brow[1],))
    k.op("s", lambda eng: eng.activation(out=r_aux, in_=r_aux, func=AF.Ln,
                                         bias=c.eps_col[0:1, :]),
         reads=(c.brow[1], c.beps), writes=(c.brow[1],))
    k.op("s", lambda eng: eng.activation(out=r_aux, in_=r_aux, func=AF.Exp, scale=-0.5),
         reads=(c.brow[1],), writes=(c.brow[1],))
    k.op("v", lambda eng: eng.tensor_tensor(out=r_mu, in0=r_mu, in1=r_aux, op=aop.mult),
         reads=(c.brow[0], c.brow[1]), writes=(c.brow[0],))
    for ci in range(4):
        k.op("t", lambda eng, ci=ci: eng.matmul(
            out=pbank(c, ci), lhsT=c.ones_f[0:1, 0:128],
            rhs=r_aux[:, ci * 512:(ci + 1) * 512], start=True, stop=True),
            reads=(c.brow[1], c.bconst), writes=(c.bps[ci],))
        k.op("t", lambda eng, ci=ci: eng.matmul(
            out=pbank(c, 4 + ci), lhsT=c.ones_f[0:1, 0:128],
            rhs=r_mu[:, ci * 512:(ci + 1) * 512], start=True, stop=True),
            reads=(c.brow[0], c.bconst), writes=(c.bps[4 + ci],))
    rstd_bc = c.psum[:, 0:2048]
    mur_bc = c.psum[:, 2048:4096]
    for kt in range(DKT):
        k.op("v", lambda eng, kt=kt: eng.tensor_tensor(
            out=xt(c, kt), in0=xt(c, kt), in1=mur_bc, op=aop.subtract),
            reads=(c.bx[kt], c.bps[4], c.bps[5], c.bps[6], c.bps[7]),
            writes=(c.bx[kt],))
        k.op("v", lambda eng, kt=kt: eng.tensor_tensor(
            out=xt(c, kt), in0=xt(c, kt), in1=rstd_bc, op=aop.mult),
            reads=(c.bx[kt], c.bps[0], c.bps[1], c.bps[2], c.bps[3]),
            writes=(c.bx[kt],))
    cast_x_bf(c)


def dense_mm(c, banks, bank_bufs, lh_tiles, lh_bufs, rhs_fn, rhs_bufs, width):
    emits = []
    nk = len(lh_tiles)
    for ci in range(width // 512):
        for kt in range(nk):
            out_ap = pbank(c, banks[ci])
            lh = lh_tiles[kt]
            rh = rhs_fn(kt, ci)
            emits.append(lambda eng, o=out_ap, lh=lh, rh=rh, s=(kt == 0),
                         p=(kt == nk - 1): eng.matmul(
                out=o, lhsT=lh, rhs=rh, start=s, stop=p))
    reads = list(lh_bufs) + list(rhs_bufs)
    return mm_acc(c, bank_bufs, emits, reads)


def emit_l0f(c):
    k = c.k
    ntb = 2
    tbs = T // ntb
    nch = tbs // 512
    bb = take_span(c, "small", 0, 160, 1)[0]
    k.dma("y", small_c(c, 0, 32), c.hbm["l0f_b1"], writes=(bb,))
    k.dma("y", small_c(c, 32, 8), c.hbm["l0f_b2"], reads=(bb,), writes=(bb,))
    hb = take_span(c, "big", 0, 65536, 32)
    hap = [c.big_b[:, i * tbs:(i + 1) * tbs] for i in range(32)]
    for tb in range(ntb):
        t0 = tb * tbs
        wb1 = take_span(c, "wb", 0, 4096, 2)
        wsl = [c.wbr[:, 0:1024], c.wbr[:, 1024:2048]]
        for ot in range(32):
            sl = ot % 2
            k.dma("p", wsl[sl], c.hbm["l0f_w1"][ot], writes=(wb1[sl],))
            banks = [(ot % 2) * 2, (ot % 2) * 2 + 1][:nch]
            dense_mm(c, banks, [c.bps[b] for b in banks],
                     [wsl[sl][:, j * 128:(j + 1) * 128] for j in range(DKT)],
                     [wb1[sl]],
                     lambda kt, ci: xbft(c, kt)[:, t0 + ci * 512:t0 + (ci + 1) * 512],
                     c.bxf, tbs)
            k.op("s", lambda eng, ot=ot, b0=banks[0]: eng.activation(
                out=hap[ot], in_=c.psum[:, b0 * 512: b0 * 512 + tbs],
                func=AF.Relu, bias=small_c(c, ot, 1)),
                reads=tuple(c.bps[b] for b in banks) + (bb,), writes=(hb[ot],))
        wb2 = take_span(c, "wb", 0, 16384, 2)
        wdl = [c.wbr[:, 0:4096], c.wbr[:, 4096:8192]]
        for ot in range(8):
            sl = ot % 2
            k.dma("p", wdl[sl], c.hbm["l0f_w2"][ot], writes=(wb2[sl],))
            banks = [4 + (ot % 2) * 2, 4 + (ot % 2) * 2 + 1][:nch]
            dense_mm(c, banks, [c.bps[b] for b in banks],
                     [wdl[sl][:, j * 128:(j + 1) * 128] for j in range(32)],
                     [wb2[sl]],
                     lambda kt, ci: hap[kt][:, ci * 512:(ci + 1) * 512],
                     hb, tbs)
            k.op("v", lambda eng, ot=ot, t0=t0, b0=banks[0]: eng.scalar_tensor_tensor(
                out=xt(c, ot)[:, t0:t0 + tbs],
                in0=c.psum[:, b0 * 512: b0 * 512 + tbs],
                scalar=small_c(c, 32 + ot, 1),
                in1=xt(c, ot)[:, t0:t0 + tbs], op0=aop.add, op1=aop.add),
                reads=tuple(c.bps[b] for b in banks) + (bb, c.bx[ot]),
                writes=(c.bx[ot],))
    post_ln(c)


EMIT = {"l0f": emit_l0f}

def emit_l1a(c):
    k = c.k
    bigv = lambda off, nb, dt: c.reg(OFF_BIG + off, nb, dt)
    qT = bigv(0, 4096, BF16)
    kT = bigv(4096, 4096, BF16)
    vr = bigv(8192, 2048, BF16)
    wbf = bigv(10240, 4096, BF16)
    scf = bigv(14336, 8192, F32)
    wTr = bigv(22528, 4096, BF16)
    zT = bigv(26624, 4096, BF16)
    zr = bigv(30720, 2048, BF16)
    bq_, bk_, bv_, bw_, bsc, bwT, bzT, bz = take_span(c, "big", 0, 32768, 8)
    wbs = take_span(c, "wb", 0, 5120, 4)
    wq_s, wk_s, wv_s = c.wbr[:, 0:512], c.wbr[:, 512:1024], c.wbr[:, 1024:1536]
    wo_s = c.wbr[:, 1536:2560]
    for nm, dst, bu in (("l1a_wq", wq_s, wbs[0]), ("l1a_wk", wk_s, wbs[1]),
                        ("l1a_wv", wv_s, wbs[2])):
        k.dma("p", dst, c.hbm[nm], writes=(bu,))
    k.dma("p", wo_s[0:64, :], c.hbm["l1a_wo"], writes=(wbs[3],))
    bsm = take_span(c, "small", 160, 200, 1)[0]
    k.dma("y", small_c(c, 40, 1)[0:64, :], c.hbm["l1a_bq"], writes=(bsm,))
    k.dma("y", small_c(c, 41, 1)[0:64, :], c.hbm["l1a_bk"], reads=(bsm,), writes=(bsm,))
    k.dma("y", small_c(c, 42, 8), c.hbm["l1a_bo"], reads=(bsm,), writes=(bsm,))
    bvbc = take_span(c, "rows", 4096, 4352, 1)[0]
    k.dma("y", c.rowsr[:, 1024:1088], c.hbm["l1a_bv"].partition_broadcast(128).squeeze(1),
          writes=(bvbc,))
    bq_col = small_c(c, 40, 1)[0:64, :]
    bk_col = small_c(c, 41, 1)[0:64, :]
    bv_bc = c.rowsr[:, 1024:1088]
    for which, wsl, wbuf, breg, regap, bcol, scale2 in (
            ("q", wq_s, wbs[0], bq_, qT, bq_col, 0.125),
            ("k", wk_s, wbs[1], bk_, kT, bk_col, 1.0)):
        for ci in range(4):
            emits = []
            for kt in range(DKT):
                o = pbank(c, ci)[0:64, :]
                lh = wsl[:, kt * 64:(kt + 1) * 64]
                rh = xbft(c, kt)[:, ci * 512:(ci + 1) * 512]
                emits.append(lambda eng, o=o, lh=lh, rh=rh, s=(kt == 0),
                             p=(kt == DKT - 1): eng.matmul(out=o, lhsT=lh, rhs=rh,
                                                           start=s, stop=p))
            mm_acc(c, [c.bps[ci]], emits, [wbuf] + list(c.bxf))
            k.op("v", lambda eng, ci=ci, regap=regap, bcol=bcol, s2=scale2:
                 eng.tensor_scalar(out=regap[0:64, ci * 512:(ci + 1) * 512],
                                   in0=pbank(c, ci)[0:64, :], scalar1=bcol,
                                   scalar2=s2, op0=aop.add, op1=aop.mult),
                 reads=(c.bps[ci], bsm), writes=(breg,))
    for it in range(16):
        b = 4 + it % 2
        emits = []
        for kt in range(DKT):
            o = pbank(c, b)[:, 0:64]
            lh = xbft(c, kt)[:, it * 128:(it + 1) * 128]
            rh = wv_s[:, kt * 64:(kt + 1) * 64]
            emits.append(lambda eng, o=o, lh=lh, rh=rh, s=(kt == 0),
                         p=(kt == DKT - 1): eng.matmul(out=o, lhsT=lh, rhs=rh,
                                                       start=s, stop=p))
        mm_acc(c, [c.bps[b]], emits, [wbs[2]] + list(c.bxf))
        k.op("v", lambda eng, it=it, b=b: eng.tensor_tensor(
            out=vr[:, it * 64:(it + 1) * 64], in0=pbank(c, b)[:, 0:64],
            in1=bv_bc, op=aop.add),
            reads=(c.bps[b], bvbc), writes=(bv_,))
    nm_col = small_c(c, 50, 1)
    den_col = small_c(c, 51, 1)
    rd_col = small_c(c, 52, 1)
    bnm = take_span(c, "small", 200, 212, 1)[0]
    for ib in range(16):
        for ci in range(4):
            o = pbank(c, ci)
            lh = qT[0:64, ib * 128:(ib + 1) * 128]
            rh = kT[0:64, ci * 512:(ci + 1) * 512]
            k.op("t", lambda eng, o=o, lh=lh, rh=rh: eng.matmul(
                out=o, lhsT=lh, rhs=rh, start=True, stop=True),
                reads=(bq_, bk_), writes=(c.bps[ci],))
        k.op("v", lambda eng: eng.tensor_copy(out=scf[:, :], in_=c.psum[:, 0:2048]),
             reads=(c.bps[0], c.bps[1], c.bps[2], c.bps[3]), writes=(bsc,))
        k.op("p", lambda eng, ib=ib: eng.affine_select(
            out=scf[:, :], in_=scf[:, :], pattern=[[-1, 2048]],
            compare_op=aop.is_ge, fill=-1e30, base=ib * 128, channel_multiplier=1),
            reads=(bsc,), writes=(bsc,))
        k.op("v", lambda eng: eng.tensor_reduce(
            out=nm_col, in_=scf[:, :], axis=AX.X, op=aop.max),
            reads=(bsc,), writes=(bnm,))
        k.op("v", lambda eng: eng.tensor_scalar_mul(out=nm_col, in0=nm_col, scalar1=-1.0),
             reads=(bnm,), writes=(bnm,))
        k.op("s", lambda eng: eng.activation(
            out=wbf[:, :], in_=scf[:, :], func=AF.Exp, bias=nm_col,
            accum_out=den_col),
            reads=(bsc, bnm), writes=(bw_, bnm))
        k.op("v", lambda eng: eng.reciprocal(out=rd_col, in_=den_col),
             reads=(bnm,), writes=(bnm,))
        zp = pbank(c, 6)[:, 0:64]
        emits = []
        for jt in range(ib + 1):
            b = 4 + jt % 2
            pt_bf = c.psum[:, b * 512:b * 512 + 256].bitcast(BF16)[:, 0:128]
            k.op("t", lambda eng, jt=jt, pt_bf=pt_bf: eng.transpose(
                out=pt_bf, in_=wbf[:, jt * 128:(jt + 1) * 128], identity=c.ident_b),
                reads=(bw_, c.bconst), writes=(c.bps[b],))
            k.op("v", lambda eng, jt=jt, pt_bf=pt_bf: eng.tensor_copy(
                out=wTr[:, jt * 128:(jt + 1) * 128], in_=pt_bf),
                reads=(c.bps[b],), writes=(bwT,))
            lh = wTr[:, jt * 128:(jt + 1) * 128]
            rh = vr[:, jt * 64:(jt + 1) * 64]
            emits.append(lambda eng, lh=lh, rh=rh, s=(jt == 0), p=(jt == ib):
                         eng.matmul(out=zp, lhsT=lh, rhs=rh, start=s, stop=p))
        mm_acc(c, [c.bps[6]], emits, [bwT, bv_])
        k.op("v", lambda eng, ib=ib, zp=zp: eng.tensor_scalar_mul(
            out=zr[:, ib * 64:(ib + 1) * 64], in0=zp, scalar1=rd_col),
            reads=(c.bps[6], bnm), writes=(bz,))
        ztp = c.psum[:, 7 * 512:7 * 512 + 256].bitcast(BF16)[:, 0:128]
        k.op("t", lambda eng, ib=ib: eng.transpose(
            out=ztp[0:64, :], in_=zr[:, ib * 64:(ib + 1) * 64], identity=c.ident_b),
            reads=(bz, c.bconst), writes=(c.bps[7],))
        k.op("v", lambda eng, ib=ib: eng.tensor_copy(
            out=zT[0:64, ib * 128:(ib + 1) * 128], in_=ztp[0:64, :]),
            reads=(c.bps[7],), writes=(bzT,))
    for ot in range(DKT):
        banks = [(ot % 2) * 2, (ot % 2) * 2 + 1, (ot % 2) * 2 + 8]
        bb2 = [(ot % 2) * 4, (ot % 2) * 4 + 1, (ot % 2) * 4 + 2, (ot % 2) * 4 + 3]
        emits = []
        for ci in range(4):
            o = pbank(c, bb2[ci])
            lh = wo_s[0:64, ot * 128:(ot + 1) * 128]
            rh = zT[0:64, ci * 512:(ci + 1) * 512]
            emits.append(lambda eng, o=o, lh=lh, rh=rh: eng.matmul(
                out=o, lhsT=lh, rhs=rh, start=True, stop=True))
        mm_acc(c, [c.bps[b] for b in bb2], emits, [wbs[3], bzT])
        k.op("v", lambda eng, ot=ot, b0=bb2[0]: eng.scalar_tensor_tensor(
            out=xt(c, ot), in0=c.psum[:, b0 * 512:b0 * 512 + 2048],
            scalar=small_c(c, 42 + ot, 1), in1=xt(c, ot), op0=aop.add, op1=aop.add),
            reads=tuple(c.bps[b] for b in bb2) + (bsm, c.bx[ot]),
            writes=(c.bx[ot],))
    post_ln(c)


EMIT["l1a"] = emit_l1a

def emit_l3f(c):
    k = c.k
    tbs = 512
    bsm = take_span(c, "small", 0, 544, 1)[0]
    k.dma("y", small_c(c, 0, 32), c.hbm["l3f_s1"], writes=(bsm,))
    for nm, c0 in (("l3f_s2", 32), ("l3f_h1s", 64), ("l3f_h2s", 96), ("l3f_s3", 128)):
        k.dma("y", small_c(c, c0, 32 if nm != "l3f_s3" else 8), c.hbm[nm],
              reads=(bsm,), writes=(bsm,))
    w1r = c.hbm["l3f_w1T"]
    w2r = c.hbm["l3f_w2T"]
    h1r = c.hbm["l3f_h1T"]
    h2r = c.hbm["l3f_h2T"]
    w3r = c.hbm["l3f_w3T"]
    lo = lambda i: c.big_b[:, i * tbs:(i + 1) * tbs]
    hi = lambda i: c.big_b[:, 16384 + i * tbs:16384 + (i + 1) * tbs]
    sg = c.rowsr[:, 1024:1536]
    tu = c.rowsr[:, 1536:2048]
    for tb in range(4):
        t0 = tb * tbs
        t1b = take_span(c, "big", 0, 32768, 32)
        wbs = take_span(c, "wb", 0, 8192, 4)
        wg = [c.wbr[:, 0:1024], c.wbr[:, 1024:2048]]
        wu = [c.wbr[:, 2048:3072], c.wbr[:, 3072:4096]]
        bscr = take_span(c, "rows", 4096, 8192, 1)[0]
        for ot in range(32):
            sl = ot % 2
            k.dma("p", wg[sl], w1r[ot], writes=(wbs[sl],))
            k.dma("p", wu[sl], w2r[ot], writes=(wbs[2 + sl],))
            bG, bU = sl * 2, sl * 2 + 1
            for (bk, wsl_, wbuf) in ((bG, wg[sl], wbs[sl]), (bU, wu[sl], wbs[2 + sl])):
                emits = []
                for kt in range(DKT):
                    o = pbank(c, bk)
                    lh = wsl_[:, kt * 128:(kt + 1) * 128]
                    rh = xbft(c, kt)[:, t0 + (0) * 512:t0 + 512]
                    emits.append(lambda eng, o=o, lh=lh, rh=rh, s=(kt == 0),
                                 p=(kt == DKT - 1): eng.matmul(
                        out=o, lhsT=lh, rhs=rh, start=s, stop=p))
                mm_acc(c, [c.bps[bk]], emits, [wbuf] + list(c.bxf))
            k.op("s", lambda eng, ot=ot, bG=bG: eng.activation(
                out=sg, in_=pbank(c, bG), func=AF.Silu, scale=small_c(c, ot, 1)),
                reads=(c.bps[bG], bsm), writes=(bscr,))
            k.op("v", lambda eng, ot=ot, bU=bU: eng.tensor_scalar_mul(
                out=tu, in0=pbank(c, bU), scalar1=small_c(c, 32 + ot, 1)),
                reads=(c.bps[bU], bsm), writes=(bscr,))
            k.op("v", lambda eng, ot=ot: eng.tensor_tensor(
                out=lo(ot), in0=sg, in1=tu, op=aop.mult),
                reads=(bscr,), writes=(t1b[ot],))
        h1b = take_span(c, "big", 32768, 65536, 32)
        wbs2 = take_span(c, "wb", 0, 16384, 2)
        wsl2 = [c.wbr[:, 0:4096], c.wbr[:, 4096:8192]]
        for ot in range(32):
            sl = ot % 2
            k.dma("p", wsl2[sl], h1r[ot], writes=(wbs2[sl],))
            bk = 4 + sl
            emits = []
            for kt in range(32):
                o = pbank(c, bk)
                lh = wsl2[sl][:, kt * 128:(kt + 1) * 128]
                rh = lo(kt)
                emits.append(lambda eng, o=o, lh=lh, rh=rh, s=(kt == 0),
                             p=(kt == 31): eng.matmul(out=o, lhsT=lh, rhs=rh,
                                                      start=s, stop=p))
            mm_acc(c, [c.bps[bk]], emits, [wbs2[sl]] + t1b)
            k.op("s", lambda eng, ot=ot, bk=bk: eng.activation(
                out=hi(ot), in_=pbank(c, bk), func=AF.Gelu, scale=small_c(c, 64 + ot, 1)),
                reads=(c.bps[bk], bsm), writes=(h1b[ot],))
        h2b = take_span(c, "big", 0, 32768, 32)
        wbs3 = take_span(c, "wb", 0, 16384, 2)
        for ot in range(32):
            sl = ot % 2
            k.dma("p", wsl2[sl], h2r[ot], writes=(wbs3[sl],))
            bk = 6 + sl
            emits = []
            for kt in range(32):
                o = pbank(c, bk)
                lh = wsl2[sl][:, kt * 128:(kt + 1) * 128]
                rh = hi(kt)
                emits.append(lambda eng, o=o, lh=lh, rh=rh, s=(kt == 0),
                             p=(kt == 31): eng.matmul(out=o, lhsT=lh, rhs=rh,
                                                      start=s, stop=p))
            mm_acc(c, [c.bps[bk]], emits, [wbs3[sl]] + h1b)
            k.op("s", lambda eng, ot=ot, bk=bk: eng.activation(
                out=lo(ot), in_=pbank(c, bk), func=AF.Gelu, scale=small_c(c, 96 + ot, 1)),
                reads=(c.bps[bk], bsm), writes=(h2b[ot],))
        wbs4 = take_span(c, "wb", 0, 16384, 2)
        for ot in range(8):
            sl = ot % 2
            k.dma("p", wsl2[sl], w3r[ot], writes=(wbs4[sl],))
            bk = sl
            emits = []
            for kt in range(32):
                o = pbank(c, bk)
                lh = wsl2[sl][:, kt * 128:(kt + 1) * 128]
                rh = lo(kt)
                emits.append(lambda eng, o=o, lh=lh, rh=rh, s=(kt == 0),
                             p=(kt == 31): eng.matmul(out=o, lhsT=lh, rhs=rh,
                                                      start=s, stop=p))
            mm_acc(c, [c.bps[bk]], emits, [wbs4[sl]] + h2b)
            k.op("v", lambda eng, ot=ot, t0=t0, bk=bk: eng.scalar_tensor_tensor(
                out=xt(c, ot)[:, t0:t0 + tbs], in0=pbank(c, bk),
                scalar=small_c(c, 128 + ot, 1), in1=xt(c, ot)[:, t0:t0 + tbs],
                op0=aop.mult, op1=aop.add),
                reads=(c.bps[bk], bsm, c.bx[ot]), writes=(c.bx[ot],))
    post_ln(c)


EMIT["l3f"] = emit_l3f

def emit_l1f(c):
    k = c.k
    wbs = take_span(c, "wb", 0, 128, 5)
    wr_s = c.wbr[:, 0:64]
    k.dma("p", wr_s, c.hbm["l1f_wrT"], writes=(wbs[4],))
    bsm = take_span(c, "small", 0, 32, 1)[0]
    k.dma("y", small_c(c, 0, 1)[0:8, :], c.hbm["l1f_rb"], writes=(bsm,))
    probs = c.rowsr[0:8, 0:2048]
    brows = take_span(c, "rows", 0, 16384, 1)[0]
    for ci in range(4):
        emits = []
        for kt in range(DKT):
            o = pbank(c, ci)[0:8, :]
            lh = wr_s[:, kt * 8:(kt + 1) * 8]
            rh = xbft(c, kt)[:, ci * 512:(ci + 1) * 512]
            emits.append(lambda eng, o=o, lh=lh, rh=rh, s=(kt == 0),
                         p=(kt == DKT - 1): eng.matmul(out=o, lhsT=lh, rhs=rh,
                                                       start=s, stop=p))
        mm_acc(c, [c.bps[ci]], emits, [wbs[4]] + list(c.bxf))
        k.op("s", lambda eng, ci=ci: eng.activation(
            out=probs[:, ci * 512:(ci + 1) * 512], in_=pbank(c, ci)[0:8, :],
            func=AF.Sigmoid, bias=small_c(c, 0, 1)[0:8, :]),
            reads=(c.bps[ci], bsm), writes=(brows,))
    cwr = c.rowsr[0:8, 2048:4096]
    pT = small_c(c, 8, 8)
    m1c, m2c = small_c(c, 16, 1), small_c(c, 17, 1)
    msk = small_c(c, 24, 8)
    bsc = take_span(c, "small", 32, 128, 1)[0]
    for it in range(16):
        pp = pbank(c, 4)[:, 0:8]
        k.op("t", lambda eng, it=it, pp=pp: eng.transpose(
            out=pp, in_=probs[:, it * 128:(it + 1) * 128], identity=c.ident_f[0:8, 0:8]),
            reads=(brows, c.bconst), writes=(c.bps[4],))
        k.op("v", lambda eng, pp=pp: eng.tensor_copy(out=pT, in_=pp),
             reads=(c.bps[4],), writes=(bsc,))
        k.op("v", lambda eng: eng.tensor_reduce(out=m1c, in_=pT, axis=AX.X, op=aop.max),
             reads=(bsc,), writes=(bsc,))
        k.op("v", lambda eng: eng.tensor_scalar(
            out=msk, in0=pT, scalar1=m1c, scalar2=-1e30, op0=aop.is_equal, op1=aop.mult),
            reads=(bsc,), writes=(bsc,))
        k.op("v", lambda eng: eng.tensor_tensor(out=msk, in0=pT, in1=msk, op=aop.add),
             reads=(bsc,), writes=(bsc,))
        k.op("v", lambda eng: eng.tensor_reduce(out=m2c, in_=msk, axis=AX.X, op=aop.max),
             reads=(bsc,), writes=(bsc,))
        k.op("v", lambda eng: eng.tensor_scalar(
            out=msk, in0=pT, scalar1=m2c, scalar2=1.0, op0=aop.is_ge, op1=aop.mult),
            reads=(bsc,), writes=(bsc,))
        k.op("v", lambda eng: eng.tensor_tensor(out=msk, in0=msk, in1=pT, op=aop.mult),
             reads=(bsc,), writes=(bsc,))
        ppb = pbank(c, 5)[0:8, 0:128]
        k.op("t", lambda eng, ppb=ppb: eng.transpose(
            out=ppb, in_=msk, identity=c.ident_f),
            reads=(bsc, c.bconst), writes=(c.bps[5],))
        k.op("v", lambda eng, it=it, ppb=ppb: eng.tensor_copy(
            out=cwr[:, it * 128:(it + 1) * 128], in_=ppb),
            reads=(c.bps[5],), writes=(brows,))
    k.dma("y", c.hrow[0:8, :], cwr, reads=(brows,), writes=(c.bhrow,))
    if globals().get("DEBUG_L1F_GATE"):
        k.op("v", lambda eng: eng.tensor_copy(out=xt(c, 0)[0:8, :], in_=probs),
             reads=(brows,), writes=(c.bx[0],))
        k.op("v", lambda eng: eng.tensor_copy(out=xt(c, 1)[0:8, :], in_=cwr),
             reads=(brows,), writes=(c.bx[1],))
        return
    cwbc = c.rowsr[:, 0:2048]
    bcw = take_span(c, "rows", 0, 8192, 1)[0]
    hb = None
    for e in range(9):
        hb = take_span(c, "big", 0, 65536, 16)
        hap = [c.big_b[:, i * T:(i + 1) * T] for i in range(16)]
        if e > 0:
            k.dma("y", cwbc, c.hrow[e - 1:e, :].partition_broadcast(128).squeeze(1),
                  reads=(c.bhrow,), writes=(bcw,))
        wgu = take_span(c, "wb", 0, 8192, 4)
        gsl = [c.wbr[:, 0:1024], c.wbr[:, 1024:2048]]
        usl = [c.wbr[:, 2048:3072], c.wbr[:, 3072:4096]]
        sgap = c.reg(OFF_ROWS + 8192, 2048, F32)
        bsg = take_span(c, "rows", 8192, 10240, 1)[0]
        wgr = c.hbm["l1f_wgT"]
        wur = c.hbm["l1f_wuT"]
        wdr = c.hbm["l1f_wdT"]
        for it in range(16):
            sl = it % 2
            k.dma("p", gsl[sl], wgr[e, it], writes=(wgu[sl],))
            k.dma("p", usl[sl], wur[e, it], writes=(wgu[2 + sl],))
            for ci in range(4):
                bG = sl * 2
                bU = sl * 2 + 1
                for (bk, wsl_, wbuf) in ((bG, gsl[sl], wgu[sl]), (bU, usl[sl], wgu[2 + sl])):
                    emits = []
                    for kt in range(DKT):
                        o = pbank(c, bk)
                        lh = wsl_[:, kt * 128:(kt + 1) * 128]
                        rh = xbft(c, kt)[:, ci * 512:(ci + 1) * 512]
                        emits.append(lambda eng, o=o, lh=lh, rh=rh, s=(kt == 0),
                                     p=(kt == DKT - 1): eng.matmul(
                            out=o, lhsT=lh, rhs=rh, start=s, stop=p))
                    mm_acc(c, [c.bps[bk]], emits, [wbuf] + list(c.bxf))
                k.op("s", lambda eng, bG=bG: eng.activation(
                    out=sgap, in_=pbank(c, bG), func=AF.Silu),
                    reads=(c.bps[bG],), writes=(bsg,))
                hout = hap[it][:, ci * 512:(ci + 1) * 512]
                if e == 0:
                    k.op("v", lambda eng, bU=bU, hout=hout: eng.tensor_tensor(
                        out=hout, in0=sgap, in1=pbank(c, bU), op=aop.mult),
                        reads=(bsg, c.bps[bU]), writes=(hb[it],))
                else:
                    k.op("v", lambda eng, bU=bU: eng.tensor_tensor(
                        out=sgap, in0=sgap, in1=pbank(c, bU), op=aop.mult),
                        reads=(bsg, c.bps[bU]), writes=(bsg,))
                    cwc = cwbc[:, ci * 512:(ci + 1) * 512]
                    k.op("v", lambda eng, hout=hout, cwc=cwc: eng.tensor_tensor(
                        out=hout, in0=sgap, in1=cwc, op=aop.mult),
                        reads=(bsg, bcw), writes=(hb[it],))
        wdd = take_span(c, "wb", 8192, 16384, 2)
        dsl = [c.wbr[:, 4096:6144], c.wbr[:, 6144:8192]]
        for ot in range(8):
            sl = ot % 2
            k.dma("p", dsl[sl], wdr[e, ot], writes=(wdd[sl],))
            for ci in range(4):
                bk = 4 + ci
                emits = []
                for kt in range(16):
                    o = pbank(c, bk)
                    lh = dsl[sl][:, kt * 128:(kt + 1) * 128]
                    rh = hap[kt][:, ci * 512:(ci + 1) * 512]
                    emits.append(lambda eng, o=o, lh=lh, rh=rh, s=(kt == 0),
                                 p=(kt == 15): eng.matmul(out=o, lhsT=lh, rhs=rh,
                                                          start=s, stop=p))
                mm_acc(c, [c.bps[bk]], emits, [wdd[sl]] + hb)
            k.op("v", lambda eng, ot=ot: eng.tensor_tensor(
                out=xt(c, ot), in0=xt(c, ot), in1=c.psum[:, 2048:4096], op=aop.add),
                reads=(c.bps[4], c.bps[5], c.bps[6], c.bps[7], c.bx[ot]),
                writes=(c.bx[ot],))
    post_ln(c)


EMIT["l1f"] = emit_l1f

def emit_l2f(c):
    k = c.k
    sqb = take_span(c, "wb", 0, 2048, 1)[0]
    sq = c.reg(OFF_WB, 2048, F32)
    onecol = c.ones_f[:, 0:1]
    for ci in range(4):
        for kt in range(DKT):
            k.op("s", lambda eng, ci=ci, kt=kt: eng.activation(
                out=sq, in_=xt(c, kt)[:, ci * 512:(ci + 1) * 512], func=AF.Square),
                reads=(c.bx[kt],), writes=(sqb,))
            k.op("t", lambda eng, ci=ci, kt=kt: eng.matmul(
                out=pbank(c, 4 + ci)[0:1, :], lhsT=onecol, rhs=sq,
                start=(kt == 0), stop=(kt == DKT - 1)),
                reads=(sqb, c.bconst), writes=(c.bps[4 + ci],))
    x2row = c.rowsr[0:1, 0:2048]
    brx = take_span(c, "rows", 0, 16384, 1)[0]
    for ci in range(4):
        k.op("s", lambda eng, ci=ci: eng.copy(
            out=x2row[:, ci * 512:(ci + 1) * 512], in_=pbank(c, 4 + ci)[0:1, :]),
            reads=(c.bps[4 + ci],), writes=(brx,))
    wbs = take_span(c, "wb", 0, 128, 3)
    centT = c.wbr[:, 0:64]
    k.dma("p", centT, c.hbm["l2f_centT"], writes=(wbs[0],))
    bsm = take_span(c, "small", 0, 288, 1)[0]
    centsq = small_c(c, 0, 64)
    k.op("s", lambda eng: eng.activation(out=centsq, in_=centT, func=AF.Square),
         reads=(wbs[0],), writes=(bsm,))
    emits = []
    for kt in range(DKT):
        o = pbank(c, 0)[0:1, 0:8]
        lh = onecol
        rh = centsq[:, kt * 8:(kt + 1) * 8]
        emits.append(lambda eng, o=o, lh=lh, rh=rh, s=(kt == 0), p=(kt == DKT - 1):
                     eng.matmul(out=o, lhsT=lh, rhs=rh, start=s, stop=p))
    mm_acc(c, [c.bps[0]], emits, [bsm, c.bconst])
    ccrow = c.rowsr[0:1, 2048:2056]
    k.op("v", lambda eng: eng.tensor_copy(out=ccrow, in_=pbank(c, 0)[0:1, 0:8]),
         reads=(c.bps[0],), writes=(brx,))
    k.dma("y", c.hrow[8:9, 0:8], ccrow, reads=(brx,), writes=(c.bhrow,))
    cc_bc = small_c(c, 64, 8)
    k.dma("y", cc_bc, c.hrow[8:9, 0:8].partition_broadcast(128).squeeze(1),
          reads=(c.bhrow,), writes=(bsm,))
    cxrow = c.rowsr[0:8, 2048:4096]
    for ci in range(4):
        emits = []
        for kt in range(DKT):
            o = pbank(c, ci)[0:8, :]
            lh = centT[:, kt * 8:(kt + 1) * 8]
            rh = xbft(c, kt)[:, ci * 512:(ci + 1) * 512]
            emits.append(lambda eng, o=o, lh=lh, rh=rh, s=(kt == 0),
                         p=(kt == DKT - 1): eng.matmul(out=o, lhsT=lh, rhs=rh,
                                                       start=s, stop=p))
        mm_acc(c, [c.bps[ci]], emits, [wbs[0]] + list(c.bxf))
        k.op("s", lambda eng, ci=ci: eng.copy(
            out=cxrow[:, ci * 512:(ci + 1) * 512], in_=pbank(c, ci)[0:8, :]),
            reads=(c.bps[ci],), writes=(brx,))
    cxT = small_c(c, 72, 8)
    x2c = small_c(c, 80, 1)
    m1c = small_c(c, 81, 1)
    m2c = small_c(c, 82, 1)
    msk = small_c(c, 83, 8)
    ew = small_c(c, 91, 8)
    dmc = small_c(c, 99, 1)
    rden = small_c(c, 100, 1)
    weT = small_c(c, 101, 8)
    bg2 = take_span(c, "small", 288, 436, 1)[0]
    for it in range(16):
        pp = pbank(c, 4)[:, 0:8]
        k.op("t", lambda eng, it=it, pp=pp: eng.transpose(
            out=pp, in_=cxrow[:, it * 128:(it + 1) * 128], identity=c.ident_f[0:8, 0:8]),
            reads=(brx, c.bconst), writes=(c.bps[4],))
        k.op("v", lambda eng, pp=pp: eng.tensor_copy(out=cxT, in_=pp),
             reads=(c.bps[4],), writes=(bg2,))
        px = pbank(c, 5)[:, 0:1]
        k.op("t", lambda eng, it=it, px=px: eng.transpose(
            out=px, in_=x2row[:, it * 128:(it + 1) * 128], identity=c.ident_f[0:1, 0:1]),
            reads=(brx, c.bconst), writes=(c.bps[5],))
        k.op("v", lambda eng, px=px: eng.tensor_copy(out=x2c, in_=px),
             reads=(c.bps[5],), writes=(bg2,))
        k.op("v", lambda eng: eng.tensor_scalar(
            out=cxT, in0=cxT, scalar1=-2.0, scalar2=x2c, op0=aop.mult, op1=aop.add),
            reads=(bg2,), writes=(bg2,))
        k.op("v", lambda eng: eng.tensor_tensor(out=cxT, in0=cxT, in1=cc_bc, op=aop.add),
             reads=(bg2, bsm), writes=(bg2,))
        k.op("v", lambda eng: eng.tensor_scalar_max(out=cxT, in0=cxT, scalar1=1e-12),
             reads=(bg2,), writes=(bg2,))
        k.op("s", lambda eng: eng.activation(out=cxT, in_=cxT, func=AF.Ln),
             reads=(bg2,), writes=(bg2,))
        k.op("s", lambda eng: eng.activation(out=cxT, in_=cxT, func=AF.Exp, scale=0.5),
             reads=(bg2,), writes=(bg2,))
        k.op("v", lambda eng: eng.tensor_reduce(out=m1c, in_=cxT, axis=AX.X, op=aop.max),
             reads=(bg2,), writes=(bg2,))
        k.op("v", lambda eng: eng.tensor_scalar(
            out=msk, in0=cxT, scalar1=m1c, scalar2=-1e30, op0=aop.is_equal, op1=aop.mult),
            reads=(bg2,), writes=(bg2,))
        k.op("v", lambda eng: eng.tensor_tensor(out=msk, in0=cxT, in1=msk, op=aop.add),
             reads=(bg2,), writes=(bg2,))
        k.op("v", lambda eng: eng.tensor_reduce(out=m2c, in_=msk, axis=AX.X, op=aop.max),
             reads=(bg2,), writes=(bg2,))
        k.op("v", lambda eng: eng.tensor_scalar(
            out=ew, in0=cxT, scalar1=m1c, scalar2=1.0, op0=aop.subtract, op1=aop.mult),
            reads=(bg2,), writes=(bg2,))
        k.op("s", lambda eng: eng.activation(out=ew, in_=ew, func=AF.Exp),
             reads=(bg2,), writes=(bg2,))
        k.op("v", lambda eng: eng.tensor_tensor(out=dmc, in0=m2c, in1=m1c, op=aop.subtract),
             reads=(bg2,), writes=(bg2,))
        k.op("s", lambda eng: eng.activation(out=dmc, in_=dmc, func=AF.Exp),
             reads=(bg2,), writes=(bg2,))
        k.op("v", lambda eng: eng.tensor_scalar_add(out=dmc, in0=dmc, scalar1=1.0),
             reads=(bg2,), writes=(bg2,))
        k.op("v", lambda eng: eng.reciprocal(out=rden, in_=dmc),
             reads=(bg2,), writes=(bg2,))
        k.op("v", lambda eng: eng.tensor_scalar(
            out=weT, in0=cxT, scalar1=m2c, scalar2=1.0, op0=aop.is_ge, op1=aop.mult),
            reads=(bg2,), writes=(bg2,))
        k.op("v", lambda eng: eng.tensor_tensor(out=weT, in0=weT, in1=ew, op=aop.mult),
             reads=(bg2,), writes=(bg2,))
        k.op("v", lambda eng: eng.tensor_scalar_mul(out=weT, in0=weT, scalar1=rden),
             reads=(bg2,), writes=(bg2,))
        ppb = pbank(c, 6)[0:8, 0:128]
        k.op("t", lambda eng, ppb=ppb: eng.transpose(
            out=ppb, in_=weT, identity=c.ident_f),
            reads=(bg2, c.bconst), writes=(c.bps[6],))
        k.op("v", lambda eng, it=it, ppb=ppb: eng.tensor_copy(
            out=cxrow[:, it * 128:(it + 1) * 128], in_=ppb),
            reads=(c.bps[6],), writes=(brx,))
    k.dma("y", c.hrow[0:8, :], cxrow, reads=(brx,), writes=(c.bhrow,))
    webc = c.rowsr[:, 0:2048]
    bwe = take_span(c, "rows", 0, 8192, 1)[0]
    sgap = c.reg(OFF_ROWS + 8192, 2048, F32)
    tuap = c.reg(OFF_ROWS + 10240, 2048, F32)
    bsg = take_span(c, "rows", 8192, 12288, 1)[0]
    for e in range(8):
        k.dma("y", webc, c.hrow[e:e + 1, :].partition_broadcast(128).squeeze(1),
              reads=(c.bhrow,), writes=(bwe,))
        bbe = take_span(c, "small", 512, 720, 1)[0]
        k.dma("y", small_c(c, 128, 22), c.hbm["l2f_bg"][e], writes=(bbe,))
        k.dma("y", small_c(c, 150, 22), c.hbm["l2f_b1"][e], reads=(bbe,), writes=(bbe,))
        k.dma("y", small_c(c, 172, 8), c.hbm["l2f_b2"][e], reads=(bbe,), writes=(bbe,))
        wgr = c.hbm["l2f_wg"][e]
        w1r = c.hbm["l2f_w1"][e]
        w2r = c.hbm["l2f_w2"][e]
        hap = [c.big_b[:, i * T:(i + 1) * T] for i in range(11)]
        for hh in range(2):
            hb = take_span(c, "big", 0, 45056, 11)
            wgu = take_span(c, "wb", 0, 8192, 4)
            gsl = [c.wbr[:, 0:1024], c.wbr[:, 1024:2048]]
            usl = [c.wbr[:, 2048:3072], c.wbr[:, 3072:4096]]
            for it in range(11):
                itg = hh * 11 + it
                sl = it % 2
                k.dma("p", gsl[sl], wgr[itg], writes=(wgu[sl],))
                k.dma("p", usl[sl], w1r[itg], writes=(wgu[2 + sl],))
                for ci in range(4):
                    bG, bU = sl * 2, sl * 2 + 1
                    for (bk, wsl_, wbuf) in ((bG, gsl[sl], wgu[sl]), (bU, usl[sl], wgu[2 + sl])):
                        emits = []
                        for kt in range(DKT):
                            o = pbank(c, bk)
                            lh = wsl_[:, kt * 128:(kt + 1) * 128]
                            rh = xbft(c, kt)[:, ci * 512:(ci + 1) * 512]
                            emits.append(lambda eng, o=o, lh=lh, rh=rh, s=(kt == 0),
                                         p=(kt == DKT - 1): eng.matmul(
                                out=o, lhsT=lh, rhs=rh, start=s, stop=p))
                        mm_acc(c, [c.bps[bk]], emits, [wbuf] + list(c.bxf))
                    k.op("s", lambda eng, bG=bG, itg=itg: eng.activation(
                        out=sgap, in_=pbank(c, bG), func=AF.Silu,
                        bias=small_c(c, 128 + itg, 1)),
                        reads=(c.bps[bG], bbe), writes=(bsg,))
                    k.op("v", lambda eng, bU=bU, itg=itg: eng.tensor_scalar_add(
                        out=tuap, in0=pbank(c, bU), scalar1=small_c(c, 150 + itg, 1)),
                        reads=(c.bps[bU], bbe), writes=(bsg,))
                    cwc = webc[:, ci * 512:(ci + 1) * 512]
                    k.op("v", lambda eng, cwc=cwc: eng.tensor_tensor(
                        out=tuap, in0=tuap, in1=cwc, op=aop.mult),
                        reads=(bsg, bwe), writes=(bsg,))
                    hout = hap[it][:, ci * 512:(ci + 1) * 512]
                    k.op("v", lambda eng, hout=hout: eng.tensor_tensor(
                        out=hout, in0=sgap, in1=tuap, op=aop.mult),
                        reads=(bsg,), writes=(hb[it],))
            wdd = take_span(c, "wb", 8192, 13824, 2)
            dsl = [c.wbr[:, 4096:5504], c.wbr[:, 5504:6912]]
            for ot in range(8):
                sl = ot % 2
                k.dma("p", dsl[sl], w2r[ot][:, hh * 1408:(hh + 1) * 1408],
                      writes=(wdd[sl],))
                for ci in range(4):
                    bk = 4 + ci
                    emits = []
                    for kt in range(11):
                        o = pbank(c, bk)
                        lh = dsl[sl][:, kt * 128:(kt + 1) * 128]
                        rh = hap[kt][:, ci * 512:(ci + 1) * 512]
                        emits.append(lambda eng, o=o, lh=lh, rh=rh, s=(kt == 0),
                                     p=(kt == 10): eng.matmul(out=o, lhsT=lh, rhs=rh,
                                                              start=s, stop=p))
                    mm_acc(c, [c.bps[bk]], emits, [wdd[sl]] + hb)
                if hh == 0:
                    k.op("v", lambda eng, ot=ot: eng.scalar_tensor_tensor(
                        out=xt(c, ot), in0=webc, scalar=small_c(c, 172 + ot, 1),
                        in1=xt(c, ot), op0=aop.mult, op1=aop.add),
                        reads=(bwe, bbe, c.bx[ot]), writes=(c.bx[ot],))
                k.op("v", lambda eng, ot=ot: eng.tensor_tensor(
                    out=xt(c, ot), in0=xt(c, ot), in1=c.psum[:, 2048:4096], op=aop.add),
                    reads=(c.bps[4], c.bps[5], c.bps[6], c.bps[7], c.bx[ot]),
                    writes=(c.bx[ot],))
    post_ln(c)


EMIT["l2f"] = emit_l2f

def emit_l2a(c):
    k = c.k
    bsm = take_span(c, "small", 0, 104, 1)[0]
    k.dma("y", small_c(c, 0, 8), c.hbm["l2a_tsg"], writes=(bsm,))
    k.dma("y", small_c(c, 8, 8), c.hbm["l2a_lnw"], reads=(bsm,), writes=(bsm,))
    k.dma("y", small_c(c, 16, 8), c.hbm["l2a_lnb"], reads=(bsm,), writes=(bsm,))
    k.dma("y", small_c(c, 24, 1)[0:64, :], c.hbm["l2a_td"], reads=(bsm,), writes=(bsm,))
    k.dma("y", small_c(c, 25, 1)[0:64, :], c.hbm["l2a_tf"], reads=(bsm,), writes=(bsm,))
    g_col = small_c(c, 32, 8)
    omg = small_c(c, 40, 8)
    netf = small_c(c, 48, 1)[0:64, :]
    bsc = take_span(c, "small", 128, 196, 1)[0]
    k.op("s", lambda eng: eng.activation(out=g_col, in_=small_c(c, 0, 8), func=AF.Sigmoid),
         reads=(bsm,), writes=(bsc,))
    k.op("v", lambda eng: eng.tensor_scalar(
        out=omg, in0=g_col, scalar1=-1.0, scalar2=1.0, op0=aop.mult, op1=aop.add),
        reads=(bsc,), writes=(bsc,))
    k.op("s", lambda eng: eng.activation(out=netf, in_=small_c(c, 25, 1)[0:64, :], func=AF.Exp),
         reads=(bsm,), writes=(bsc,))
    k.op("v", lambda eng: eng.tensor_scalar_mul(out=netf, in0=netf, scalar1=-1.0),
         reads=(bsc,), writes=(bsc,))
    wbs = take_span(c, "wb", 0, 4096, 3)
    kw_s = c.wbr[:, 0:512]
    vw_s = c.wbr[:, 512:1024]
    ow_s = c.wbr[:, 1024:2048]
    k.dma("p", kw_s, c.hbm["l2a_kwT"], writes=(wbs[0],))
    k.dma("p", vw_s, c.hbm["l2a_vwT"], writes=(wbs[1],))
    k.dma("p", ow_s[0:64, :], c.hbm["l2a_owT"], writes=(wbs[2],))
    xsb = [c.big_b[:, i * T:(i + 1) * T] for i in range(DKT)]
    bxs = take_span(c, "big", 0, 32768, DKT)
    S1 = c.reg(OFF_BIG + 32768, 8192, F32)
    S2 = c.reg(OFF_BIG + 40960, 8192, F32)
    S3 = c.reg(OFF_BIG + 49152, 8192, F32)
    S4 = c.reg(OFF_BIG + 57344, 8192, F32)
    bS1, bS2, bS3, bS4 = take_span(c, "big", 32768, 65536, 4)
    sqb = take_span(c, "wb", 4096, 6144, 1)[0]
    sq = c.reg(OFF_WB + 4096, 2048, F32)
    onecol = c.ones_f[:, 0:1]
    for kt in range(DKT):
        for half in range(2):
            t0, to = half * 1024, (1 - half) * 1024
            k.op("v", lambda eng, kt=kt, t0=t0, to=to: eng.tensor_scalar_mul(
                out=S1[:, t0:t0 + 1024], in0=xt(c, kt)[:, to:to + 1024],
                scalar1=g_col[:, kt:kt + 1]),
                reads=(c.bx[kt], bsc), writes=(bS1,))
            k.op("v", lambda eng, kt=kt, t0=t0: eng.scalar_tensor_tensor(
                out=S1[:, t0:t0 + 1024], in0=xt(c, kt)[:, t0:t0 + 1024],
                scalar=omg[:, kt:kt + 1], in1=S1[:, t0:t0 + 1024],
                op0=aop.mult, op1=aop.add),
                reads=(c.bx[kt], bsc, bS1), writes=(bS1,))
        for ci in range(4):
            k.op("t", lambda eng, kt=kt, ci=ci: eng.matmul(
                out=pbank(c, 4 + ci)[0:1, :], lhsT=onecol,
                rhs=S1[:, ci * 512:(ci + 1) * 512],
                start=(kt == 0), stop=(kt == DKT - 1)),
                reads=(bS1, c.bconst), writes=(c.bps[4 + ci],))
            k.op("s", lambda eng, ci=ci: eng.activation(
                out=sq, in_=S1[:, ci * 512:(ci + 1) * 512], func=AF.Square),
                reads=(bS1,), writes=(sqb,))
            k.op("t", lambda eng, kt=kt, ci=ci: eng.matmul(
                out=pbank(c, ci)[0:1, :], lhsT=onecol, rhs=sq,
                start=(kt == 0), stop=(kt == DKT - 1)),
                reads=(sqb, c.bconst), writes=(c.bps[ci],))
        k.op("s", lambda eng, kt=kt: eng.copy(out=xsb[kt], in_=S1),
             reads=(bS1,), writes=(bxs[kt],))
    r_mu, r_aux = row_slot(c, 0), row_slot(c, 1)
    brmu = take_span(c, "rows", 0, 16384, 2)
    for ci in range(4):
        k.op("s", lambda eng, ci=ci: eng.mul(
            out=r_mu[:, ci * 512:(ci + 1) * 512], in_=pbank(c, 4 + ci)[0:1, :], mul=1.0 / D),
            reads=(c.bps[4 + ci],), writes=(brmu[0],))
        k.op("s", lambda eng, ci=ci: eng.mul(
            out=r_aux[:, ci * 512:(ci + 1) * 512], in_=pbank(c, ci)[0:1, :], mul=1.0 / D),
            reads=(c.bps[ci],), writes=(brmu[1],))
    msq = c.psum[0:1, 0:2048]
    k.op("v", lambda eng: eng.tensor_tensor(out=msq, in0=r_mu, in1=r_mu, op=aop.mult),
         reads=(brmu[0],), writes=(c.bps[0], c.bps[1], c.bps[2], c.bps[3]))
    k.op("v", lambda eng: eng.tensor_tensor(out=r_aux, in0=r_aux, in1=msq, op=aop.subtract),
         reads=(brmu[1], c.bps[0], c.bps[1], c.bps[2], c.bps[3]), writes=(brmu[1],))
    k.op("s", lambda eng: eng.activation(out=r_aux, in_=r_aux, func=AF.Ln,
                                         bias=c.eps_col[0:1, :]),
         reads=(brmu[1], c.beps), writes=(brmu[1],))
    k.op("s", lambda eng: eng.activation(out=r_aux, in_=r_aux, func=AF.Exp, scale=-0.5),
         reads=(brmu[1],), writes=(brmu[1],))
    k.op("v", lambda eng: eng.tensor_tensor(out=r_mu, in0=r_mu, in1=r_aux, op=aop.mult),
         reads=(brmu[0], brmu[1]), writes=(brmu[0],))
    for ci in range(4):
        k.op("t", lambda eng, ci=ci: eng.matmul(
            out=pbank(c, ci), lhsT=c.ones_f[0:1, 0:128],
            rhs=r_aux[:, ci * 512:(ci + 1) * 512], start=True, stop=True),
            reads=(brmu[1], c.bconst), writes=(c.bps[ci],))
        k.op("t", lambda eng, ci=ci: eng.matmul(
            out=pbank(c, 4 + ci), lhsT=c.ones_f[0:1, 0:128],
            rhs=r_mu[:, ci * 512:(ci + 1) * 512], start=True, stop=True),
            reads=(brmu[0], c.bconst), writes=(c.bps[4 + ci],))
    rstd_bc = c.psum[:, 0:2048]
    mur_bc = c.psum[:, 2048:4096]
    for kt in range(DKT):
        k.op("v", lambda eng, kt=kt: eng.tensor_tensor(
            out=xsb[kt], in0=xsb[kt], in1=mur_bc, op=aop.subtract),
            reads=(bxs[kt], c.bps[4], c.bps[5], c.bps[6], c.bps[7]),
            writes=(bxs[kt],))
        k.op("v", lambda eng, kt=kt: eng.tensor_tensor(
            out=xsb[kt], in0=xsb[kt], in1=rstd_bc, op=aop.mult),
            reads=(bxs[kt], c.bps[0], c.bps[1], c.bps[2], c.bps[3]),
            writes=(bxs[kt],))
        k.op("v", lambda eng, kt=kt: eng.tensor_scalar(
            out=xsb[kt], in0=xsb[kt], scalar1=small_c(c, 8 + kt, 1),
            scalar2=small_c(c, 16 + kt, 1), op0=aop.mult, op1=aop.add),
            reads=(bxs[kt], bsm), writes=(bxs[kt],))
    for which, wsl, wbuf, banks in (("k", kw_s, wbs[0], (0, 1, 2, 3)),
                                    ("v", vw_s, wbs[1], (4, 5, 6, 7))):
        for ci in range(4):
            emits = []
            for kt in range(DKT):
                o = pbank(c, banks[ci])[0:64, :]
                lh = wsl[:, kt * 64:(kt + 1) * 64]
                rh = xsb[kt][:, ci * 512:(ci + 1) * 512]
                emits.append(lambda eng, o=o, lh=lh, rh=rh, s=(kt == 0),
                             p=(kt == DKT - 1): eng.matmul(out=o, lhsT=lh, rhs=rh,
                                                           start=s, stop=p))
            mm_acc(c, [c.bps[banks[ci]]], emits, [wbuf] + bxs)
    k.op("s", lambda eng: eng.activation(
        out=S2[0:64, :], in_=c.psum[0:64, 0:2048], func=AF.Exp, scale=netf),
        reads=(c.bps[0], c.bps[1], c.bps[2], c.bps[3], bsc), writes=(bS2,))
    k.op("v", lambda eng: eng.tensor_tensor(
        out=S2[0:64, :], in0=S2[0:64, :], in1=c.psum[0:64, 2048:4096], op=aop.mult),
        reads=(bS2, c.bps[4], c.bps[5], c.bps[6], c.bps[7]), writes=(bS2,))
    k.op("s", lambda eng: eng.activation(
        out=S3[0:64, :], in_=S2[0:64, :], func=AF.Exp, scale=0.0,
        bias=small_c(c, 24, 1)[0:64, :]),
        reads=(bS2, bsm), writes=(bS3,))
    k.op("v", lambda eng: eng.tensor_tensor_scan(
        out=S4[0:64, :], data0=S3[0:64, :], data1=S2[0:64, :], initial=0.0,
        op0=aop.mult, op1=aop.add),
        reads=(bS3, bS2), writes=(bS4,))
    stb = c.big_b[:, 0:2048]
    k.op("s", lambda eng: eng.copy(out=stb[0:64, :], in_=S4[0:64, :]),
         reads=(bS4,), writes=(bxs[0],))
    for ot in range(DKT):
        bk = [(ot % 2) * 4, (ot % 2) * 4 + 1, (ot % 2) * 4 + 2, (ot % 2) * 4 + 3]
        emits = []
        for ci in range(4):
            o = pbank(c, bk[ci])
            lh = ow_s[0:64, ot * 128:(ot + 1) * 128]
            rh = stb[0:64, ci * 512:(ci + 1) * 512]
            emits.append(lambda eng, o=o, lh=lh, rh=rh: eng.matmul(
                out=o, lhsT=lh, rhs=rh, start=True, stop=True))
        mm_acc(c, [c.bps[b] for b in bk], emits, [wbs[2], bxs[0]])
        k.op("v", lambda eng, ot=ot, b0=bk[0]: eng.tensor_tensor(
            out=xt(c, ot), in0=xt(c, ot), in1=c.psum[:, b0 * 512:b0 * 512 + 2048],
            op=aop.add),
            reads=tuple(c.bps[b] for b in bk) + (c.bx[ot],), writes=(c.bx[ot],))
    post_ln(c)


EMIT["l2a"] = emit_l2a

def emit_l0a(c):
    k = c.k
    bsm = take_span(c, "small", 0, 4096, 1)[0]
    k.dma("y", small_c(c, 0, 16), c.hbm["l0a_bg"], writes=(bsm,))
    k.dma("y", small_c(c, 16, 32), c.hbm["l0a_bin"], reads=(bsm,), writes=(bsm,))
    k.dma("y", small_c(c, 48, 8), c.hbm["l0a_bo"], reads=(bsm,), writes=(bsm,))
    k.dma("y", small_c(c, 56, 16), c.hbm["l0a_dv"], reads=(bsm,), writes=(bsm,))
    k.dma("y", small_c(c, 100, 256), c.hbm["l0a_alog"], reads=(bsm,), writes=(bsm,))
    k.dma("y", small_c(c, 612, 256), c.hbm["l0a_bmat"], reads=(bsm,), writes=(bsm,))
    c0bc = small_c(c, 868, 16)
    k.dma("y", c0bc, c.hbm["l0a_c0"].partition_broadcast(128).squeeze(1),
          reads=(bsm,), writes=(bsm,))
    acol = small_c(c, 100, 256)
    cbcol = small_c(c, 612, 256)
    carry = small_c(c, 356, 256)
    bcar = take_span(c, "small", 1424, 2448, 1)[0]
    k.op("s", lambda eng: eng.activation(out=acol, in_=acol, func=AF.Exp),
         reads=(bsm,), writes=(bsm,))
    k.op("v", lambda eng: eng.tensor_scalar_mul(out=acol, in0=acol, scalar1=-1.0),
         reads=(bsm,), writes=(bsm,))
    for s in range(16):
        k.op("v", lambda eng, s=s: eng.tensor_tensor(
            out=cbcol[:, s * 16:(s + 1) * 16], in0=cbcol[:, s * 16:(s + 1) * 16],
            in1=c0bc, op=aop.mult),
            reads=(bsm,), writes=(bsm,))
    XPS = c.reg(OFF_BIG + 32768, 4096, F32)
    DLS = c.reg(OFF_BIG + 36864, 4096, F32)
    YS = c.reg(OFF_BIG + 40960, 2048, BF16)
    US = c.reg(OFF_BIG + 45056, 2048, BF16)
    HSs = [c.reg(OFF_BIG + 47104, 2048, BF16), c.reg(OFF_BIG + 51200, 2048, BF16)]
    GTS = c.reg(OFF_BIG + 49152, 2048, BF16)
    bXP, bDL, bY, bU, bGT = take_span(c, "big", 32768, 51200, 5)
    bHS = [take_span(c, "big", 47104, 49152, 1)[0],
           take_span(c, "big", 51200, 53248, 1)[0]]
    gzb = take_span(c, "big", 0, 32768, 16)
    gz = [c.big_b[:, i * 1024:(i + 1) * 1024] for i in range(16)]
    wgTr = c.hbm["l0a_wgT"]
    winTr = c.hbm["l0a_winT"]
    woTr = c.hbm["l0a_woT"]
    for tb in range(2):
        t0 = tb * 1024
        wbs = take_span(c, "wb", 0, 12288, 6)
        sl_xp = [c.wbr[:, 0:1024], c.wbr[:, 1024:2048]]
        sl_dl = [c.wbr[:, 2048:3072], c.wbr[:, 3072:4096]]
        sl_gt = [c.wbr[:, 4096:5120], c.wbr[:, 5120:6144]]
        for s in range(16):
            sl = s % 2
            k.dma("p", sl_xp[sl], winTr[s], writes=(wbs[sl],))
            k.dma("p", sl_dl[sl], winTr[16 + s], writes=(wbs[2 + sl],))
            k.dma("p", sl_gt[sl], wgTr[s], writes=(wbs[4 + sl],))
            for (wsl_, wbuf, b0) in ((sl_xp[sl], wbs[sl], 0),
                                     (sl_dl[sl], wbs[2 + sl], 2),
                                     (sl_gt[sl], wbs[4 + sl], 4)):
                for ci in range(2):
                    emits = []
                    for kt in range(DKT):
                        o = pbank(c, b0 + ci)
                        lh = wsl_[:, kt * 128:(kt + 1) * 128]
                        rh = xbft(c, kt)[:, t0 + ci * 512:t0 + (ci + 1) * 512]
                        emits.append(lambda eng, o=o, lh=lh, rh=rh, s_=(kt == 0),
                                     p=(kt == DKT - 1): eng.matmul(
                            out=o, lhsT=lh, rhs=rh, start=s_, stop=p))
                    mm_acc(c, [c.bps[b0 + ci]], emits, [wbuf] + list(c.bxf))
            k.op("v", lambda eng, s=s: eng.tensor_scalar_add(
                out=XPS, in0=c.psum[:, 0:1024], scalar1=small_c(c, 16 + s, 1)),
                reads=(c.bps[0], c.bps[1], bsm), writes=(bXP,))
            k.op("s", lambda eng, s=s: eng.activation(
                out=DLS, in_=c.psum[:, 1024:2048], func=AF.Relu,
                bias=small_c(c, 32 + s, 1)),
                reads=(c.bps[2], c.bps[3], bsm), writes=(bDL,))
            k.op("s", lambda eng, s=s: eng.activation(
                out=GTS, in_=c.psum[:, 2048:3072], func=AF.Sigmoid,
                bias=small_c(c, 0 + s, 1)),
                reads=(c.bps[4], c.bps[5], bsm), writes=(bGT,))
            k.op("v", lambda eng: eng.tensor_tensor(
                out=US, in0=XPS, in1=DLS, op=aop.mult),
                reads=(bXP, bDL), writes=(bU,))
            for n in range(16):
                sn = s * 16 + n
                pr = n % 2
                k.op("s", lambda eng, sn=sn, pr=pr: eng.activation(
                    out=HSs[pr], in_=DLS, func=AF.Exp, scale=acol[:, sn:sn + 1]),
                    reads=(bDL, bsm), writes=(bHS[pr],))
                if tb == 0:
                    k.op("v", lambda eng, pr=pr: eng.tensor_tensor_scan(
                        out=HSs[pr], data0=HSs[pr], data1=US, initial=0.0,
                        op0=aop.mult, op1=aop.add),
                        reads=(bU,), writes=(bHS[pr],))
                else:
                    k.op("v", lambda eng, sn=sn, pr=pr: eng.tensor_tensor_scan(
                        out=HSs[pr], data0=HSs[pr], data1=US, initial=carry[:, sn:sn + 1],
                        op0=aop.mult, op1=aop.add),
                        reads=(bU, bcar), writes=(bHS[pr],))
                if tb == 0:
                    k.op("v", lambda eng, sn=sn, pr=pr: eng.tensor_copy(
                        out=carry[:, sn:sn + 1], in_=HSs[pr][:, 1023:1024]),
                        reads=(bHS[pr],), writes=(bcar,))
                if n == 0:
                    k.op("v", lambda eng, sn=sn, pr=pr: eng.tensor_scalar_mul(
                        out=YS, in0=HSs[pr], scalar1=cbcol[:, sn:sn + 1]),
                        reads=(bHS[pr], bsm), writes=(bY,))
                else:
                    k.op("v", lambda eng, sn=sn, pr=pr: eng.scalar_tensor_tensor(
                        out=YS, in0=HSs[pr], scalar=cbcol[:, sn:sn + 1], in1=YS,
                        op0=aop.mult, op1=aop.add),
                        reads=(bHS[pr], bsm, bY), writes=(bY,))
            k.op("v", lambda eng, s=s: eng.scalar_tensor_tensor(
                out=YS, in0=XPS, scalar=small_c(c, 56 + s, 1), in1=YS,
                op0=aop.mult, op1=aop.add),
                reads=(bXP, bsm, bY), writes=(bY,))
            k.op("v", lambda eng, s=s: eng.tensor_tensor(
                out=gz[s], in0=GTS, in1=YS, op=aop.mult),
                reads=(bGT, bY), writes=(gzb[s],))
        wbo = take_span(c, "wb", 8192, 16384, 2)
        sl_wo = [c.wbr[:, 4096:6144], c.wbr[:, 6144:8192]]
        for ot in range(DKT):
            sl = ot % 2
            k.dma("p", sl_wo[sl], woTr[ot], writes=(wbo[sl],))
            b0 = 6 if ot % 2 == 0 else 6
            bk = [6, 7]
            emits = []
            for ci in range(2):
                for kt in range(16):
                    o = pbank(c, bk[ci])
                    lh = sl_wo[sl][:, kt * 128:(kt + 1) * 128]
                    rh = gz[kt][:, ci * 512:(ci + 1) * 512]
                    emits.append(lambda eng, o=o, lh=lh, rh=rh, s_=(kt == 0),
                                 p=(kt == 15): eng.matmul(out=o, lhsT=lh, rhs=rh,
                                                          start=s_, stop=p))
            mm_acc(c, [c.bps[6], c.bps[7]], emits, [wbo[sl]] + gzb)
            k.op("v", lambda eng, ot=ot, t0=t0: eng.scalar_tensor_tensor(
                out=xt(c, ot)[:, t0:t0 + 1024], in0=c.psum[:, 6 * 512:6 * 512 + 1024],
                scalar=small_c(c, 48 + ot, 1), in1=xt(c, ot)[:, t0:t0 + 1024],
                op0=aop.add, op1=aop.add),
                reads=(c.bps[6], c.bps[7], bsm, c.bx[ot]), writes=(c.bx[ot],))
    post_ln(c)


EMIT["l0a"] = emit_l0a

def emit_l3a(c):
    k = c.k
    bsm = take_span(c, "small", 0, 1504, 1)[0]
    k.dma("y", small_c(c, 0, 8), c.hbm["l3a_normw"], writes=(bsm,))
    k.dma("y", small_c(c, 8, 16), c.hbm["l3a_cb"], reads=(bsm,), writes=(bsm,))
    k.dma("y", small_c(c, 24, 64), c.hbm["l3a_cw"], reads=(bsm,), writes=(bsm,))
    k.dma("y", small_c(c, 88, 16), c.hbm["l3a_dtb"], reads=(bsm,), writes=(bsm,))
    k.dma("y", small_c(c, 104, 256), c.hbm["l3a_alog"], reads=(bsm,), writes=(bsm,))
    k.dma("y", small_c(c, 360, 16), c.hbm["l3a_dv"], reads=(bsm,), writes=(bsm,))
    acol = small_c(c, 104, 256)
    k.op("s", lambda eng: eng.activation(out=acol, in_=acol, func=AF.Exp),
         reads=(bsm,), writes=(bsm,))
    k.op("v", lambda eng: eng.tensor_scalar_mul(out=acol, in0=acol, scalar1=-1.0),
         reads=(bsm,), writes=(bsm,))
    sqb = take_span(c, "wb", 0, 2048, 1)[0]
    sq = c.reg(OFF_WB, 2048, F32)
    onecol = c.ones_f[:, 0:1]
    for ci in range(4):
        for kt in range(DKT):
            k.op("s", lambda eng, ci=ci, kt=kt: eng.activation(
                out=sq, in_=xt(c, kt)[:, ci * 512:(ci + 1) * 512], func=AF.Square),
                reads=(c.bx[kt],), writes=(sqb,))
            k.op("t", lambda eng, ci=ci, kt=kt: eng.matmul(
                out=pbank(c, 4 + ci)[0:1, :], lhsT=onecol, rhs=sq,
                start=(kt == 0), stop=(kt == DKT - 1)),
                reads=(sqb, c.bconst), writes=(c.bps[4 + ci],))
    r_aux = row_slot(c, 1)
    brr = take_span(c, "rows", 8192, 16384, 1)[0]
    for ci in range(4):
        k.op("s", lambda eng, ci=ci: eng.copy(
            out=r_aux[:, ci * 512:(ci + 1) * 512], in_=pbank(c, 4 + ci)[0:1, :]),
            reads=(c.bps[4 + ci],), writes=(brr,))
    k.op("s", lambda eng: eng.activation(out=r_aux, in_=r_aux, func=AF.Ln),
         reads=(brr,), writes=(brr,))
    k.op("s", lambda eng: eng.activation(out=r_aux, in_=r_aux, func=AF.Exp, scale=0.5),
         reads=(brr,), writes=(brr,))
    k.op("s", lambda eng: eng.mul(out=r_aux, in_=r_aux, mul=1.0 / 32.0),
         reads=(brr,), writes=(brr,))
    k.op("v", lambda eng: eng.tensor_scalar_add(out=r_aux, in0=r_aux, scalar1=1e-5),
         reads=(brr,), writes=(brr,))
    k.op("v", lambda eng: eng.reciprocal(out=r_aux, in_=r_aux),
         reads=(brr,), writes=(brr,))

    def bc_hn_and_apply(hn_tiles, hn_bufs):
        for ci in range(4):
            k.op("t", lambda eng, ci=ci: eng.matmul(
                out=pbank(c, ci), lhsT=c.ones_f[0:1, 0:128],
                rhs=r_aux[:, ci * 512:(ci + 1) * 512], start=True, stop=True),
                reads=(brr, c.bconst), writes=(c.bps[ci],))
        rbc = c.psum[:, 0:2048]
        for kt in range(DKT):
            k.op("v", lambda eng, kt=kt: eng.tensor_tensor(
                out=hn_tiles[kt], in0=xt(c, kt), in1=rbc, op=aop.mult),
                reads=(c.bx[kt], c.bps[0], c.bps[1], c.bps[2], c.bps[3]),
                writes=(hn_bufs[kt],))
            k.op("v", lambda eng, kt=kt: eng.tensor_scalar_mul(
                out=hn_tiles[kt], in0=hn_tiles[kt], scalar1=small_c(c, kt, 1)),
                reads=(hn_bufs[kt], bsm), writes=(hn_bufs[kt],))

    hnb = take_span(c, "xbf", 0, 32768, DKT)
    hnt = [c.xbf_borrow_b[:, i * T:(i + 1) * T] for i in range(DKT)]
    bc_hn_and_apply(hnt, hnb)
    xcb = take_span(c, "big", 0, 65536, 16)
    xc = [c.big_b[:, i * T:(i + 1) * T] for i in range(16)]
    winTr = c.hbm["l3a_winT"]
    wbx = take_span(c, "wb", 0, 4096, 2)
    sl_x = [c.reg(OFF_WB, 2048, BF16), c.reg(OFF_WB + 2048, 2048, BF16)]
    xpad = c.reg(OFF_WB + 4096, 8448, F32)
    acc = c.reg(OFF_WB + 12800, 2048, F32)
    bxp = take_span(c, "wb", 4096, 12544, 1)[0]
    bacc = take_span(c, "wb", 12800, 14848, 1)[0]
    k.op("v", lambda eng: eng.memset(xpad[:, 0:3], 0.0), writes=(bxp,))
    for s in range(16):
        sl = s % 2
        k.dma("p", sl_x[sl], winTr[s], writes=(wbx[sl],))
        for ci in range(4):
            emits = []
            for kt in range(DKT):
                o = pbank(c, ci)
                lh = sl_x[sl][:, kt * 128:(kt + 1) * 128]
                rh = hnt[kt][:, ci * 512:(ci + 1) * 512]
                emits.append(lambda eng, o=o, lh=lh, rh=rh, s_=(kt == 0),
                             p=(kt == DKT - 1): eng.matmul(out=o, lhsT=lh, rhs=rh,
                                                           start=s_, stop=p))
            mm_acc(c, [c.bps[ci]], emits, [wbx[sl]] + hnb)
        k.op("v", lambda eng: eng.tensor_copy(out=xpad[:, 3:2051], in_=c.psum[:, 0:2048]),
             reads=(c.bps[0], c.bps[1], c.bps[2], c.bps[3]), writes=(bxp,))
        for ci in range(4):
            k.op("v", lambda eng, s=s, ci=ci: eng.tensor_scalar_mul(
                out=acc, in0=xpad[:, ci * 512:ci * 512 + 512],
                scalar1=small_c(c, 24 + s * 4, 1)),
                reads=(bxp, bsm), writes=(bacc,))
            for kk in range(1, 4):
                k.op("v", lambda eng, s=s, ci=ci, kk=kk: eng.scalar_tensor_tensor(
                    out=acc, in0=xpad[:, ci * 512 + kk:ci * 512 + kk + 512],
                    scalar=small_c(c, 24 + s * 4 + kk, 1), in1=acc,
                    op0=aop.mult, op1=aop.add),
                    reads=(bxp, bsm, bacc), writes=(bacc,))
            k.op("s", lambda eng, s=s, ci=ci: eng.activation(
                out=xc[s][:, ci * 512:(ci + 1) * 512], in_=acc, func=AF.Silu,
                bias=small_c(c, 8 + s, 1)),
                reads=(bacc, bsm), writes=(xcb[s],))
    wbB = [take_span(c, "wb", 0, 3072, 1)[0], take_span(c, "wb", 3072, 7168, 1)[0],
           take_span(c, "wb", 7168, 11264, 1)[0]]
    xpT_s = c.reg(OFF_WB, 3072, BF16)
    k.dma("p", xpT_s, c.hbm["l3a_xpT"], writes=(wbB[0],))
    for ci in range(4):
        emits = []
        for kt in range(16):
            o = pbank(c, 4 + ci)[0:96, :]
            lh = xpT_s[:, kt * 96:(kt + 1) * 96]
            rh = xc[kt][:, ci * 512:(ci + 1) * 512] if False else None
            rh = xc[kt][:, ci * 512:(ci + 1) * 512]
            emits.append(lambda eng, o=o, lh=lh, rh=rh, s_=(kt == 0),
                         p=(kt == 15): eng.matmul(out=o, lhsT=lh, rhs=rh,
                                                  start=s_, stop=p))
        mm_acc(c, [c.bps[4 + ci]], emits, [wbB[0]] + xcb)
    bcst = c.reg(OFF_XBF, 8192, F32)
    bst = take_span(c, "xbf", 0, 8192, 1)[0]
    k.op("v", lambda eng: eng.tensor_copy(
        out=bcst[0:32, :], in_=c.psum[64:96, 2048:4096]),
        reads=(c.bps[4], c.bps[5], c.bps[6], c.bps[7]), writes=(bst,))
    k.dma("p", c.hrow.bitcast(BF16)[0:32, 0:2048], bcst[0:32, :],
          reads=(bst,), writes=(c.bhrow,))
    dtr_bf = c.reg(OFF_WB + 3072, 4096, BF16)
    k.op("s", lambda eng: eng.copy(out=dtr_bf[0:64, :], in_=c.psum[0:64, 2048:4096]),
         reads=(c.bps[4], c.bps[5], c.bps[6], c.bps[7]), writes=(wbB[1],))
    dtwT_s = c.reg(OFF_WB + 7168, 4096, BF16)
    k.dma("p", dtwT_s[0:64, :], c.hbm["l3a_dtwT"], writes=(wbB[2],))
    DTB = c.reg(OFF_XBF, 4096, BF16)
    US = c.reg(OFF_XBF + 4096, 4096, BF16)
    YS = c.reg(OFF_XBF + 8192, 4096, BF16)
    HS = c.reg(OFF_XBF + 12288, 4096, BF16)
    UBS = c.reg(OFF_XBF + 16384, 4096, BF16)
    SPA1 = c.reg(OFF_XBF + 12288, 8192, F32)
    SPA2 = c.reg(OFF_XBF + 20480, 8192, F32)
    BCs = [c.reg(OFF_XBF + 20480, 8192, BF16), c.reg(OFF_ROWS, 8192, BF16)]
    bDT, bU, bY = take_span(c, "xbf", 0, 12288, 3)
    bHS = take_span(c, "xbf", 12288, 16384, 1)[0]
    bUB = take_span(c, "xbf", 16384, 20480, 1)[0]
    bBC = [take_span(c, "xbf", 20480, 28672, 1)[0],
           take_span(c, "rows", 0, 8192, 1)[0]]
    hrow_bf = c.hrow.bitcast(BF16)
    for s in range(16):
        for ci in range(4):
            o = pbank(c, ci)
            lh = dtwT_s[0:64, s * 128:(s + 1) * 128]
            rh = dtr_bf[0:64, ci * 512:(ci + 1) * 512]
            k.op("t", lambda eng, o=o, lh=lh, rh=rh: eng.matmul(
                out=o, lhsT=lh, rhs=rh, start=True, stop=True),
                reads=(wbB[1], wbB[2]), writes=(c.bps[ci],))
        pall = c.psum[:, 0:2048]
        SPl = SPA1[:, 0:2048]
        SPr = SPA2[:, 0:2048]
        k.op("s", lambda eng, s=s: eng.activation(
            out=SPl, in_=pall, func=AF.Abs, bias=small_c(c, 88 + s, 1)),
            reads=(c.bps[0], c.bps[1], c.bps[2], c.bps[3], bsm),
            writes=(bHS, bUB))
        k.op("s", lambda eng: eng.activation(out=SPl, in_=SPl, func=AF.Exp, scale=-1.0),
             reads=(bHS,), writes=(bHS, bUB))
        k.op("s", lambda eng: eng.activation(out=SPl, in_=SPl, func=AF.Ln, bias=1.0),
             reads=(bHS,), writes=(bHS, bUB))
        k.op("s", lambda eng, s=s: eng.activation(
            out=SPr, in_=pall, func=AF.Relu, bias=small_c(c, 88 + s, 1)),
            reads=(c.bps[0], c.bps[1], c.bps[2], c.bps[3], bsm),
            writes=(bBC[0],))
        k.op("v", lambda eng: eng.tensor_tensor(out=DTB, in0=SPl, in1=SPr, op=aop.add),
             reads=(bHS, bBC[0]), writes=(bDT,))
        k.op("v", lambda eng, s=s: eng.tensor_tensor(
            out=US, in0=DTB, in1=xc[s], op=aop.mult),
            reads=(bDT, xcb[s]), writes=(bU,))
        for n in range(16):
            sn = s * 16 + n
            pr = n % 2
            Bb = BCs[pr][:, 0:2048]
            Cb = BCs[pr][:, 2048:4096]
            k.dma("p", BCs[pr].rearrange("p (two f) -> p two f", two=2),
                  hrow_bf[n:n + 17:16, 0:2048].partition_broadcast(128).squeeze(1)
                  if False else
                  hrow_bf[n:n + 17:16, 0:2048].partition_broadcast(128),
                  reads=(c.bhrow,), writes=(bBC[pr],))
            k.op("v", lambda eng, pr=pr, Bb=Bb: eng.tensor_tensor(
                out=UBS, in0=US, in1=Bb, op=aop.mult),
                reads=(bU, bBC[pr]), writes=(bUB,))
            k.op("s", lambda eng, sn=sn: eng.activation(
                out=HS, in_=DTB, func=AF.Exp, scale=acol[:, sn:sn + 1]),
                reads=(bDT, bsm), writes=(bHS,))
            k.op("v", lambda eng: eng.tensor_tensor_scan(
                out=HS, data0=HS, data1=UBS, initial=0.0,
                op0=aop.mult, op1=aop.add),
                reads=(bUB,), writes=(bHS,))
            k.op("v", lambda eng, pr=pr, Cb=Cb: eng.tensor_tensor(
                out=UBS, in0=HS, in1=Cb, op=aop.mult),
                reads=(bHS, bBC[pr]), writes=(bUB,))
            if n == 0:
                k.op("v", lambda eng: eng.tensor_copy(out=YS, in_=UBS),
                     reads=(bUB,), writes=(bY,))
            else:
                k.op("v", lambda eng: eng.tensor_tensor(
                    out=YS, in0=YS, in1=UBS, op=aop.add),
                    reads=(bY, bUB), writes=(bY,))
        k.op("v", lambda eng, s=s: eng.scalar_tensor_tensor(
            out=xc[s], in0=xc[s], scalar=small_c(c, 360 + s, 1), in1=YS,
            op0=aop.mult, op1=aop.add),
            reads=(xcb[s], bsm, bY), writes=(xcb[s],))
    hnb2 = take_span(c, "xbf", 0, 32768, DKT)
    bc_hn_and_apply(hnt, hnb2)
    wbz = take_span(c, "wb", 0, 4096, 2)
    sl_z = [c.reg(OFF_WB, 2048, BF16), c.reg(OFF_WB + 2048, 2048, BF16)]
    zf = c.reg(OFF_WB + 4096, 8192, F32)
    bzf = take_span(c, "wb", 4096, 12288, 1)[0]
    for s in range(16):
        sl = s % 2
        k.dma("p", sl_z[sl], winTr[16 + s], writes=(wbz[sl],))
        for ci in range(4):
            emits = []
            for kt in range(DKT):
                o = pbank(c, 4 + ci)
                lh = sl_z[sl][:, kt * 128:(kt + 1) * 128]
                rh = hnt[kt][:, ci * 512:(ci + 1) * 512]
                emits.append(lambda eng, o=o, lh=lh, rh=rh, s_=(kt == 0),
                             p=(kt == DKT - 1): eng.matmul(out=o, lhsT=lh, rhs=rh,
                                                           start=s_, stop=p))
            mm_acc(c, [c.bps[4 + ci]], emits, [wbz[sl]] + hnb2)
        k.op("s", lambda eng: eng.activation(out=zf, in_=c.psum[:, 2048:4096], func=AF.Silu),
             reads=(c.bps[4], c.bps[5], c.bps[6], c.bps[7]), writes=(bzf,))
        k.op("v", lambda eng, s=s: eng.tensor_tensor(
            out=xc[s], in0=xc[s], in1=zf, op=aop.mult),
            reads=(xcb[s], bzf), writes=(xcb[s],))
    woTr = c.hbm["l3a_woT"]
    wbo = take_span(c, "wb", 0, 8192, 2)
    sl_o = [c.reg(OFF_WB, 4096, BF16), c.reg(OFF_WB + 4096, 4096, BF16)]
    for ot in range(DKT):
        sl = ot % 2
        k.dma("p", sl_o[sl], woTr[ot], writes=(wbo[sl],))
        for ci in range(4):
            bk = ci
            emits = []
            for kt in range(16):
                o = pbank(c, bk)
                lh = sl_o[sl][:, kt * 128:(kt + 1) * 128]
                rh = xc[kt][:, ci * 512:(ci + 1) * 512]
                emits.append(lambda eng, o=o, lh=lh, rh=rh, s_=(kt == 0),
                             p=(kt == 15): eng.matmul(out=o, lhsT=lh, rhs=rh,
                                                      start=s_, stop=p))
            mm_acc(c, [c.bps[bk]], emits, [wbo[sl]] + xcb)
        k.op("v", lambda eng, ot=ot: eng.tensor_tensor(
            out=xt(c, ot), in0=xt(c, ot), in1=c.psum[:, 0:2048], op=aop.add),
            reads=(c.bps[0], c.bps[1], c.bps[2], c.bps[3], c.bx[ot]),
            writes=(c.bx[ot],))
    post_ln(c)


EMIT["l3a"] = emit_l3a









def load_x_tmajor(c):
    k = c.k
    tscr = c.rowsr[:, 0:1024]
    c.brow[0] = take_span(c, "rows", 0, 4096, 1)[0]
    for tt in range(T // 128):
        dtok = k.dma("y", tscr, c.x_in[tt * 128:(tt + 1) * 128, :],
                     writes=(c.brow[0],))
        for kt in range(DKT):
            b = kt % 4
            k.op("t", lambda eng, kt=kt, b=b: eng.transpose(
                out=pbank(c, b)[:, 0:128], in_=tscr[:, kt * 128:(kt + 1) * 128],
                identity=c.ident_f[:, :]),
                reads=(c.brow[0], c.bconst), writes=(c.bps[b],))
            k.op("v", lambda eng, kt=kt, tt=tt, b=b: eng.tensor_copy(
                out=xt(c, kt)[:, tt * 128:tt * 128 + 128], in_=pbank(c, b)[:, 0:128]),
                reads=(c.bps[b],), writes=(c.bx[kt],))
    cast_x_bf(c)


def store_x_tmajor(c):
    k = c.k
    tscr = c.rowsr[:, 0:1024]
    c.brow[0] = take_span(c, "rows", 0, 4096, 1)[0]
    for tt in range(T // 128):
        for kt in range(DKT):
            b = 4 + kt % 4
            k.op("t", lambda eng, kt=kt, tt=tt, b=b: eng.transpose(
                out=pbank(c, b)[:, 0:128], in_=xt(c, kt)[:, tt * 128:tt * 128 + 128],
                identity=c.ident_f[:, :]),
                reads=(c.bx[kt], c.bconst), writes=(c.bps[b],))
            k.op("v", lambda eng, kt=kt, b=b: eng.tensor_copy(
                out=tscr[:, kt * 128:(kt + 1) * 128], in_=pbank(c, b)[:, 0:128]),
                reads=(c.bps[b],), writes=(c.brow[0],))
        k.dma("y", c.x_out[tt * 128:(tt + 1) * 128, :], tscr,
              reads=(c.brow[0],))


WNEEDS = {
    "l0a": ["l0a_wgT", "l0a_winT", "l0a_woT", "l0a_bg", "l0a_bin", "l0a_bo",
            "l0a_alog", "l0a_bmat", "l0a_c0", "l0a_dv"],
    "l0f": ["l0f_w1", "l0f_b1", "l0f_w2", "l0f_b2"],
    "l1a": ["l1a_wq", "l1a_wk", "l1a_wv", "l1a_wo", "l1a_bq", "l1a_bk",
            "l1a_bv", "l1a_bo"],
    "l1f": ["l1f_wgT", "l1f_wuT", "l1f_wdT", "l1f_wrT", "l1f_rb"],
    "l2a": ["l2a_tsg", "l2a_lnw", "l2a_lnb", "l2a_kwT", "l2a_vwT", "l2a_owT",
            "l2a_td", "l2a_tf"],
    "l2f": ["l2f_wg", "l2f_w1", "l2f_w2", "l2f_bg", "l2f_b1", "l2f_b2",
            "l2f_centT"],
    "l3a": ["l3a_normw", "l3a_winT", "l3a_cw", "l3a_cb", "l3a_xpT",
            "l3a_dtwT", "l3a_dtb", "l3a_alog", "l3a_dv", "l3a_woT"],
    "l3f": ["l3f_w1T", "l3f_w2T", "l3f_h1T", "l3f_h2T", "l3f_w3T",
            "l3f_s1", "l3f_s2", "l3f_h1s", "l3f_h2s", "l3f_s3"],
}
CONST_W = ["ident_b", "ident_f", "ones_b", "ones_f"]


def build_program(layers, wspecs, first, last):
    nc = bass.Bass("TRN2", target_bir_lowering=False)
    es = contextlib.ExitStack()
    with es:
        c = build_ctx(nc, es, wspecs, first, last)
        k = c.k
        if first:
            load_x_tmajor(c)
        else:
            for kt in range(DKT):
                k.dma("y", xt(c, kt), c.x_in[kt * 128:(kt + 1) * 128, :],
                      writes=(c.bx[kt],))
            cast_x_bf(c)
        for name in layers:
            EMIT[name](c)
        if last:
            store_x_tmajor(c)
        else:
            for kt in range(DKT):
                k.dma("y", c.x_out[kt * 128:(kt + 1) * 128, :], xt(c, kt),
                      reads=(c.bx[kt],))
        toks = k.barrier_tokens()
        k.op("y", lambda eng: eng.nop(), extra=toks)
        k.finalize()
    return nc


def kernel(**inputs):
    w = prep_weights(inputs)
    wspecs = {name: (arr.shape, BF16 if arr.dtype == BF else F32)
              for name, arr in w.items()}
    xs = [np.ascontiguousarray(np.asarray(inputs["L0_attn_x"])[i].astype(np.float32))
          for i in range(NCORES)]
    if FUSE:
        groups = [list(LAYERS)]
    else:
        groups = [[l] for l in LAYERS]
    for gi, group in enumerate(groups):
        first = gi == 0
        last = gi == len(groups) - 1
        names = list(CONST_W)
        for l in group:
            names += WNEEDS[l]
        gw = {n: w[n] for n in names}
        gs = {n: wspecs[n] for n in names}
        nc = build_program(group, gs, first, last)
        in_maps = [dict(gw, x_in=xs[i]) for i in range(NCORES)]
        res = run_bass_kernel_spmd(nc, in_maps, core_ids=list(range(NCORES)))
        xs = [np.asarray(res.results[i]["x_out"]) for i in range(NCORES)]
    out = np.stack(xs, 0).astype(np.float32)
    return out


# version 06 (final)
# speedup vs baseline: 1.2746x; 1.0071x over previous
import contextlib
import numpy as np
import ml_dtypes
import concourse.bass as bass
import concourse.mybir as mybir
from concourse.bass_utils import run_bass_kernel_spmd
from concourse.alu_op_type import AluOpType as aop

F32 = mybir.dt.float32
BF16 = mybir.dt.bfloat16
AF = mybir.ActivationFunctionType
AX = mybir.AxisListType

NCORES = 8
T = 2048
D = 1024
DKT = D // 128
BF = ml_dtypes.bfloat16

LAYERS = ["l0a", "l0f", "l1a", "l1f", "l2a", "l2f", "l3a", "l3f"]
FUSE = True
DEBUG_BARRIER = False


class Buf:
    __slots__ = ("name", "w", "readers", "guard")

    def __init__(self, name, guard=()):
        self.name = name
        self.w = None
        self.readers = []
        self.guard = list(guard)


def toks_of(bufs):
    out = []
    for b in bufs:
        if b.w is not None:
            out.append(b.w)
        out.extend(b.readers)
        out.extend(b.guard)
    return out


class K:
    ENGS = ("t", "v", "s", "p", "y")

    def __init__(self, nc, es):
        self.nc = nc
        self.sem = {e: es.enter_context(nc.semaphore(f"sem_{e}")) for e in self.ENGS}
        self.cnt = {e: 0 for e in self.ENGS}
        self.dsem = {q: es.enter_context(nc.semaphore(f"dsem_{q}")) for q in ("y", "p", "s")}
        self.dcnt = {q: 0 for q in ("y", "p", "s")}
        self.waited = {e: {} for e in self.ENGS}
        self.prog = {e: [] for e in self.ENGS}

    def _wait(self, e, tok):
        key, handle, val = tok
        if self.waited[e].get(key, 0) >= val:
            return
        self.waited[e][key] = val
        self.prog[e].append(lambda eng, h=handle, v=val: eng.wait_ge(h, v))

    def _deps(self, e, reads, writes, extra):
        deps = []
        for b in reads:
            if b.w is not None:
                deps.append(b.w)
            deps.extend(b.guard)
        deps.extend(toks_of(writes))
        deps.extend(extra)
        best = {}
        for tok in deps:
            key = tok[0]
            if key not in best or best[key][2] < tok[2]:
                best[key] = tok
        for tok in best.values():
            self._wait(e, tok)

    def _post(self, tok, reads, writes):
        for b in reads:
            b.readers.append(tok)
        for b in writes:
            b.w = tok
            b.readers = []
            b.guard = []

    def op(self, e, emit, reads=(), writes=(), extra=()):
        if DEBUG_BARRIER:
            extra = list(extra) + self.barrier_tokens()
        self._deps(e, reads, writes, extra)
        self.cnt[e] += 1
        sem = self.sem[e]
        self.prog[e].append(lambda eng, f=emit, s=sem: f(eng).then_inc(s, 1))
        tok = (("e", e), sem, self.cnt[e])
        self._post(tok, reads, writes)
        return tok

    def group(self, e, emits, reads=(), writes=(), extra=()):
        if DEBUG_BARRIER:
            extra = list(extra) + self.barrier_tokens()
        self._deps(e, reads, writes, extra)
        for f in emits[:-1]:
            self.prog[e].append(lambda eng, f=f: f(eng))
        self.cnt[e] += 1
        sem = self.sem[e]
        last = emits[-1]
        self.prog[e].append(lambda eng, f=last, s=sem: f(eng).then_inc(s, 1))
        tok = (("e", e), sem, self.cnt[e])
        self._post(tok, reads, writes)
        return tok

    def dma(self, q, out_ap, in_ap, reads=(), writes=(), extra=()):
        if DEBUG_BARRIER:
            extra = list(extra) + self.barrier_tokens()
        self._deps(q, reads, writes, extra)
        self.dcnt[q] += 1
        dsem = self.dsem[q]
        self.prog[q].append(
            lambda eng, o=out_ap, i=in_ap, s=dsem: eng.dma_start(out=o, in_=i).then_inc(s, 16))
        tok = (("d", q), dsem, 16 * self.dcnt[q])
        self._post(tok, reads, writes)
        return tok

    def barrier_tokens(self):
        toks = []
        for e in self.ENGS:
            if self.cnt[e]:
                toks.append((("e", e), self.sem[e], self.cnt[e]))
        for q in self.dsem:
            if self.dcnt[q]:
                toks.append((("d", q), self.dsem[q], 16 * self.dcnt[q]))
        return toks

    def finalize(self):
        nc = self.nc
        with nc.Block() as block:
            for e in self.ENGS:
                items = self.prog[e]
                if not items:
                    continue
                dec = {"t": block.tensor, "v": block.vector, "s": block.scalar,
                       "p": block.gpsimd, "y": block.sync}[e]

                def body(eng, items=items):
                    for f in items:
                        f(eng)
                dec(body)


def _t(a):
    return np.ascontiguousarray(np.asarray(a).T)


def _cols(vec, ntiles):
    return np.ascontiguousarray(np.asarray(vec).reshape(ntiles, 128).T)


def _sm(WT, nkt, not_):
    WT = np.asarray(WT)
    return np.ascontiguousarray(
        WT.reshape(nkt, 128, not_, 128).transpose(2, 1, 0, 3)
        .reshape(not_, 128, nkt * 128))


def _pk(WT, nkt):
    WT = np.asarray(WT)
    K, O = WT.shape
    return np.ascontiguousarray(WT.reshape(nkt, 128, O).transpose(1, 0, 2).reshape(128, nkt * O))


def _slabcols(mat, nslab, nn):
    m = np.asarray(mat).reshape(nslab, 128, nn)
    return np.ascontiguousarray(m.transpose(1, 0, 2).reshape(128, nslab * nn))


def prep_weights(inp):
    w = {}
    w["ident_b"] = np.eye(128, dtype=BF)
    w["ident_f"] = np.eye(128, dtype=np.float32)
    w["ones_b"] = np.ones((128, 128), dtype=BF)
    w["ones_f"] = np.ones((128, 128), dtype=np.float32)
    w["l0a_wgT"] = _sm(_t(inp["L0_attn_W_gate"]), 8, 16)
    w["l0a_winT"] = _sm(_t(inp["L0_attn_W_in"]), 8, 32)
    w["l0a_woT"] = _sm(_t(inp["L0_attn_W_out"]), 16, 8)
    w["l0a_bg"] = _cols(inp["L0_attn_b_gate"], 16)
    w["l0a_bin"] = _cols(inp["L0_attn_b_in"], 32)
    w["l0a_bo"] = _cols(inp["L0_attn_b_out"], 8)
    w["l0a_alog"] = _slabcols(inp["L0_attn_A_log"], 16, 16)
    w["l0a_bmat"] = _slabcols(inp["L0_attn_B_mat"], 16, 16)
    w["l0a_c0"] = np.ascontiguousarray(np.asarray(inp["L0_attn_C_mat"])[0:1, :])
    w["l0a_dv"] = _cols(inp["L0_attn_D_vec"], 16)
    w["l0f_w1"] = _sm(inp["L0_ffn_W1"], 8, 32)
    w["l0f_b1"] = _cols(inp["L0_ffn_b1"], 32)
    w["l0f_w2"] = _sm(inp["L0_ffn_W2"], 32, 8)
    w["l0f_b2"] = _cols(inp["L0_ffn_b2"], 8)
    w["l1a_wq"] = _pk(inp["L1_attn_wq"], 8)
    w["l1a_wk"] = _pk(inp["L1_attn_wk"], 8)
    w["l1a_wv"] = _pk(inp["L1_attn_wv"], 8)
    w["l1a_wo"] = np.ascontiguousarray(inp["L1_attn_wo"])
    w["l1a_bq"] = np.asarray(inp["L1_attn_bq"]).reshape(64, 1)
    w["l1a_bk"] = np.asarray(inp["L1_attn_bk"]).reshape(64, 1)
    w["l1a_bv"] = np.asarray(inp["L1_attn_bv"]).reshape(1, 64)
    w["l1a_bo"] = _cols(inp["L1_attn_bo"], 8)
    wg = np.concatenate([np.asarray(inp["L1_ffn_Wg_s"])[None], np.asarray(inp["L1_ffn_Wg"])], 0)
    wu = np.concatenate([np.asarray(inp["L1_ffn_Wu_s"])[None], np.asarray(inp["L1_ffn_Wu"])], 0)
    wd = np.concatenate([np.asarray(inp["L1_ffn_Wd_s"])[None], np.asarray(inp["L1_ffn_Wd"])], 0)
    w["l1f_wgT"] = np.stack([_sm(wg[e].T, 8, 16) for e in range(9)])
    w["l1f_wuT"] = np.stack([_sm(wu[e].T, 8, 16) for e in range(9)])
    w["l1f_wdT"] = np.stack([_sm(wd[e].T, 16, 8) for e in range(9)])
    w["l1f_wrT"] = _pk(_t(inp["L1_ffn_Wr"]), 8)
    w["l1f_rb"] = np.asarray(inp["L1_ffn_rb"]).reshape(8, 1)
    w["l2a_tsg"] = _cols(inp["L2_attn_time_shift_gates"], 8)
    w["l2a_lnw"] = _cols(inp["L2_attn_ln_w"], 8)
    w["l2a_lnb"] = _cols(inp["L2_attn_ln_b"], 8)
    w["l2a_kwT"] = _pk(_t(inp["L2_attn_key_w"]), 8)
    w["l2a_vwT"] = _pk(_t(inp["L2_attn_value_w"]), 8)
    w["l2a_owT"] = _t(inp["L2_attn_output_w"])
    w["l2a_td"] = np.asarray(inp["L2_attn_time_decay"]).reshape(64, 1)
    w["l2a_tf"] = np.asarray(inp["L2_attn_time_first"]).reshape(64, 1)
    H2, H2P = 2736, 2816
    wg2 = np.zeros((8, 1024, H2P), np.float32)
    w12 = np.zeros((8, 1024, H2P), np.float32)
    w22 = np.zeros((8, H2P, 1024), np.float32)
    bg2 = np.zeros((8, H2P), np.float32)
    b12 = np.zeros((8, H2P), np.float32)
    wg2[:, :, :H2] = inp["L2_ffn_Wg"]
    w12[:, :, :H2] = inp["L2_ffn_W1"]
    w22[:, :H2, :] = inp["L2_ffn_W2"]
    bg2[:, :H2] = inp["L2_ffn_bg"]
    b12[:, :H2] = inp["L2_ffn_b1"]
    w["l2f_wg"] = np.stack([_sm(wg2[e], 8, 22) for e in range(8)])
    w["l2f_w1"] = np.stack([_sm(w12[e], 8, 22) for e in range(8)])
    w["l2f_w2"] = np.stack([_sm(w22[e], 22, 8) for e in range(8)])
    w["l2f_bg"] = np.ascontiguousarray(bg2.reshape(8, 22, 128).transpose(0, 2, 1))
    w["l2f_b1"] = np.ascontiguousarray(b12.reshape(8, 22, 128).transpose(0, 2, 1))
    w["l2f_b2"] = np.ascontiguousarray(
        np.asarray(inp["L2_ffn_b2"]).reshape(8, 8, 128).transpose(0, 2, 1))
    w["l2f_centT"] = _pk(_t(inp["L2_ffn_centroid"]), 8)
    w["l3a_normw"] = _cols(inp["L3_attn_norm_weight"], 8)
    w["l3a_winT"] = _sm(_t(inp["L3_attn_in_proj_w"]), 8, 32)
    w["l3a_cw"] = np.ascontiguousarray(
        np.asarray(inp["L3_attn_conv_w"]).reshape(16, 128, 4).transpose(1, 0, 2).reshape(128, 64))
    w["l3a_cb"] = _cols(inp["L3_attn_conv_b"], 16)
    w["l3a_xpT"] = _pk(_t(inp["L3_attn_x_proj_w"]), 16)
    w["l3a_dtwT"] = _t(inp["L3_attn_dt_proj_w"])
    w["l3a_dtb"] = _cols(inp["L3_attn_dt_proj_b"], 16)
    w["l3a_alog"] = _slabcols(inp["L3_attn_A_log"], 16, 16)
    w["l3a_dv"] = _cols(inp["L3_attn_D"], 16)
    w["l3a_woT"] = _sm(_t(inp["L3_attn_out_proj_w"]), 16, 8)
    w["l3f_w1T"] = _sm(_t(inp["L3_ffn_w1_w"]), 8, 32)
    w["l3f_w2T"] = _sm(_t(inp["L3_ffn_w2_w"]), 8, 32)
    w["l3f_h1T"] = _sm(_t(inp["L3_ffn_h1_w"]), 32, 32)
    w["l3f_h2T"] = _sm(_t(inp["L3_ffn_h2_w"]), 32, 32)
    w["l3f_w3T"] = _sm(_t(inp["L3_ffn_w3_w"]), 32, 8)
    w["l3f_s1"] = _cols(inp["L3_ffn_w1_s"], 32)
    w["l3f_s2"] = _cols(inp["L3_ffn_w2_s"], 32)
    w["l3f_h1s"] = _cols(inp["L3_ffn_h1_s"], 32)
    w["l3f_h2s"] = _cols(inp["L3_ffn_h2_s"], 32)
    w["l3f_s3"] = _cols(inp["L3_ffn_w3_s"], 8)
    return w


ARENA_BYTES = 200 * 1024
OFF_XSB = 0
OFF_XBF = 65536
OFF_WB = 98304
OFF_SMALL = 114688
OFF_ROWS = 118784
OFF_CONST = 135168
OFF_BIG = 137216
BIG_BYTES = ARENA_BYTES - OFF_BIG
BIGC = BIG_BYTES // 2


class Ctx:
    pass


def build_ctx(nc, es, weight_names, first, last):
    c = Ctx()
    c.nc = nc
    k = K(nc, es)
    c.k = k
    c.hbm = {}
    for name, (shape, dt) in weight_names.items():
        c.hbm[name] = nc.dram_tensor(name, list(shape), dt, kind="ExternalInput").ap()
    c.x_in = nc.dram_tensor("x_in", [T, D] if first else [D, T], F32,
                            kind="ExternalInput").ap()
    c.x_out = nc.dram_tensor("x_out", [T, D] if last else [D, T], F32,
                             kind="ExternalOutput").ap()
    c.hrow = nc.dram_tensor("hrow", [64, T], F32, kind="Internal").ap()

    c.arena_h = es.enter_context(nc.sbuf_tensor("arena", [128, ARENA_BYTES], mybir.dt.uint8))
    c.psum_h = es.enter_context(nc.psum_tensor("ps", [128, 4096], F32))
    c.aap = c.arena_h.ap()
    c.psum = c.psum_h

    def reg(off, nbytes, dt):
        return c.aap[:, off:off + nbytes].bitcast(dt)

    c.reg = reg
    c.x_sb = reg(OFF_XSB, 65536, F32)
    c.x_bf = reg(OFF_XBF, 32768, BF16)
    c.wbr = reg(OFF_WB, 16384, BF16)
    c.smallr = reg(OFF_SMALL, 4096, F32)
    c.rowsr = reg(OFF_ROWS, 16384, F32)
    c.big_b = reg(OFF_BIG, BIG_BYTES, BF16)
    c.big_f = reg(OFF_BIG, BIG_BYTES, F32)
    c.xbf_borrow_b = reg(OFF_XBF, 32768, BF16)
    c.xbf_borrow_f = reg(OFF_XBF, 32768, F32)

    c.ones_f = reg(OFF_CONST, 512, F32)
    c.ones_b = reg(OFF_CONST + 512, 256, BF16)
    c.ident_f = reg(OFF_CONST + 768, 512, F32)
    c.ident_b = reg(OFF_CONST + 1280, 256, BF16)

    c.bx = [Buf(f"x{i}") for i in range(DKT)]
    c.bxf = [Buf(f"xf{i}") for i in range(DKT)]
    c.bps = [Buf(f"pb{i}") for i in range(8)]
    c.brow = [Buf(f"row{i}") for i in range(3)]
    c.bsmall = [Buf(f"sm{i}") for i in range(4)]
    c.bwb = [Buf(f"wb{i}") for i in range(8)]
    c.bconst = Buf("const")
    c.bhrow = Buf("hrow")
    c.spans = []

    c.eps_col = c.smallr[:, 1020:1021]
    c.beps = Buf("eps")
    k.op("v", lambda eng: eng.memset(c.eps_col, 1e-5), writes=(c.beps,))
    k.dma("y", c.ones_f[:, :], c.hbm["ones_f"], writes=(c.bconst,))
    k.dma("y", c.ones_b[:, :], c.hbm["ones_b"], extra=toks_of([c.bconst]))
    k.dma("y", c.ident_f[:, :], c.hbm["ident_f"], extra=toks_of([c.bconst]))
    k.dma("y", c.ident_b[:, :], c.hbm["ident_b"], extra=toks_of([c.bconst]))
    return c


def take_span(c, arena, lo, hi, nbufs):
    guard = []
    for (a, l, h, bufs) in c.spans:
        if a == arena and not (h <= lo or hi <= l):
            guard.extend(toks_of(bufs))
    out = [Buf(f"{arena}:{lo}:{i}", guard=guard) for i in range(nbufs)]
    c.spans.append((arena, lo, hi, out))
    return out


def row_slot(c, i):
    return c.rowsr[0:1, i * 2048:(i + 1) * 2048]


def pbank(c, b, off=0, n=512):
    return c.psum[:, b * 512 + off: b * 512 + off + n]


def small_c(c, col0, ncols):
    return c.smallr[:, col0:col0 + ncols]


def xt(c, kt):
    return c.x_sb[:, kt * T:(kt + 1) * T]


def xbft(c, kt):
    return c.x_bf[:, kt * T:(kt + 1) * T]


def cast_x_bf(c):
    k = c.k
    nb = take_span(c, "xbf", 0, 32768, DKT)
    for kt in range(DKT):
        k.op("s", lambda eng, kt=kt: eng.copy(out=xbft(c, kt), in_=xt(c, kt)),
             reads=(c.bx[kt],), writes=(c.bxf[kt], nb[kt]))


def mm_acc(c, bank_bufs, emits, reads, extra=()):
    return c.k.group("t", emits, reads=tuple(reads), writes=tuple(bank_bufs), extra=extra)


def post_ln(c):
    k = c.k
    onecol_b = c.ones_b[:, 0:1]
    cast_x_bf(c)
    sqb = take_span(c, "wb", 0, 1024, 1)[0]
    sq = c.reg(OFF_WB, 1024, BF16)
    for ci in range(4):
        emits = []
        reads = [c.bconst]
        for kt in range(DKT):
            emits.append(lambda eng, ci=ci, kt=kt: eng.matmul(
                out=pbank(c, 4 + ci)[0:1, :],
                lhsT=onecol_b, rhs=xbft(c, kt)[:, ci * 512:(ci + 1) * 512],
                start=(kt == 0), stop=(kt == DKT - 1)))
            reads.append(c.bxf[kt])
        mm_acc(c, [c.bps[4 + ci]], emits, reads)
    for ci in range(4):
        for kt in range(DKT):
            k.op("s", lambda eng, ci=ci, kt=kt: eng.activation(
                out=sq, in_=xbft(c, kt)[:, ci * 512:(ci + 1) * 512], func=AF.Square),
                reads=(c.bxf[kt],), writes=(sqb,))
            k.op("t", lambda eng, ci=ci, kt=kt: eng.matmul(
                out=pbank(c, ci)[0:1, :], lhsT=onecol_b, rhs=sq,
                start=(kt == 0), stop=(kt == DKT - 1)),
                reads=(sqb, c.bconst), writes=(c.bps[ci],))
    r_mu, r_aux = row_slot(c, 0), row_slot(c, 1)
    rb_ = take_span(c, "rows", 0, 16384, 2)
    c.brow[0], c.brow[1] = rb_[0], rb_[1]
    for ci in range(4):
        k.op("s", lambda eng, ci=ci: eng.mul(
            out=r_mu[:, ci * 512:(ci + 1) * 512], in_=pbank(c, 4 + ci)[0:1, :], mul=1.0 / D),
            reads=(c.bps[4 + ci],), writes=(c.brow[0],))
        k.op("s", lambda eng, ci=ci: eng.mul(
            out=r_aux[:, ci * 512:(ci + 1) * 512], in_=pbank(c, ci)[0:1, :], mul=1.0 / D),
            reads=(c.bps[ci],), writes=(c.brow[1],))
    msq = c.psum[0:1, 0:2048]
    k.op("v", lambda eng: eng.tensor_tensor(out=msq, in0=r_mu, in1=r_mu, op=aop.mult),
         reads=(c.brow[0],), writes=(c.bps[0], c.bps[1], c.bps[2], c.bps[3]))
    k.op("v", lambda eng: eng.tensor_tensor(out=r_aux, in0=r_aux, in1=msq, op=aop.subtract),
         reads=(c.brow[1], c.bps[0], c.bps[1], c.bps[2], c.bps[3]), writes=(c.brow[1],))
    k.op("s", lambda eng: eng.activation(out=r_aux, in_=r_aux, func=AF.Ln,
                                         bias=c.eps_col[0:1, :]),
         reads=(c.brow[1], c.beps), writes=(c.brow[1],))
    k.op("s", lambda eng: eng.activation(out=r_aux, in_=r_aux, func=AF.Exp, scale=-0.5),
         reads=(c.brow[1],), writes=(c.brow[1],))
    k.op("v", lambda eng: eng.tensor_tensor(out=r_mu, in0=r_mu, in1=r_aux, op=aop.mult),
         reads=(c.brow[0], c.brow[1]), writes=(c.brow[0],))
    for ci in range(4):
        k.op("t", lambda eng, ci=ci: eng.matmul(
            out=pbank(c, ci), lhsT=c.ones_f[0:1, 0:128],
            rhs=r_aux[:, ci * 512:(ci + 1) * 512], start=True, stop=True),
            reads=(c.brow[1], c.bconst), writes=(c.bps[ci],))
        k.op("t", lambda eng, ci=ci: eng.matmul(
            out=pbank(c, 4 + ci), lhsT=c.ones_f[0:1, 0:128],
            rhs=r_mu[:, ci * 512:(ci + 1) * 512], start=True, stop=True),
            reads=(c.brow[0], c.bconst), writes=(c.bps[4 + ci],))
    rstd_bc = c.psum[:, 0:2048]
    mur_bc = c.psum[:, 2048:4096]
    for kt in range(DKT):
        k.op("v", lambda eng, kt=kt: eng.tensor_tensor(
            out=xt(c, kt), in0=xt(c, kt), in1=mur_bc, op=aop.subtract),
            reads=(c.bx[kt], c.bps[4], c.bps[5], c.bps[6], c.bps[7]),
            writes=(c.bx[kt],))
        k.op("v", lambda eng, kt=kt: eng.tensor_tensor(
            out=xt(c, kt), in0=xt(c, kt), in1=rstd_bc, op=aop.mult),
            reads=(c.bx[kt], c.bps[0], c.bps[1], c.bps[2], c.bps[3]),
            writes=(c.bx[kt],))
    cast_x_bf(c)


def dense_mm(c, banks, bank_bufs, lh_tiles, lh_bufs, rhs_fn, rhs_bufs, width):
    emits = []
    nk = len(lh_tiles)
    for ci in range(width // 512):
        for kt in range(nk):
            out_ap = pbank(c, banks[ci])
            lh = lh_tiles[kt]
            rh = rhs_fn(kt, ci)
            emits.append(lambda eng, o=out_ap, lh=lh, rh=rh, s=(kt == 0),
                         p=(kt == nk - 1): eng.matmul(
                out=o, lhsT=lh, rhs=rh, start=s, stop=p))
    reads = list(lh_bufs) + list(rhs_bufs)
    return mm_acc(c, bank_bufs, emits, reads)


def emit_l0f(c):
    k = c.k
    ntb = 2
    tbs = T // ntb
    nch = tbs // 512
    bb = take_span(c, "small", 0, 160, 1)[0]
    k.dma("y", small_c(c, 0, 32), c.hbm["l0f_b1"], writes=(bb,))
    k.dma("y", small_c(c, 32, 8), c.hbm["l0f_b2"], reads=(bb,), writes=(bb,))
    hb = take_span(c, "big", 0, 65536, 32)
    hap = [c.big_b[:, i * tbs:(i + 1) * tbs] for i in range(32)]
    for tb in range(ntb):
        t0 = tb * tbs
        wb1 = take_span(c, "wb", 0, 4096, 2)
        wsl = [c.wbr[:, 0:1024], c.wbr[:, 1024:2048]]
        for ot in range(32):
            sl = ot % 2
            k.dma("p", wsl[sl], c.hbm["l0f_w1"][ot], writes=(wb1[sl],))
            banks = [(ot % 2) * 2, (ot % 2) * 2 + 1][:nch]
            dense_mm(c, banks, [c.bps[b] for b in banks],
                     [wsl[sl][:, j * 128:(j + 1) * 128] for j in range(DKT)],
                     [wb1[sl]],
                     lambda kt, ci: xbft(c, kt)[:, t0 + ci * 512:t0 + (ci + 1) * 512],
                     c.bxf, tbs)
            k.op("s", lambda eng, ot=ot, b0=banks[0]: eng.activation(
                out=hap[ot], in_=c.psum[:, b0 * 512: b0 * 512 + tbs],
                func=AF.Relu, bias=small_c(c, ot, 1)),
                reads=tuple(c.bps[b] for b in banks) + (bb,), writes=(hb[ot],))
        wb2 = take_span(c, "wb", 0, 16384, 2)
        wdl = [c.wbr[:, 0:4096], c.wbr[:, 4096:8192]]
        for ot in range(8):
            sl = ot % 2
            k.dma("p", wdl[sl], c.hbm["l0f_w2"][ot], writes=(wb2[sl],))
            banks = [4 + (ot % 2) * 2, 4 + (ot % 2) * 2 + 1][:nch]
            dense_mm(c, banks, [c.bps[b] for b in banks],
                     [wdl[sl][:, j * 128:(j + 1) * 128] for j in range(32)],
                     [wb2[sl]],
                     lambda kt, ci: hap[kt][:, ci * 512:(ci + 1) * 512],
                     hb, tbs)
            k.op("v", lambda eng, ot=ot, t0=t0, b0=banks[0]: eng.scalar_tensor_tensor(
                out=xt(c, ot)[:, t0:t0 + tbs],
                in0=c.psum[:, b0 * 512: b0 * 512 + tbs],
                scalar=small_c(c, 32 + ot, 1),
                in1=xt(c, ot)[:, t0:t0 + tbs], op0=aop.add, op1=aop.add),
                reads=tuple(c.bps[b] for b in banks) + (bb, c.bx[ot]),
                writes=(c.bx[ot],))
    post_ln(c)


EMIT = {"l0f": emit_l0f}

def emit_l1a(c):
    k = c.k
    bigv = lambda off, nb, dt: c.reg(OFF_BIG + off, nb, dt)
    qT = bigv(0, 4096, BF16)
    kT = bigv(4096, 4096, BF16)
    vr = bigv(8192, 2048, BF16)
    wbf = bigv(10240, 4096, BF16)
    scf = bigv(14336, 8192, F32)
    wTr = bigv(22528, 4096, BF16)
    zT = bigv(26624, 4096, BF16)
    zr = bigv(30720, 2048, BF16)
    bq_, bk_, bv_, bw_, bsc, bwT, bzT, bz = take_span(c, "big", 0, 32768, 8)
    wbs = take_span(c, "wb", 0, 5120, 4)
    wq_s, wk_s, wv_s = c.wbr[:, 0:512], c.wbr[:, 512:1024], c.wbr[:, 1024:1536]
    wo_s = c.wbr[:, 1536:2560]
    for nm, dst, bu in (("l1a_wq", wq_s, wbs[0]), ("l1a_wk", wk_s, wbs[1]),
                        ("l1a_wv", wv_s, wbs[2])):
        k.dma("p", dst, c.hbm[nm], writes=(bu,))
    k.dma("p", wo_s[0:64, :], c.hbm["l1a_wo"], writes=(wbs[3],))
    bsm = take_span(c, "small", 160, 200, 1)[0]
    k.dma("y", small_c(c, 40, 1)[0:64, :], c.hbm["l1a_bq"], writes=(bsm,))
    k.dma("y", small_c(c, 41, 1)[0:64, :], c.hbm["l1a_bk"], reads=(bsm,), writes=(bsm,))
    k.dma("y", small_c(c, 42, 8), c.hbm["l1a_bo"], reads=(bsm,), writes=(bsm,))
    bvbc = take_span(c, "rows", 4096, 4352, 1)[0]
    k.dma("y", c.rowsr[:, 1024:1088], c.hbm["l1a_bv"].partition_broadcast(128).squeeze(1),
          writes=(bvbc,))
    bq_col = small_c(c, 40, 1)[0:64, :]
    bk_col = small_c(c, 41, 1)[0:64, :]
    bv_bc = c.rowsr[:, 1024:1088]
    for which, wsl, wbuf, breg, regap, bcol, scale2 in (
            ("q", wq_s, wbs[0], bq_, qT, bq_col, 0.125),
            ("k", wk_s, wbs[1], bk_, kT, bk_col, 1.0)):
        for ci in range(4):
            emits = []
            for kt in range(DKT):
                o = pbank(c, ci)[0:64, :]
                lh = wsl[:, kt * 64:(kt + 1) * 64]
                rh = xbft(c, kt)[:, ci * 512:(ci + 1) * 512]
                emits.append(lambda eng, o=o, lh=lh, rh=rh, s=(kt == 0),
                             p=(kt == DKT - 1): eng.matmul(out=o, lhsT=lh, rhs=rh,
                                                           start=s, stop=p))
            mm_acc(c, [c.bps[ci]], emits, [wbuf] + list(c.bxf))
            k.op("v", lambda eng, ci=ci, regap=regap, bcol=bcol, s2=scale2:
                 eng.tensor_scalar(out=regap[0:64, ci * 512:(ci + 1) * 512],
                                   in0=pbank(c, ci)[0:64, :], scalar1=bcol,
                                   scalar2=s2, op0=aop.add, op1=aop.mult),
                 reads=(c.bps[ci], bsm), writes=(breg,))
    for it in range(16):
        b = 4 + it % 2
        emits = []
        for kt in range(DKT):
            o = pbank(c, b)[:, 0:64]
            lh = xbft(c, kt)[:, it * 128:(it + 1) * 128]
            rh = wv_s[:, kt * 64:(kt + 1) * 64]
            emits.append(lambda eng, o=o, lh=lh, rh=rh, s=(kt == 0),
                         p=(kt == DKT - 1): eng.matmul(out=o, lhsT=lh, rhs=rh,
                                                       start=s, stop=p))
        mm_acc(c, [c.bps[b]], emits, [wbs[2]] + list(c.bxf))
        k.op("v", lambda eng, it=it, b=b: eng.tensor_tensor(
            out=vr[:, it * 64:(it + 1) * 64], in0=pbank(c, b)[:, 0:64],
            in1=bv_bc, op=aop.add),
            reads=(c.bps[b], bvbc), writes=(bv_,))
    nm_col = small_c(c, 50, 1)
    den_col = small_c(c, 51, 1)
    rd_col = small_c(c, 52, 1)
    bnm = take_span(c, "small", 200, 212, 1)[0]
    for ib in range(16):
        for ci in range(4):
            o = pbank(c, ci)
            lh = qT[0:64, ib * 128:(ib + 1) * 128]
            rh = kT[0:64, ci * 512:(ci + 1) * 512]
            k.op("t", lambda eng, o=o, lh=lh, rh=rh: eng.matmul(
                out=o, lhsT=lh, rhs=rh, start=True, stop=True),
                reads=(bq_, bk_), writes=(c.bps[ci],))
        k.op("v", lambda eng: eng.tensor_copy(out=scf[:, :], in_=c.psum[:, 0:2048]),
             reads=(c.bps[0], c.bps[1], c.bps[2], c.bps[3]), writes=(bsc,))
        k.op("p", lambda eng, ib=ib: eng.affine_select(
            out=scf[:, :], in_=scf[:, :], pattern=[[-1, 2048]],
            compare_op=aop.is_ge, fill=-1e30, base=ib * 128, channel_multiplier=1),
            reads=(bsc,), writes=(bsc,))
        k.op("v", lambda eng: eng.tensor_reduce(
            out=nm_col, in_=scf[:, :], axis=AX.X, op=aop.max),
            reads=(bsc,), writes=(bnm,))
        k.op("v", lambda eng: eng.tensor_scalar_mul(out=nm_col, in0=nm_col, scalar1=-1.0),
             reads=(bnm,), writes=(bnm,))
        k.op("s", lambda eng: eng.activation(
            out=wbf[:, :], in_=scf[:, :], func=AF.Exp, bias=nm_col,
            accum_out=den_col),
            reads=(bsc, bnm), writes=(bw_, bnm))
        k.op("v", lambda eng: eng.reciprocal(out=rd_col, in_=den_col),
             reads=(bnm,), writes=(bnm,))
        zp = pbank(c, 6)[:, 0:64]
        emits = []
        for jt in range(ib + 1):
            b = 4 + jt % 2
            pt_bf = c.psum[:, b * 512:b * 512 + 256].bitcast(BF16)[:, 0:128]
            k.op("t", lambda eng, jt=jt, pt_bf=pt_bf: eng.transpose(
                out=pt_bf, in_=wbf[:, jt * 128:(jt + 1) * 128], identity=c.ident_b),
                reads=(bw_, c.bconst), writes=(c.bps[b],))
            k.op("v", lambda eng, jt=jt, pt_bf=pt_bf: eng.tensor_copy(
                out=wTr[:, jt * 128:(jt + 1) * 128], in_=pt_bf),
                reads=(c.bps[b],), writes=(bwT,))
            lh = wTr[:, jt * 128:(jt + 1) * 128]
            rh = vr[:, jt * 64:(jt + 1) * 64]
            emits.append(lambda eng, lh=lh, rh=rh, s=(jt == 0), p=(jt == ib):
                         eng.matmul(out=zp, lhsT=lh, rhs=rh, start=s, stop=p))
        mm_acc(c, [c.bps[6]], emits, [bwT, bv_])
        k.op("v", lambda eng, ib=ib, zp=zp: eng.tensor_scalar_mul(
            out=zr[:, ib * 64:(ib + 1) * 64], in0=zp, scalar1=rd_col),
            reads=(c.bps[6], bnm), writes=(bz,))
        ztp = c.psum[:, 7 * 512:7 * 512 + 256].bitcast(BF16)[:, 0:128]
        k.op("t", lambda eng, ib=ib: eng.transpose(
            out=ztp[0:64, :], in_=zr[:, ib * 64:(ib + 1) * 64], identity=c.ident_b),
            reads=(bz, c.bconst), writes=(c.bps[7],))
        k.op("v", lambda eng, ib=ib: eng.tensor_copy(
            out=zT[0:64, ib * 128:(ib + 1) * 128], in_=ztp[0:64, :]),
            reads=(c.bps[7],), writes=(bzT,))
    for ot in range(DKT):
        banks = [(ot % 2) * 2, (ot % 2) * 2 + 1, (ot % 2) * 2 + 8]
        bb2 = [(ot % 2) * 4, (ot % 2) * 4 + 1, (ot % 2) * 4 + 2, (ot % 2) * 4 + 3]
        emits = []
        for ci in range(4):
            o = pbank(c, bb2[ci])
            lh = wo_s[0:64, ot * 128:(ot + 1) * 128]
            rh = zT[0:64, ci * 512:(ci + 1) * 512]
            emits.append(lambda eng, o=o, lh=lh, rh=rh: eng.matmul(
                out=o, lhsT=lh, rhs=rh, start=True, stop=True))
        mm_acc(c, [c.bps[b] for b in bb2], emits, [wbs[3], bzT])
        k.op("v", lambda eng, ot=ot, b0=bb2[0]: eng.scalar_tensor_tensor(
            out=xt(c, ot), in0=c.psum[:, b0 * 512:b0 * 512 + 2048],
            scalar=small_c(c, 42 + ot, 1), in1=xt(c, ot), op0=aop.add, op1=aop.add),
            reads=tuple(c.bps[b] for b in bb2) + (bsm, c.bx[ot]),
            writes=(c.bx[ot],))
    post_ln(c)


EMIT["l1a"] = emit_l1a

def emit_l3f(c):
    k = c.k
    tbs = 512
    bsm = take_span(c, "small", 0, 544, 1)[0]
    k.dma("y", small_c(c, 0, 32), c.hbm["l3f_s1"], writes=(bsm,))
    for nm, c0 in (("l3f_s2", 32), ("l3f_h1s", 64), ("l3f_h2s", 96), ("l3f_s3", 128)):
        k.dma("y", small_c(c, c0, 32 if nm != "l3f_s3" else 8), c.hbm[nm],
              reads=(bsm,), writes=(bsm,))
    w1r = c.hbm["l3f_w1T"]
    w2r = c.hbm["l3f_w2T"]
    h1r = c.hbm["l3f_h1T"]
    h2r = c.hbm["l3f_h2T"]
    w3r = c.hbm["l3f_w3T"]
    lo = lambda i: c.big_b[:, i * tbs:(i + 1) * tbs]
    hi = lambda i: c.big_b[:, 16384 + i * tbs:16384 + (i + 1) * tbs]
    sg = c.rowsr[:, 1024:1536]
    tu = c.rowsr[:, 1536:2048]
    for tb in range(4):
        t0 = tb * tbs
        t1b = take_span(c, "big", 0, 32768, 32)
        wbs = take_span(c, "wb", 0, 8192, 4)
        wg = [c.wbr[:, 0:1024], c.wbr[:, 1024:2048]]
        wu = [c.wbr[:, 2048:3072], c.wbr[:, 3072:4096]]
        bscr = take_span(c, "rows", 4096, 8192, 1)[0]
        for ot in range(32):
            sl = ot % 2
            k.dma("p", wg[sl], w1r[ot], writes=(wbs[sl],))
            k.dma("p", wu[sl], w2r[ot], writes=(wbs[2 + sl],))
            bG, bU = sl * 2, sl * 2 + 1
            for (bk, wsl_, wbuf) in ((bG, wg[sl], wbs[sl]), (bU, wu[sl], wbs[2 + sl])):
                emits = []
                for kt in range(DKT):
                    o = pbank(c, bk)
                    lh = wsl_[:, kt * 128:(kt + 1) * 128]
                    rh = xbft(c, kt)[:, t0 + (0) * 512:t0 + 512]
                    emits.append(lambda eng, o=o, lh=lh, rh=rh, s=(kt == 0),
                                 p=(kt == DKT - 1): eng.matmul(
                        out=o, lhsT=lh, rhs=rh, start=s, stop=p))
                mm_acc(c, [c.bps[bk]], emits, [wbuf] + list(c.bxf))
            k.op("s", lambda eng, ot=ot, bG=bG: eng.activation(
                out=sg, in_=pbank(c, bG), func=AF.Silu, scale=small_c(c, ot, 1)),
                reads=(c.bps[bG], bsm), writes=(bscr,))
            k.op("v", lambda eng, ot=ot, bU=bU: eng.tensor_scalar_mul(
                out=tu, in0=pbank(c, bU), scalar1=small_c(c, 32 + ot, 1)),
                reads=(c.bps[bU], bsm), writes=(bscr,))
            k.op("v", lambda eng, ot=ot: eng.tensor_tensor(
                out=lo(ot), in0=sg, in1=tu, op=aop.mult),
                reads=(bscr,), writes=(t1b[ot],))
        h1b = take_span(c, "big", 32768, 65536, 32)
        wbs2 = take_span(c, "wb", 0, 16384, 2)
        wsl2 = [c.wbr[:, 0:4096], c.wbr[:, 4096:8192]]
        for ot in range(32):
            sl = ot % 2
            k.dma("p", wsl2[sl], h1r[ot], writes=(wbs2[sl],))
            bk = 4 + sl
            emits = []
            for kt in range(32):
                o = pbank(c, bk)
                lh = wsl2[sl][:, kt * 128:(kt + 1) * 128]
                rh = lo(kt)
                emits.append(lambda eng, o=o, lh=lh, rh=rh, s=(kt == 0),
                             p=(kt == 31): eng.matmul(out=o, lhsT=lh, rhs=rh,
                                                      start=s, stop=p))
            mm_acc(c, [c.bps[bk]], emits, [wbs2[sl]] + t1b)
            k.op("s", lambda eng, ot=ot, bk=bk: eng.activation(
                out=hi(ot), in_=pbank(c, bk), func=AF.Gelu, scale=small_c(c, 64 + ot, 1)),
                reads=(c.bps[bk], bsm), writes=(h1b[ot],))
        h2b = take_span(c, "big", 0, 32768, 32)
        wbs3 = take_span(c, "wb", 0, 16384, 2)
        for ot in range(32):
            sl = ot % 2
            k.dma("p", wsl2[sl], h2r[ot], writes=(wbs3[sl],))
            bk = 6 + sl
            emits = []
            for kt in range(32):
                o = pbank(c, bk)
                lh = wsl2[sl][:, kt * 128:(kt + 1) * 128]
                rh = hi(kt)
                emits.append(lambda eng, o=o, lh=lh, rh=rh, s=(kt == 0),
                             p=(kt == 31): eng.matmul(out=o, lhsT=lh, rhs=rh,
                                                      start=s, stop=p))
            mm_acc(c, [c.bps[bk]], emits, [wbs3[sl]] + h1b)
            k.op("s", lambda eng, ot=ot, bk=bk: eng.activation(
                out=lo(ot), in_=pbank(c, bk), func=AF.Gelu, scale=small_c(c, 96 + ot, 1)),
                reads=(c.bps[bk], bsm), writes=(h2b[ot],))
        wbs4 = take_span(c, "wb", 0, 16384, 2)
        for ot in range(8):
            sl = ot % 2
            k.dma("p", wsl2[sl], w3r[ot], writes=(wbs4[sl],))
            bk = sl
            emits = []
            for kt in range(32):
                o = pbank(c, bk)
                lh = wsl2[sl][:, kt * 128:(kt + 1) * 128]
                rh = lo(kt)
                emits.append(lambda eng, o=o, lh=lh, rh=rh, s=(kt == 0),
                             p=(kt == 31): eng.matmul(out=o, lhsT=lh, rhs=rh,
                                                      start=s, stop=p))
            mm_acc(c, [c.bps[bk]], emits, [wbs4[sl]] + h2b)
            k.op("v", lambda eng, ot=ot, t0=t0, bk=bk: eng.scalar_tensor_tensor(
                out=xt(c, ot)[:, t0:t0 + tbs], in0=pbank(c, bk),
                scalar=small_c(c, 128 + ot, 1), in1=xt(c, ot)[:, t0:t0 + tbs],
                op0=aop.mult, op1=aop.add),
                reads=(c.bps[bk], bsm, c.bx[ot]), writes=(c.bx[ot],))
    post_ln(c)


EMIT["l3f"] = emit_l3f

def emit_l1f(c):
    k = c.k
    wbs = take_span(c, "wb", 0, 128, 5)
    wr_s = c.wbr[:, 0:64]
    k.dma("p", wr_s, c.hbm["l1f_wrT"], writes=(wbs[4],))
    bsm = take_span(c, "small", 0, 32, 1)[0]
    k.dma("y", small_c(c, 0, 1)[0:8, :], c.hbm["l1f_rb"], writes=(bsm,))
    probs = c.rowsr[0:8, 0:2048]
    brows = take_span(c, "rows", 0, 16384, 1)[0]
    for ci in range(4):
        emits = []
        for kt in range(DKT):
            o = pbank(c, ci)[0:8, :]
            lh = wr_s[:, kt * 8:(kt + 1) * 8]
            rh = xbft(c, kt)[:, ci * 512:(ci + 1) * 512]
            emits.append(lambda eng, o=o, lh=lh, rh=rh, s=(kt == 0),
                         p=(kt == DKT - 1): eng.matmul(out=o, lhsT=lh, rhs=rh,
                                                       start=s, stop=p))
        mm_acc(c, [c.bps[ci]], emits, [wbs[4]] + list(c.bxf))
        k.op("s", lambda eng, ci=ci: eng.activation(
            out=probs[:, ci * 512:(ci + 1) * 512], in_=pbank(c, ci)[0:8, :],
            func=AF.Sigmoid, bias=small_c(c, 0, 1)[0:8, :]),
            reads=(c.bps[ci], bsm), writes=(brows,))
    cwr = c.rowsr[0:8, 2048:4096]
    pT = small_c(c, 8, 8)
    m1c, m2c = small_c(c, 16, 1), small_c(c, 17, 1)
    msk = small_c(c, 24, 8)
    bsc = take_span(c, "small", 32, 128, 1)[0]
    for it in range(16):
        pp = pbank(c, 4)[:, 0:8]
        k.op("t", lambda eng, it=it, pp=pp: eng.transpose(
            out=pp, in_=probs[:, it * 128:(it + 1) * 128], identity=c.ident_f[0:8, 0:8]),
            reads=(brows, c.bconst), writes=(c.bps[4],))
        k.op("v", lambda eng, pp=pp: eng.tensor_copy(out=pT, in_=pp),
             reads=(c.bps[4],), writes=(bsc,))
        k.op("v", lambda eng: eng.tensor_reduce(out=m1c, in_=pT, axis=AX.X, op=aop.max),
             reads=(bsc,), writes=(bsc,))
        k.op("v", lambda eng: eng.tensor_scalar(
            out=msk, in0=pT, scalar1=m1c, scalar2=-1e30, op0=aop.is_equal, op1=aop.mult),
            reads=(bsc,), writes=(bsc,))
        k.op("v", lambda eng: eng.tensor_tensor(out=msk, in0=pT, in1=msk, op=aop.add),
             reads=(bsc,), writes=(bsc,))
        k.op("v", lambda eng: eng.tensor_reduce(out=m2c, in_=msk, axis=AX.X, op=aop.max),
             reads=(bsc,), writes=(bsc,))
        k.op("v", lambda eng: eng.tensor_scalar(
            out=msk, in0=pT, scalar1=m2c, scalar2=1.0, op0=aop.is_ge, op1=aop.mult),
            reads=(bsc,), writes=(bsc,))
        k.op("v", lambda eng: eng.tensor_tensor(out=msk, in0=msk, in1=pT, op=aop.mult),
             reads=(bsc,), writes=(bsc,))
        ppb = pbank(c, 5)[0:8, 0:128]
        k.op("t", lambda eng, ppb=ppb: eng.transpose(
            out=ppb, in_=msk, identity=c.ident_f),
            reads=(bsc, c.bconst), writes=(c.bps[5],))
        k.op("v", lambda eng, it=it, ppb=ppb: eng.tensor_copy(
            out=cwr[:, it * 128:(it + 1) * 128], in_=ppb),
            reads=(c.bps[5],), writes=(brows,))
    k.dma("y", c.hrow[0:8, :], cwr, reads=(brows,), writes=(c.bhrow,))
    if globals().get("DEBUG_L1F_GATE"):
        k.op("v", lambda eng: eng.tensor_copy(out=xt(c, 0)[0:8, :], in_=probs),
             reads=(brows,), writes=(c.bx[0],))
        k.op("v", lambda eng: eng.tensor_copy(out=xt(c, 1)[0:8, :], in_=cwr),
             reads=(brows,), writes=(c.bx[1],))
        return
    cwbc = c.rowsr[:, 0:2048]
    bcw = take_span(c, "rows", 0, 8192, 1)[0]
    hb = None
    for e in range(9):
        hb = take_span(c, "big", 0, 65536, 16)
        hap = [c.big_b[:, i * T:(i + 1) * T] for i in range(16)]
        if e > 0:
            k.dma("y", cwbc, c.hrow[e - 1:e, :].partition_broadcast(128).squeeze(1),
                  reads=(c.bhrow,), writes=(bcw,))
        wgu = take_span(c, "wb", 0, 8192, 4)
        gsl = [c.wbr[:, 0:1024], c.wbr[:, 1024:2048]]
        usl = [c.wbr[:, 2048:3072], c.wbr[:, 3072:4096]]
        sgap = c.reg(OFF_ROWS + 8192, 2048, F32)
        bsg = take_span(c, "rows", 8192, 10240, 1)[0]
        wgr = c.hbm["l1f_wgT"]
        wur = c.hbm["l1f_wuT"]
        wdr = c.hbm["l1f_wdT"]
        for it in range(16):
            sl = it % 2
            k.dma("p", gsl[sl], wgr[e, it], writes=(wgu[sl],))
            k.dma("p", usl[sl], wur[e, it], writes=(wgu[2 + sl],))
            for ci in range(4):
                bG = sl * 2
                bU = sl * 2 + 1
                for (bk, wsl_, wbuf) in ((bG, gsl[sl], wgu[sl]), (bU, usl[sl], wgu[2 + sl])):
                    emits = []
                    for kt in range(DKT):
                        o = pbank(c, bk)
                        lh = wsl_[:, kt * 128:(kt + 1) * 128]
                        rh = xbft(c, kt)[:, ci * 512:(ci + 1) * 512]
                        emits.append(lambda eng, o=o, lh=lh, rh=rh, s=(kt == 0),
                                     p=(kt == DKT - 1): eng.matmul(
                            out=o, lhsT=lh, rhs=rh, start=s, stop=p))
                    mm_acc(c, [c.bps[bk]], emits, [wbuf] + list(c.bxf))
                k.op("s", lambda eng, bG=bG: eng.activation(
                    out=sgap, in_=pbank(c, bG), func=AF.Silu),
                    reads=(c.bps[bG],), writes=(bsg,))
                hout = hap[it][:, ci * 512:(ci + 1) * 512]
                if e == 0:
                    k.op("v", lambda eng, bU=bU, hout=hout: eng.tensor_tensor(
                        out=hout, in0=sgap, in1=pbank(c, bU), op=aop.mult),
                        reads=(bsg, c.bps[bU]), writes=(hb[it],))
                else:
                    k.op("v", lambda eng, bU=bU: eng.tensor_tensor(
                        out=sgap, in0=sgap, in1=pbank(c, bU), op=aop.mult),
                        reads=(bsg, c.bps[bU]), writes=(bsg,))
                    cwc = cwbc[:, ci * 512:(ci + 1) * 512]
                    k.op("v", lambda eng, hout=hout, cwc=cwc: eng.tensor_tensor(
                        out=hout, in0=sgap, in1=cwc, op=aop.mult),
                        reads=(bsg, bcw), writes=(hb[it],))
        wdd = take_span(c, "wb", 8192, 16384, 2)
        dsl = [c.wbr[:, 4096:6144], c.wbr[:, 6144:8192]]
        for ot in range(8):
            sl = ot % 2
            k.dma("p", dsl[sl], wdr[e, ot], writes=(wdd[sl],))
            for ci in range(4):
                bk = 4 + ci
                emits = []
                for kt in range(16):
                    o = pbank(c, bk)
                    lh = dsl[sl][:, kt * 128:(kt + 1) * 128]
                    rh = hap[kt][:, ci * 512:(ci + 1) * 512]
                    emits.append(lambda eng, o=o, lh=lh, rh=rh, s=(kt == 0),
                                 p=(kt == 15): eng.matmul(out=o, lhsT=lh, rhs=rh,
                                                          start=s, stop=p))
                mm_acc(c, [c.bps[bk]], emits, [wdd[sl]] + hb)
            k.op("v", lambda eng, ot=ot: eng.tensor_tensor(
                out=xt(c, ot), in0=xt(c, ot), in1=c.psum[:, 2048:4096], op=aop.add),
                reads=(c.bps[4], c.bps[5], c.bps[6], c.bps[7], c.bx[ot]),
                writes=(c.bx[ot],))
    post_ln(c)


EMIT["l1f"] = emit_l1f

def emit_l2f(c):
    k = c.k
    sqb = take_span(c, "wb", 0, 2048, 1)[0]
    sq = c.reg(OFF_WB, 2048, F32)
    onecol = c.ones_f[:, 0:1]
    for ci in range(4):
        for kt in range(DKT):
            k.op("s", lambda eng, ci=ci, kt=kt: eng.activation(
                out=sq, in_=xt(c, kt)[:, ci * 512:(ci + 1) * 512], func=AF.Square),
                reads=(c.bx[kt],), writes=(sqb,))
            k.op("t", lambda eng, ci=ci, kt=kt: eng.matmul(
                out=pbank(c, 4 + ci)[0:1, :], lhsT=onecol, rhs=sq,
                start=(kt == 0), stop=(kt == DKT - 1)),
                reads=(sqb, c.bconst), writes=(c.bps[4 + ci],))
    x2row = c.rowsr[0:1, 0:2048]
    brx = take_span(c, "rows", 0, 16384, 1)[0]
    for ci in range(4):
        k.op("s", lambda eng, ci=ci: eng.copy(
            out=x2row[:, ci * 512:(ci + 1) * 512], in_=pbank(c, 4 + ci)[0:1, :]),
            reads=(c.bps[4 + ci],), writes=(brx,))
    wbs = take_span(c, "wb", 0, 128, 3)
    centT = c.wbr[:, 0:64]
    k.dma("p", centT, c.hbm["l2f_centT"], writes=(wbs[0],))
    bsm = take_span(c, "small", 0, 288, 1)[0]
    centsq = small_c(c, 0, 64)
    k.op("s", lambda eng: eng.activation(out=centsq, in_=centT, func=AF.Square),
         reads=(wbs[0],), writes=(bsm,))
    emits = []
    for kt in range(DKT):
        o = pbank(c, 0)[0:1, 0:8]
        lh = onecol
        rh = centsq[:, kt * 8:(kt + 1) * 8]
        emits.append(lambda eng, o=o, lh=lh, rh=rh, s=(kt == 0), p=(kt == DKT - 1):
                     eng.matmul(out=o, lhsT=lh, rhs=rh, start=s, stop=p))
    mm_acc(c, [c.bps[0]], emits, [bsm, c.bconst])
    ccrow = c.rowsr[0:1, 2048:2056]
    k.op("v", lambda eng: eng.tensor_copy(out=ccrow, in_=pbank(c, 0)[0:1, 0:8]),
         reads=(c.bps[0],), writes=(brx,))
    k.dma("y", c.hrow[8:9, 0:8], ccrow, reads=(brx,), writes=(c.bhrow,))
    cc_bc = small_c(c, 64, 8)
    k.dma("y", cc_bc, c.hrow[8:9, 0:8].partition_broadcast(128).squeeze(1),
          reads=(c.bhrow,), writes=(bsm,))
    cxrow = c.rowsr[0:8, 2048:4096]
    for ci in range(4):
        emits = []
        for kt in range(DKT):
            o = pbank(c, ci)[0:8, :]
            lh = centT[:, kt * 8:(kt + 1) * 8]
            rh = xbft(c, kt)[:, ci * 512:(ci + 1) * 512]
            emits.append(lambda eng, o=o, lh=lh, rh=rh, s=(kt == 0),
                         p=(kt == DKT - 1): eng.matmul(out=o, lhsT=lh, rhs=rh,
                                                       start=s, stop=p))
        mm_acc(c, [c.bps[ci]], emits, [wbs[0]] + list(c.bxf))
        k.op("s", lambda eng, ci=ci: eng.copy(
            out=cxrow[:, ci * 512:(ci + 1) * 512], in_=pbank(c, ci)[0:8, :]),
            reads=(c.bps[ci],), writes=(brx,))
    cxT = small_c(c, 72, 8)
    x2c = small_c(c, 80, 1)
    m1c = small_c(c, 81, 1)
    m2c = small_c(c, 82, 1)
    msk = small_c(c, 83, 8)
    ew = small_c(c, 91, 8)
    dmc = small_c(c, 99, 1)
    rden = small_c(c, 100, 1)
    weT = small_c(c, 101, 8)
    bg2 = take_span(c, "small", 288, 436, 1)[0]
    for it in range(16):
        pp = pbank(c, 4)[:, 0:8]
        k.op("t", lambda eng, it=it, pp=pp: eng.transpose(
            out=pp, in_=cxrow[:, it * 128:(it + 1) * 128], identity=c.ident_f[0:8, 0:8]),
            reads=(brx, c.bconst), writes=(c.bps[4],))
        k.op("v", lambda eng, pp=pp: eng.tensor_copy(out=cxT, in_=pp),
             reads=(c.bps[4],), writes=(bg2,))
        px = pbank(c, 5)[:, 0:1]
        k.op("t", lambda eng, it=it, px=px: eng.transpose(
            out=px, in_=x2row[:, it * 128:(it + 1) * 128], identity=c.ident_f[0:1, 0:1]),
            reads=(brx, c.bconst), writes=(c.bps[5],))
        k.op("v", lambda eng, px=px: eng.tensor_copy(out=x2c, in_=px),
             reads=(c.bps[5],), writes=(bg2,))
        k.op("v", lambda eng: eng.tensor_scalar(
            out=cxT, in0=cxT, scalar1=-2.0, scalar2=x2c, op0=aop.mult, op1=aop.add),
            reads=(bg2,), writes=(bg2,))
        k.op("v", lambda eng: eng.tensor_tensor(out=cxT, in0=cxT, in1=cc_bc, op=aop.add),
             reads=(bg2, bsm), writes=(bg2,))
        k.op("v", lambda eng: eng.tensor_scalar_max(out=cxT, in0=cxT, scalar1=1e-12),
             reads=(bg2,), writes=(bg2,))
        k.op("s", lambda eng: eng.activation(out=cxT, in_=cxT, func=AF.Ln),
             reads=(bg2,), writes=(bg2,))
        k.op("s", lambda eng: eng.activation(out=cxT, in_=cxT, func=AF.Exp, scale=0.5),
             reads=(bg2,), writes=(bg2,))
        k.op("v", lambda eng: eng.tensor_reduce(out=m1c, in_=cxT, axis=AX.X, op=aop.max),
             reads=(bg2,), writes=(bg2,))
        k.op("v", lambda eng: eng.tensor_scalar(
            out=msk, in0=cxT, scalar1=m1c, scalar2=-1e30, op0=aop.is_equal, op1=aop.mult),
            reads=(bg2,), writes=(bg2,))
        k.op("v", lambda eng: eng.tensor_tensor(out=msk, in0=cxT, in1=msk, op=aop.add),
             reads=(bg2,), writes=(bg2,))
        k.op("v", lambda eng: eng.tensor_reduce(out=m2c, in_=msk, axis=AX.X, op=aop.max),
             reads=(bg2,), writes=(bg2,))
        k.op("v", lambda eng: eng.tensor_scalar(
            out=ew, in0=cxT, scalar1=m1c, scalar2=1.0, op0=aop.subtract, op1=aop.mult),
            reads=(bg2,), writes=(bg2,))
        k.op("s", lambda eng: eng.activation(out=ew, in_=ew, func=AF.Exp),
             reads=(bg2,), writes=(bg2,))
        k.op("v", lambda eng: eng.tensor_tensor(out=dmc, in0=m2c, in1=m1c, op=aop.subtract),
             reads=(bg2,), writes=(bg2,))
        k.op("s", lambda eng: eng.activation(out=dmc, in_=dmc, func=AF.Exp),
             reads=(bg2,), writes=(bg2,))
        k.op("v", lambda eng: eng.tensor_scalar_add(out=dmc, in0=dmc, scalar1=1.0),
             reads=(bg2,), writes=(bg2,))
        k.op("v", lambda eng: eng.reciprocal(out=rden, in_=dmc),
             reads=(bg2,), writes=(bg2,))
        k.op("v", lambda eng: eng.tensor_scalar(
            out=weT, in0=cxT, scalar1=m2c, scalar2=1.0, op0=aop.is_ge, op1=aop.mult),
            reads=(bg2,), writes=(bg2,))
        k.op("v", lambda eng: eng.tensor_tensor(out=weT, in0=weT, in1=ew, op=aop.mult),
             reads=(bg2,), writes=(bg2,))
        k.op("v", lambda eng: eng.tensor_scalar_mul(out=weT, in0=weT, scalar1=rden),
             reads=(bg2,), writes=(bg2,))
        ppb = pbank(c, 6)[0:8, 0:128]
        k.op("t", lambda eng, ppb=ppb: eng.transpose(
            out=ppb, in_=weT, identity=c.ident_f),
            reads=(bg2, c.bconst), writes=(c.bps[6],))
        k.op("v", lambda eng, it=it, ppb=ppb: eng.tensor_copy(
            out=cxrow[:, it * 128:(it + 1) * 128], in_=ppb),
            reads=(c.bps[6],), writes=(brx,))
    k.dma("y", c.hrow[0:8, :], cxrow, reads=(brx,), writes=(c.bhrow,))
    webc = c.rowsr[:, 0:2048]
    bwe = take_span(c, "rows", 0, 8192, 1)[0]
    sgap = c.reg(OFF_ROWS + 8192, 2048, F32)
    tuap = c.reg(OFF_ROWS + 10240, 2048, F32)
    bsg = take_span(c, "rows", 8192, 12288, 1)[0]
    for e in range(8):
        k.dma("y", webc, c.hrow[e:e + 1, :].partition_broadcast(128).squeeze(1),
              reads=(c.bhrow,), writes=(bwe,))
        bbe = take_span(c, "small", 512, 720, 1)[0]
        k.dma("y", small_c(c, 128, 22), c.hbm["l2f_bg"][e], writes=(bbe,))
        k.dma("y", small_c(c, 150, 22), c.hbm["l2f_b1"][e], reads=(bbe,), writes=(bbe,))
        k.dma("y", small_c(c, 172, 8), c.hbm["l2f_b2"][e], reads=(bbe,), writes=(bbe,))
        wgr = c.hbm["l2f_wg"][e]
        w1r = c.hbm["l2f_w1"][e]
        w2r = c.hbm["l2f_w2"][e]
        hap = [c.big_b[:, i * T:(i + 1) * T] for i in range(11)]
        for hh in range(2):
            hb = take_span(c, "big", 0, 45056, 11)
            wgu = take_span(c, "wb", 0, 8192, 4)
            gsl = [c.wbr[:, 0:1024], c.wbr[:, 1024:2048]]
            usl = [c.wbr[:, 2048:3072], c.wbr[:, 3072:4096]]
            for it in range(11):
                itg = hh * 11 + it
                sl = it % 2
                k.dma("p", gsl[sl], wgr[itg], writes=(wgu[sl],))
                k.dma("p", usl[sl], w1r[itg], writes=(wgu[2 + sl],))
                for ci in range(4):
                    bG, bU = sl * 2, sl * 2 + 1
                    for (bk, wsl_, wbuf) in ((bG, gsl[sl], wgu[sl]), (bU, usl[sl], wgu[2 + sl])):
                        emits = []
                        for kt in range(DKT):
                            o = pbank(c, bk)
                            lh = wsl_[:, kt * 128:(kt + 1) * 128]
                            rh = xbft(c, kt)[:, ci * 512:(ci + 1) * 512]
                            emits.append(lambda eng, o=o, lh=lh, rh=rh, s=(kt == 0),
                                         p=(kt == DKT - 1): eng.matmul(
                                out=o, lhsT=lh, rhs=rh, start=s, stop=p))
                        mm_acc(c, [c.bps[bk]], emits, [wbuf] + list(c.bxf))
                    k.op("s", lambda eng, bG=bG, itg=itg: eng.activation(
                        out=sgap, in_=pbank(c, bG), func=AF.Silu,
                        bias=small_c(c, 128 + itg, 1)),
                        reads=(c.bps[bG], bbe), writes=(bsg,))
                    k.op("v", lambda eng, bU=bU, itg=itg: eng.tensor_scalar_add(
                        out=tuap, in0=pbank(c, bU), scalar1=small_c(c, 150 + itg, 1)),
                        reads=(c.bps[bU], bbe), writes=(bsg,))
                    cwc = webc[:, ci * 512:(ci + 1) * 512]
                    k.op("v", lambda eng, cwc=cwc: eng.tensor_tensor(
                        out=tuap, in0=tuap, in1=cwc, op=aop.mult),
                        reads=(bsg, bwe), writes=(bsg,))
                    hout = hap[it][:, ci * 512:(ci + 1) * 512]
                    k.op("v", lambda eng, hout=hout: eng.tensor_tensor(
                        out=hout, in0=sgap, in1=tuap, op=aop.mult),
                        reads=(bsg,), writes=(hb[it],))
            wdd = take_span(c, "wb", 8192, 13824, 2)
            dsl = [c.wbr[:, 4096:5504], c.wbr[:, 5504:6912]]
            for ot in range(8):
                sl = ot % 2
                k.dma("p", dsl[sl], w2r[ot][:, hh * 1408:(hh + 1) * 1408],
                      writes=(wdd[sl],))
                for ci in range(4):
                    bk = 4 + ci
                    emits = []
                    for kt in range(11):
                        o = pbank(c, bk)
                        lh = dsl[sl][:, kt * 128:(kt + 1) * 128]
                        rh = hap[kt][:, ci * 512:(ci + 1) * 512]
                        emits.append(lambda eng, o=o, lh=lh, rh=rh, s=(kt == 0),
                                     p=(kt == 10): eng.matmul(out=o, lhsT=lh, rhs=rh,
                                                              start=s, stop=p))
                    mm_acc(c, [c.bps[bk]], emits, [wdd[sl]] + hb)
                if hh == 0:
                    k.op("v", lambda eng, ot=ot: eng.scalar_tensor_tensor(
                        out=xt(c, ot), in0=webc, scalar=small_c(c, 172 + ot, 1),
                        in1=xt(c, ot), op0=aop.mult, op1=aop.add),
                        reads=(bwe, bbe, c.bx[ot]), writes=(c.bx[ot],))
                k.op("v", lambda eng, ot=ot: eng.tensor_tensor(
                    out=xt(c, ot), in0=xt(c, ot), in1=c.psum[:, 2048:4096], op=aop.add),
                    reads=(c.bps[4], c.bps[5], c.bps[6], c.bps[7], c.bx[ot]),
                    writes=(c.bx[ot],))
    post_ln(c)


EMIT["l2f"] = emit_l2f

def emit_l2a(c):
    k = c.k
    bsm = take_span(c, "small", 0, 104, 1)[0]
    k.dma("y", small_c(c, 0, 8), c.hbm["l2a_tsg"], writes=(bsm,))
    k.dma("y", small_c(c, 8, 8), c.hbm["l2a_lnw"], reads=(bsm,), writes=(bsm,))
    k.dma("y", small_c(c, 16, 8), c.hbm["l2a_lnb"], reads=(bsm,), writes=(bsm,))
    k.dma("y", small_c(c, 24, 1)[0:64, :], c.hbm["l2a_td"], reads=(bsm,), writes=(bsm,))
    k.dma("y", small_c(c, 25, 1)[0:64, :], c.hbm["l2a_tf"], reads=(bsm,), writes=(bsm,))
    g_col = small_c(c, 32, 8)
    omg = small_c(c, 40, 8)
    netf = small_c(c, 48, 1)[0:64, :]
    bsc = take_span(c, "small", 128, 196, 1)[0]
    k.op("s", lambda eng: eng.activation(out=g_col, in_=small_c(c, 0, 8), func=AF.Sigmoid),
         reads=(bsm,), writes=(bsc,))
    k.op("v", lambda eng: eng.tensor_scalar(
        out=omg, in0=g_col, scalar1=-1.0, scalar2=1.0, op0=aop.mult, op1=aop.add),
        reads=(bsc,), writes=(bsc,))
    k.op("s", lambda eng: eng.activation(out=netf, in_=small_c(c, 25, 1)[0:64, :], func=AF.Exp),
         reads=(bsm,), writes=(bsc,))
    k.op("v", lambda eng: eng.tensor_scalar_mul(out=netf, in0=netf, scalar1=-1.0),
         reads=(bsc,), writes=(bsc,))
    wbs = take_span(c, "wb", 0, 4096, 3)
    kw_s = c.wbr[:, 0:512]
    vw_s = c.wbr[:, 512:1024]
    ow_s = c.wbr[:, 1024:2048]
    k.dma("p", kw_s, c.hbm["l2a_kwT"], writes=(wbs[0],))
    k.dma("p", vw_s, c.hbm["l2a_vwT"], writes=(wbs[1],))
    k.dma("p", ow_s[0:64, :], c.hbm["l2a_owT"], writes=(wbs[2],))
    xsb = [c.big_b[:, i * T:(i + 1) * T] for i in range(DKT)]
    bxs = take_span(c, "big", 0, 32768, DKT)
    S1 = c.reg(OFF_BIG + 32768, 8192, F32)
    S2 = c.reg(OFF_BIG + 40960, 8192, F32)
    S3 = c.reg(OFF_BIG + 49152, 8192, F32)
    S4 = c.reg(OFF_BIG + 57344, 8192, F32)
    bS1, bS2, bS3, bS4 = take_span(c, "big", 32768, 65536, 4)
    sqb = take_span(c, "wb", 4096, 6144, 1)[0]
    sq = c.reg(OFF_WB + 4096, 2048, F32)
    onecol = c.ones_f[:, 0:1]
    for kt in range(DKT):
        for half in range(2):
            t0, to = half * 1024, (1 - half) * 1024
            k.op("v", lambda eng, kt=kt, t0=t0, to=to: eng.tensor_scalar_mul(
                out=S1[:, t0:t0 + 1024], in0=xt(c, kt)[:, to:to + 1024],
                scalar1=g_col[:, kt:kt + 1]),
                reads=(c.bx[kt], bsc), writes=(bS1,))
            k.op("v", lambda eng, kt=kt, t0=t0: eng.scalar_tensor_tensor(
                out=S1[:, t0:t0 + 1024], in0=xt(c, kt)[:, t0:t0 + 1024],
                scalar=omg[:, kt:kt + 1], in1=S1[:, t0:t0 + 1024],
                op0=aop.mult, op1=aop.add),
                reads=(c.bx[kt], bsc, bS1), writes=(bS1,))
        for ci in range(4):
            k.op("t", lambda eng, kt=kt, ci=ci: eng.matmul(
                out=pbank(c, 4 + ci)[0:1, :], lhsT=onecol,
                rhs=S1[:, ci * 512:(ci + 1) * 512],
                start=(kt == 0), stop=(kt == DKT - 1)),
                reads=(bS1, c.bconst), writes=(c.bps[4 + ci],))
            k.op("s", lambda eng, ci=ci: eng.activation(
                out=sq, in_=S1[:, ci * 512:(ci + 1) * 512], func=AF.Square),
                reads=(bS1,), writes=(sqb,))
            k.op("t", lambda eng, kt=kt, ci=ci: eng.matmul(
                out=pbank(c, ci)[0:1, :], lhsT=onecol, rhs=sq,
                start=(kt == 0), stop=(kt == DKT - 1)),
                reads=(sqb, c.bconst), writes=(c.bps[ci],))
        k.op("s", lambda eng, kt=kt: eng.copy(out=xsb[kt], in_=S1),
             reads=(bS1,), writes=(bxs[kt],))
    r_mu, r_aux = row_slot(c, 0), row_slot(c, 1)
    brmu = take_span(c, "rows", 0, 16384, 2)
    for ci in range(4):
        k.op("s", lambda eng, ci=ci: eng.mul(
            out=r_mu[:, ci * 512:(ci + 1) * 512], in_=pbank(c, 4 + ci)[0:1, :], mul=1.0 / D),
            reads=(c.bps[4 + ci],), writes=(brmu[0],))
        k.op("s", lambda eng, ci=ci: eng.mul(
            out=r_aux[:, ci * 512:(ci + 1) * 512], in_=pbank(c, ci)[0:1, :], mul=1.0 / D),
            reads=(c.bps[ci],), writes=(brmu[1],))
    msq = c.psum[0:1, 0:2048]
    k.op("v", lambda eng: eng.tensor_tensor(out=msq, in0=r_mu, in1=r_mu, op=aop.mult),
         reads=(brmu[0],), writes=(c.bps[0], c.bps[1], c.bps[2], c.bps[3]))
    k.op("v", lambda eng: eng.tensor_tensor(out=r_aux, in0=r_aux, in1=msq, op=aop.subtract),
         reads=(brmu[1], c.bps[0], c.bps[1], c.bps[2], c.bps[3]), writes=(brmu[1],))
    k.op("s", lambda eng: eng.activation(out=r_aux, in_=r_aux, func=AF.Ln,
                                         bias=c.eps_col[0:1, :]),
         reads=(brmu[1], c.beps), writes=(brmu[1],))
    k.op("s", lambda eng: eng.activation(out=r_aux, in_=r_aux, func=AF.Exp, scale=-0.5),
         reads=(brmu[1],), writes=(brmu[1],))
    k.op("v", lambda eng: eng.tensor_tensor(out=r_mu, in0=r_mu, in1=r_aux, op=aop.mult),
         reads=(brmu[0], brmu[1]), writes=(brmu[0],))
    for ci in range(4):
        k.op("t", lambda eng, ci=ci: eng.matmul(
            out=pbank(c, ci), lhsT=c.ones_f[0:1, 0:128],
            rhs=r_aux[:, ci * 512:(ci + 1) * 512], start=True, stop=True),
            reads=(brmu[1], c.bconst), writes=(c.bps[ci],))
        k.op("t", lambda eng, ci=ci: eng.matmul(
            out=pbank(c, 4 + ci), lhsT=c.ones_f[0:1, 0:128],
            rhs=r_mu[:, ci * 512:(ci + 1) * 512], start=True, stop=True),
            reads=(brmu[0], c.bconst), writes=(c.bps[4 + ci],))
    rstd_bc = c.psum[:, 0:2048]
    mur_bc = c.psum[:, 2048:4096]
    for kt in range(DKT):
        k.op("v", lambda eng, kt=kt: eng.tensor_tensor(
            out=xsb[kt], in0=xsb[kt], in1=mur_bc, op=aop.subtract),
            reads=(bxs[kt], c.bps[4], c.bps[5], c.bps[6], c.bps[7]),
            writes=(bxs[kt],))
        k.op("v", lambda eng, kt=kt: eng.tensor_tensor(
            out=xsb[kt], in0=xsb[kt], in1=rstd_bc, op=aop.mult),
            reads=(bxs[kt], c.bps[0], c.bps[1], c.bps[2], c.bps[3]),
            writes=(bxs[kt],))
        k.op("v", lambda eng, kt=kt: eng.tensor_scalar(
            out=xsb[kt], in0=xsb[kt], scalar1=small_c(c, 8 + kt, 1),
            scalar2=small_c(c, 16 + kt, 1), op0=aop.mult, op1=aop.add),
            reads=(bxs[kt], bsm), writes=(bxs[kt],))
    for which, wsl, wbuf, banks in (("k", kw_s, wbs[0], (0, 1, 2, 3)),
                                    ("v", vw_s, wbs[1], (4, 5, 6, 7))):
        for ci in range(4):
            emits = []
            for kt in range(DKT):
                o = pbank(c, banks[ci])[0:64, :]
                lh = wsl[:, kt * 64:(kt + 1) * 64]
                rh = xsb[kt][:, ci * 512:(ci + 1) * 512]
                emits.append(lambda eng, o=o, lh=lh, rh=rh, s=(kt == 0),
                             p=(kt == DKT - 1): eng.matmul(out=o, lhsT=lh, rhs=rh,
                                                           start=s, stop=p))
            mm_acc(c, [c.bps[banks[ci]]], emits, [wbuf] + bxs)
    k.op("s", lambda eng: eng.activation(
        out=S2[0:64, :], in_=c.psum[0:64, 0:2048], func=AF.Exp, scale=netf),
        reads=(c.bps[0], c.bps[1], c.bps[2], c.bps[3], bsc), writes=(bS2,))
    k.op("v", lambda eng: eng.tensor_tensor(
        out=S2[0:64, :], in0=S2[0:64, :], in1=c.psum[0:64, 2048:4096], op=aop.mult),
        reads=(bS2, c.bps[4], c.bps[5], c.bps[6], c.bps[7]), writes=(bS2,))
    k.op("s", lambda eng: eng.activation(
        out=S3[0:64, :], in_=S2[0:64, :], func=AF.Exp, scale=0.0,
        bias=small_c(c, 24, 1)[0:64, :]),
        reads=(bS2, bsm), writes=(bS3,))
    k.op("v", lambda eng: eng.tensor_tensor_scan(
        out=S4[0:64, :], data0=S3[0:64, :], data1=S2[0:64, :], initial=0.0,
        op0=aop.mult, op1=aop.add),
        reads=(bS3, bS2), writes=(bS4,))
    stb = c.big_b[:, 0:2048]
    k.op("s", lambda eng: eng.copy(out=stb[0:64, :], in_=S4[0:64, :]),
         reads=(bS4,), writes=(bxs[0],))
    for ot in range(DKT):
        bk = [(ot % 2) * 4, (ot % 2) * 4 + 1, (ot % 2) * 4 + 2, (ot % 2) * 4 + 3]
        emits = []
        for ci in range(4):
            o = pbank(c, bk[ci])
            lh = ow_s[0:64, ot * 128:(ot + 1) * 128]
            rh = stb[0:64, ci * 512:(ci + 1) * 512]
            emits.append(lambda eng, o=o, lh=lh, rh=rh: eng.matmul(
                out=o, lhsT=lh, rhs=rh, start=True, stop=True))
        mm_acc(c, [c.bps[b] for b in bk], emits, [wbs[2], bxs[0]])
        k.op("v", lambda eng, ot=ot, b0=bk[0]: eng.tensor_tensor(
            out=xt(c, ot), in0=xt(c, ot), in1=c.psum[:, b0 * 512:b0 * 512 + 2048],
            op=aop.add),
            reads=tuple(c.bps[b] for b in bk) + (c.bx[ot],), writes=(c.bx[ot],))
    post_ln(c)


EMIT["l2a"] = emit_l2a

def emit_l0a(c):
    k = c.k
    bsm = take_span(c, "small", 0, 4096, 1)[0]
    k.dma("y", small_c(c, 0, 16), c.hbm["l0a_bg"], writes=(bsm,))
    k.dma("y", small_c(c, 16, 32), c.hbm["l0a_bin"], reads=(bsm,), writes=(bsm,))
    k.dma("y", small_c(c, 48, 8), c.hbm["l0a_bo"], reads=(bsm,), writes=(bsm,))
    k.dma("y", small_c(c, 56, 16), c.hbm["l0a_dv"], reads=(bsm,), writes=(bsm,))
    k.dma("y", small_c(c, 100, 256), c.hbm["l0a_alog"], reads=(bsm,), writes=(bsm,))
    k.dma("y", small_c(c, 612, 256), c.hbm["l0a_bmat"], reads=(bsm,), writes=(bsm,))
    c0bc = small_c(c, 868, 16)
    k.dma("y", c0bc, c.hbm["l0a_c0"].partition_broadcast(128).squeeze(1),
          reads=(bsm,), writes=(bsm,))
    acol = small_c(c, 100, 256)
    cbcol = small_c(c, 612, 256)
    carry = small_c(c, 356, 256)
    bcar = take_span(c, "small", 1424, 2448, 1)[0]
    k.op("s", lambda eng: eng.activation(out=acol, in_=acol, func=AF.Exp),
         reads=(bsm,), writes=(bsm,))
    k.op("v", lambda eng: eng.tensor_scalar_mul(out=acol, in0=acol, scalar1=-1.0),
         reads=(bsm,), writes=(bsm,))
    for s in range(16):
        k.op("v", lambda eng, s=s: eng.tensor_tensor(
            out=cbcol[:, s * 16:(s + 1) * 16], in0=cbcol[:, s * 16:(s + 1) * 16],
            in1=c0bc, op=aop.mult),
            reads=(bsm,), writes=(bsm,))
    XPS = c.reg(OFF_BIG + 32768, 4096, F32)
    DLS = c.reg(OFF_BIG + 36864, 4096, F32)
    YS = c.reg(OFF_BIG + 40960, 2048, BF16)
    US = c.reg(OFF_BIG + 45056, 2048, BF16)
    HSs = [c.reg(OFF_BIG + 47104, 2048, BF16), c.reg(OFF_BIG + 51200, 2048, BF16)]
    GTS = c.reg(OFF_BIG + 49152, 2048, BF16)
    bXP, bDL, bY, bU, bGT = take_span(c, "big", 32768, 51200, 5)
    bHS = [take_span(c, "big", 47104, 49152, 1)[0],
           take_span(c, "big", 51200, 53248, 1)[0]]
    gzb = take_span(c, "big", 0, 32768, 16)
    gz = [c.big_b[:, i * 1024:(i + 1) * 1024] for i in range(16)]
    wgTr = c.hbm["l0a_wgT"]
    winTr = c.hbm["l0a_winT"]
    woTr = c.hbm["l0a_woT"]
    for tb in range(2):
        t0 = tb * 1024
        wbs = take_span(c, "wb", 0, 12288, 6)
        sl_xp = [c.wbr[:, 0:1024], c.wbr[:, 1024:2048]]
        sl_dl = [c.wbr[:, 2048:3072], c.wbr[:, 3072:4096]]
        sl_gt = [c.wbr[:, 4096:5120], c.wbr[:, 5120:6144]]
        for s in range(16):
            sl = s % 2
            k.dma("p", sl_xp[sl], winTr[s], writes=(wbs[sl],))
            k.dma("p", sl_dl[sl], winTr[16 + s], writes=(wbs[2 + sl],))
            k.dma("p", sl_gt[sl], wgTr[s], writes=(wbs[4 + sl],))
            for (wsl_, wbuf, b0) in ((sl_xp[sl], wbs[sl], 0),
                                     (sl_dl[sl], wbs[2 + sl], 2),
                                     (sl_gt[sl], wbs[4 + sl], 4)):
                for ci in range(2):
                    emits = []
                    for kt in range(DKT):
                        o = pbank(c, b0 + ci)
                        lh = wsl_[:, kt * 128:(kt + 1) * 128]
                        rh = xbft(c, kt)[:, t0 + ci * 512:t0 + (ci + 1) * 512]
                        emits.append(lambda eng, o=o, lh=lh, rh=rh, s_=(kt == 0),
                                     p=(kt == DKT - 1): eng.matmul(
                            out=o, lhsT=lh, rhs=rh, start=s_, stop=p))
                    mm_acc(c, [c.bps[b0 + ci]], emits, [wbuf] + list(c.bxf))
            k.op("v", lambda eng, s=s: eng.tensor_scalar_add(
                out=XPS, in0=c.psum[:, 0:1024], scalar1=small_c(c, 16 + s, 1)),
                reads=(c.bps[0], c.bps[1], bsm), writes=(bXP,))
            k.op("s", lambda eng, s=s: eng.activation(
                out=DLS, in_=c.psum[:, 1024:2048], func=AF.Relu,
                bias=small_c(c, 32 + s, 1)),
                reads=(c.bps[2], c.bps[3], bsm), writes=(bDL,))
            k.op("s", lambda eng, s=s: eng.activation(
                out=GTS, in_=c.psum[:, 2048:3072], func=AF.Sigmoid,
                bias=small_c(c, 0 + s, 1)),
                reads=(c.bps[4], c.bps[5], bsm), writes=(bGT,))
            k.op("v", lambda eng: eng.tensor_tensor(
                out=US, in0=XPS, in1=DLS, op=aop.mult),
                reads=(bXP, bDL), writes=(bU,))
            for n in range(16):
                sn = s * 16 + n
                pr = n % 2
                k.op("s", lambda eng, sn=sn, pr=pr: eng.activation(
                    out=HSs[pr], in_=DLS, func=AF.Exp, scale=acol[:, sn:sn + 1]),
                    reads=(bDL, bsm), writes=(bHS[pr],))
                if tb == 0:
                    k.op("v", lambda eng, pr=pr: eng.tensor_tensor_scan(
                        out=HSs[pr], data0=HSs[pr], data1=US, initial=0.0,
                        op0=aop.mult, op1=aop.add),
                        reads=(bU,), writes=(bHS[pr],))
                else:
                    k.op("v", lambda eng, sn=sn, pr=pr: eng.tensor_tensor_scan(
                        out=HSs[pr], data0=HSs[pr], data1=US, initial=carry[:, sn:sn + 1],
                        op0=aop.mult, op1=aop.add),
                        reads=(bU, bcar), writes=(bHS[pr],))
                if tb == 0:
                    k.op("v", lambda eng, sn=sn, pr=pr: eng.tensor_copy(
                        out=carry[:, sn:sn + 1], in_=HSs[pr][:, 1023:1024]),
                        reads=(bHS[pr],), writes=(bcar,))
                if n == 0:
                    k.op("v", lambda eng, sn=sn, pr=pr: eng.tensor_scalar_mul(
                        out=YS, in0=HSs[pr], scalar1=cbcol[:, sn:sn + 1]),
                        reads=(bHS[pr], bsm), writes=(bY,))
                else:
                    k.op("v", lambda eng, sn=sn, pr=pr: eng.scalar_tensor_tensor(
                        out=YS, in0=HSs[pr], scalar=cbcol[:, sn:sn + 1], in1=YS,
                        op0=aop.mult, op1=aop.add),
                        reads=(bHS[pr], bsm, bY), writes=(bY,))
            k.op("v", lambda eng, s=s: eng.scalar_tensor_tensor(
                out=YS, in0=XPS, scalar=small_c(c, 56 + s, 1), in1=YS,
                op0=aop.mult, op1=aop.add),
                reads=(bXP, bsm, bY), writes=(bY,))
            k.op("v", lambda eng, s=s: eng.tensor_tensor(
                out=gz[s], in0=GTS, in1=YS, op=aop.mult),
                reads=(bGT, bY), writes=(gzb[s],))
        wbo = take_span(c, "wb", 8192, 16384, 2)
        sl_wo = [c.wbr[:, 4096:6144], c.wbr[:, 6144:8192]]
        for ot in range(DKT):
            sl = ot % 2
            k.dma("p", sl_wo[sl], woTr[ot], writes=(wbo[sl],))
            b0 = 6 if ot % 2 == 0 else 6
            bk = [6, 7]
            emits = []
            for ci in range(2):
                for kt in range(16):
                    o = pbank(c, bk[ci])
                    lh = sl_wo[sl][:, kt * 128:(kt + 1) * 128]
                    rh = gz[kt][:, ci * 512:(ci + 1) * 512]
                    emits.append(lambda eng, o=o, lh=lh, rh=rh, s_=(kt == 0),
                                 p=(kt == 15): eng.matmul(out=o, lhsT=lh, rhs=rh,
                                                          start=s_, stop=p))
            mm_acc(c, [c.bps[6], c.bps[7]], emits, [wbo[sl]] + gzb)
            k.op("v", lambda eng, ot=ot, t0=t0: eng.scalar_tensor_tensor(
                out=xt(c, ot)[:, t0:t0 + 1024], in0=c.psum[:, 6 * 512:6 * 512 + 1024],
                scalar=small_c(c, 48 + ot, 1), in1=xt(c, ot)[:, t0:t0 + 1024],
                op0=aop.add, op1=aop.add),
                reads=(c.bps[6], c.bps[7], bsm, c.bx[ot]), writes=(c.bx[ot],))
    post_ln(c)


EMIT["l0a"] = emit_l0a

def emit_l3a(c):
    k = c.k
    bsm = take_span(c, "small", 0, 1504, 1)[0]
    k.dma("y", small_c(c, 0, 8), c.hbm["l3a_normw"], writes=(bsm,))
    k.dma("y", small_c(c, 8, 16), c.hbm["l3a_cb"], reads=(bsm,), writes=(bsm,))
    k.dma("y", small_c(c, 24, 64), c.hbm["l3a_cw"], reads=(bsm,), writes=(bsm,))
    k.dma("y", small_c(c, 88, 16), c.hbm["l3a_dtb"], reads=(bsm,), writes=(bsm,))
    k.dma("y", small_c(c, 104, 256), c.hbm["l3a_alog"], reads=(bsm,), writes=(bsm,))
    k.dma("y", small_c(c, 360, 16), c.hbm["l3a_dv"], reads=(bsm,), writes=(bsm,))
    acol = small_c(c, 104, 256)
    k.op("s", lambda eng: eng.activation(out=acol, in_=acol, func=AF.Exp),
         reads=(bsm,), writes=(bsm,))
    k.op("v", lambda eng: eng.tensor_scalar_mul(out=acol, in0=acol, scalar1=-1.0),
         reads=(bsm,), writes=(bsm,))
    sqb = take_span(c, "wb", 0, 2048, 1)[0]
    sq = c.reg(OFF_WB, 2048, F32)
    onecol = c.ones_f[:, 0:1]
    for ci in range(4):
        for kt in range(DKT):
            k.op("s", lambda eng, ci=ci, kt=kt: eng.activation(
                out=sq, in_=xt(c, kt)[:, ci * 512:(ci + 1) * 512], func=AF.Square),
                reads=(c.bx[kt],), writes=(sqb,))
            k.op("t", lambda eng, ci=ci, kt=kt: eng.matmul(
                out=pbank(c, 4 + ci)[0:1, :], lhsT=onecol, rhs=sq,
                start=(kt == 0), stop=(kt == DKT - 1)),
                reads=(sqb, c.bconst), writes=(c.bps[4 + ci],))
    r_aux = row_slot(c, 1)
    brr = take_span(c, "rows", 8192, 16384, 1)[0]
    for ci in range(4):
        k.op("s", lambda eng, ci=ci: eng.copy(
            out=r_aux[:, ci * 512:(ci + 1) * 512], in_=pbank(c, 4 + ci)[0:1, :]),
            reads=(c.bps[4 + ci],), writes=(brr,))
    k.op("s", lambda eng: eng.activation(out=r_aux, in_=r_aux, func=AF.Ln),
         reads=(brr,), writes=(brr,))
    k.op("s", lambda eng: eng.activation(out=r_aux, in_=r_aux, func=AF.Exp, scale=0.5),
         reads=(brr,), writes=(brr,))
    k.op("s", lambda eng: eng.mul(out=r_aux, in_=r_aux, mul=1.0 / 32.0),
         reads=(brr,), writes=(brr,))
    k.op("v", lambda eng: eng.tensor_scalar_add(out=r_aux, in0=r_aux, scalar1=1e-5),
         reads=(brr,), writes=(brr,))
    k.op("v", lambda eng: eng.reciprocal(out=r_aux, in_=r_aux),
         reads=(brr,), writes=(brr,))
    k.dma("p", c.hrow[33:34, 0:2048], r_aux, reads=(brr,), writes=(c.bhrow,))

    def bc_hn_and_apply(hn_tiles, hn_bufs):
        for ci in range(4):
            k.op("t", lambda eng, ci=ci: eng.matmul(
                out=pbank(c, ci), lhsT=c.ones_f[0:1, 0:128],
                rhs=r_aux[:, ci * 512:(ci + 1) * 512], start=True, stop=True),
                reads=(brr, c.bconst), writes=(c.bps[ci],))
        rbc = c.psum[:, 0:2048]
        for kt in range(DKT):
            k.op("v", lambda eng, kt=kt: eng.tensor_tensor(
                out=hn_tiles[kt], in0=xt(c, kt), in1=rbc, op=aop.mult),
                reads=(c.bx[kt], c.bps[0], c.bps[1], c.bps[2], c.bps[3]),
                writes=(hn_bufs[kt],))
            k.op("v", lambda eng, kt=kt: eng.tensor_scalar_mul(
                out=hn_tiles[kt], in0=hn_tiles[kt], scalar1=small_c(c, kt, 1)),
                reads=(hn_bufs[kt], bsm), writes=(hn_bufs[kt],))

    hnb = take_span(c, "xbf", 0, 32768, DKT)
    hnt = [c.xbf_borrow_b[:, i * T:(i + 1) * T] for i in range(DKT)]
    bc_hn_and_apply(hnt, hnb)
    xcb = take_span(c, "big", 0, 65536, 16)
    xc = [c.big_b[:, i * T:(i + 1) * T] for i in range(16)]
    winTr = c.hbm["l3a_winT"]
    wbx = take_span(c, "wb", 0, 4096, 2)
    sl_x = [c.reg(OFF_WB, 2048, BF16), c.reg(OFF_WB + 2048, 2048, BF16)]
    xpad = c.reg(OFF_WB + 4096, 8448, F32)
    acc = c.reg(OFF_WB + 12800, 2048, F32)
    bxp = take_span(c, "wb", 4096, 12544, 1)[0]
    bacc = take_span(c, "wb", 12800, 14848, 1)[0]
    k.op("v", lambda eng: eng.memset(xpad[:, 0:3], 0.0), writes=(bxp,))
    for s in range(16):
        sl = s % 2
        k.dma("p", sl_x[sl], winTr[s], writes=(wbx[sl],))
        for ci in range(4):
            emits = []
            for kt in range(DKT):
                o = pbank(c, ci)
                lh = sl_x[sl][:, kt * 128:(kt + 1) * 128]
                rh = hnt[kt][:, ci * 512:(ci + 1) * 512]
                emits.append(lambda eng, o=o, lh=lh, rh=rh, s_=(kt == 0),
                             p=(kt == DKT - 1): eng.matmul(out=o, lhsT=lh, rhs=rh,
                                                           start=s_, stop=p))
            mm_acc(c, [c.bps[ci]], emits, [wbx[sl]] + hnb)
        k.op("v", lambda eng: eng.tensor_copy(out=xpad[:, 3:2051], in_=c.psum[:, 0:2048]),
             reads=(c.bps[0], c.bps[1], c.bps[2], c.bps[3]), writes=(bxp,))
        for ci in range(4):
            k.op("v", lambda eng, s=s, ci=ci: eng.tensor_scalar_mul(
                out=acc, in0=xpad[:, ci * 512:ci * 512 + 512],
                scalar1=small_c(c, 24 + s * 4, 1)),
                reads=(bxp, bsm), writes=(bacc,))
            for kk in range(1, 4):
                k.op("v", lambda eng, s=s, ci=ci, kk=kk: eng.scalar_tensor_tensor(
                    out=acc, in0=xpad[:, ci * 512 + kk:ci * 512 + kk + 512],
                    scalar=small_c(c, 24 + s * 4 + kk, 1), in1=acc,
                    op0=aop.mult, op1=aop.add),
                    reads=(bxp, bsm, bacc), writes=(bacc,))
            k.op("s", lambda eng, s=s, ci=ci: eng.activation(
                out=xc[s][:, ci * 512:(ci + 1) * 512], in_=acc, func=AF.Silu,
                bias=small_c(c, 8 + s, 1)),
                reads=(bacc, bsm), writes=(xcb[s],))
    wbB = [take_span(c, "wb", 0, 3072, 1)[0], take_span(c, "wb", 3072, 7168, 1)[0],
           take_span(c, "wb", 7168, 11264, 1)[0]]
    xpT_s = c.reg(OFF_WB, 3072, BF16)
    k.dma("p", xpT_s, c.hbm["l3a_xpT"], writes=(wbB[0],))
    for ci in range(4):
        emits = []
        for kt in range(16):
            o = pbank(c, 4 + ci)[0:96, :]
            lh = xpT_s[:, kt * 96:(kt + 1) * 96]
            rh = xc[kt][:, ci * 512:(ci + 1) * 512] if False else None
            rh = xc[kt][:, ci * 512:(ci + 1) * 512]
            emits.append(lambda eng, o=o, lh=lh, rh=rh, s_=(kt == 0),
                         p=(kt == 15): eng.matmul(out=o, lhsT=lh, rhs=rh,
                                                  start=s_, stop=p))
        mm_acc(c, [c.bps[4 + ci]], emits, [wbB[0]] + xcb)
    bcst = c.reg(OFF_XBF, 8192, F32)
    bst = take_span(c, "xbf", 0, 8192, 1)[0]
    k.op("v", lambda eng: eng.tensor_copy(
        out=bcst[0:32, :], in_=c.psum[64:96, 2048:4096]),
        reads=(c.bps[4], c.bps[5], c.bps[6], c.bps[7]), writes=(bst,))
    k.dma("p", c.hrow.bitcast(BF16)[0:32, 0:2048], bcst[0:32, :],
          reads=(bst,), writes=(c.bhrow,))
    dtr_bf = c.reg(OFF_WB + 3072, 4096, BF16)
    k.op("s", lambda eng: eng.copy(out=dtr_bf[0:64, :], in_=c.psum[0:64, 2048:4096]),
         reads=(c.bps[4], c.bps[5], c.bps[6], c.bps[7]), writes=(wbB[1],))
    dtwT_s = c.reg(OFF_WB + 7168, 4096, BF16)
    k.dma("p", dtwT_s[0:64, :], c.hbm["l3a_dtwT"], writes=(wbB[2],))
    DTB = c.reg(OFF_XBF, 4096, BF16)
    US = c.reg(OFF_XBF + 4096, 4096, BF16)
    YS = c.reg(OFF_XBF + 8192, 4096, BF16)
    HS = c.reg(OFF_XBF + 12288, 4096, BF16)
    UBS = c.reg(OFF_XBF + 16384, 4096, BF16)
    SPA1 = c.reg(OFF_XBF + 12288, 8192, F32)
    SPA2 = c.reg(OFF_XBF + 20480, 8192, F32)
    BCs = [c.reg(OFF_XBF + 20480, 8192, BF16), c.reg(OFF_ROWS, 8192, BF16),
           c.reg(OFF_ROWS + 8192, 8192, BF16)]
    bDT, bU, bY = take_span(c, "xbf", 0, 12288, 3)
    bHS = take_span(c, "xbf", 12288, 16384, 1)[0]
    bUB = take_span(c, "xbf", 16384, 20480, 1)[0]
    bBC = [take_span(c, "xbf", 20480, 28672, 1)[0],
           take_span(c, "rows", 0, 8192, 1)[0],
           take_span(c, "rows", 8192, 16384, 1)[0]]
    hrow_bf = c.hrow.bitcast(BF16)
    for s in range(16):
        for ci in range(4):
            o = pbank(c, ci)
            lh = dtwT_s[0:64, s * 128:(s + 1) * 128]
            rh = dtr_bf[0:64, ci * 512:(ci + 1) * 512]
            k.op("t", lambda eng, o=o, lh=lh, rh=rh: eng.matmul(
                out=o, lhsT=lh, rhs=rh, start=True, stop=True),
                reads=(wbB[1], wbB[2]), writes=(c.bps[ci],))
        pall = c.psum[:, 0:2048]
        SPl = SPA1[:, 0:2048]
        SPr = SPA2[:, 0:2048]
        k.op("s", lambda eng, s=s: eng.activation(
            out=SPl, in_=pall, func=AF.Abs, bias=small_c(c, 88 + s, 1)),
            reads=(c.bps[0], c.bps[1], c.bps[2], c.bps[3], bsm),
            writes=(bHS, bUB))
        k.op("s", lambda eng: eng.activation(out=SPl, in_=SPl, func=AF.Exp, scale=-1.0),
             reads=(bHS,), writes=(bHS, bUB))
        k.op("s", lambda eng: eng.activation(out=SPl, in_=SPl, func=AF.Ln, bias=1.0),
             reads=(bHS,), writes=(bHS, bUB))
        k.op("s", lambda eng, s=s: eng.activation(
            out=SPr, in_=pall, func=AF.Relu, bias=small_c(c, 88 + s, 1)),
            reads=(c.bps[0], c.bps[1], c.bps[2], c.bps[3], bsm),
            writes=(bBC[0],))
        k.op("v", lambda eng: eng.tensor_tensor(out=DTB, in0=SPl, in1=SPr, op=aop.add),
             reads=(bHS, bBC[0]), writes=(bDT,))
        k.op("v", lambda eng, s=s: eng.tensor_tensor(
            out=US, in0=DTB, in1=xc[s], op=aop.mult),
            reads=(bDT, xcb[s]), writes=(bU,))
        for n in range(16):
            sn = s * 16 + n
            pr = n % 3
            Bb = BCs[pr][:, 0:2048]
            Cb = BCs[pr][:, 2048:4096]
            k.dma("p", BCs[pr].rearrange("p (two f) -> p two f", two=2),
                  hrow_bf[n:n + 17:16, 0:2048].partition_broadcast(128).squeeze(1)
                  if False else
                  hrow_bf[n:n + 17:16, 0:2048].partition_broadcast(128),
                  reads=(c.bhrow,), writes=(bBC[pr],))
            k.op("v", lambda eng, pr=pr, Bb=Bb: eng.tensor_tensor(
                out=UBS, in0=US, in1=Bb, op=aop.mult),
                reads=(bU, bBC[pr]), writes=(bUB,))
            k.op("s", lambda eng, sn=sn: eng.activation(
                out=HS, in_=DTB, func=AF.Exp, scale=acol[:, sn:sn + 1]),
                reads=(bDT, bsm), writes=(bHS,))
            k.op("v", lambda eng: eng.tensor_tensor_scan(
                out=HS, data0=HS, data1=UBS, initial=0.0,
                op0=aop.mult, op1=aop.add),
                reads=(bUB,), writes=(bHS,))
            k.op("v", lambda eng, pr=pr, Cb=Cb: eng.tensor_tensor(
                out=UBS, in0=HS, in1=Cb, op=aop.mult),
                reads=(bHS, bBC[pr]), writes=(bUB,))
            if n == 0:
                k.op("v", lambda eng: eng.tensor_copy(out=YS, in_=UBS),
                     reads=(bUB,), writes=(bY,))
            else:
                k.op("v", lambda eng: eng.tensor_tensor(
                    out=YS, in0=YS, in1=UBS, op=aop.add),
                    reads=(bY, bUB), writes=(bY,))
        k.op("v", lambda eng, s=s: eng.scalar_tensor_tensor(
            out=xc[s], in0=xc[s], scalar=small_c(c, 360 + s, 1), in1=YS,
            op0=aop.mult, op1=aop.add),
            reads=(xcb[s], bsm, bY), writes=(xcb[s],))
    hnb2 = take_span(c, "xbf", 0, 32768, DKT)
    brr = take_span(c, "rows", 8192, 16384, 1)[0]
    k.dma("p", r_aux, c.hrow[33:34, 0:2048], reads=(c.bhrow,), writes=(brr,))
    bc_hn_and_apply(hnt, hnb2)
    wbz = take_span(c, "wb", 0, 4096, 2)
    sl_z = [c.reg(OFF_WB, 2048, BF16), c.reg(OFF_WB + 2048, 2048, BF16)]
    zf = c.reg(OFF_WB + 4096, 8192, F32)
    bzf = take_span(c, "wb", 4096, 12288, 1)[0]
    for s in range(16):
        sl = s % 2
        k.dma("p", sl_z[sl], winTr[16 + s], writes=(wbz[sl],))
        for ci in range(4):
            emits = []
            for kt in range(DKT):
                o = pbank(c, 4 + ci)
                lh = sl_z[sl][:, kt * 128:(kt + 1) * 128]
                rh = hnt[kt][:, ci * 512:(ci + 1) * 512]
                emits.append(lambda eng, o=o, lh=lh, rh=rh, s_=(kt == 0),
                             p=(kt == DKT - 1): eng.matmul(out=o, lhsT=lh, rhs=rh,
                                                           start=s_, stop=p))
            mm_acc(c, [c.bps[4 + ci]], emits, [wbz[sl]] + hnb2)
        k.op("s", lambda eng: eng.activation(out=zf, in_=c.psum[:, 2048:4096], func=AF.Silu),
             reads=(c.bps[4], c.bps[5], c.bps[6], c.bps[7]), writes=(bzf,))
        k.op("v", lambda eng, s=s: eng.tensor_tensor(
            out=xc[s], in0=xc[s], in1=zf, op=aop.mult),
            reads=(xcb[s], bzf), writes=(xcb[s],))
    woTr = c.hbm["l3a_woT"]
    wbo = take_span(c, "wb", 0, 8192, 2)
    sl_o = [c.reg(OFF_WB, 4096, BF16), c.reg(OFF_WB + 4096, 4096, BF16)]
    for ot in range(DKT):
        sl = ot % 2
        k.dma("p", sl_o[sl], woTr[ot], writes=(wbo[sl],))
        for ci in range(4):
            bk = ci
            emits = []
            for kt in range(16):
                o = pbank(c, bk)
                lh = sl_o[sl][:, kt * 128:(kt + 1) * 128]
                rh = xc[kt][:, ci * 512:(ci + 1) * 512]
                emits.append(lambda eng, o=o, lh=lh, rh=rh, s_=(kt == 0),
                             p=(kt == 15): eng.matmul(out=o, lhsT=lh, rhs=rh,
                                                      start=s_, stop=p))
            mm_acc(c, [c.bps[bk]], emits, [wbo[sl]] + xcb)
        k.op("v", lambda eng, ot=ot: eng.tensor_tensor(
            out=xt(c, ot), in0=xt(c, ot), in1=c.psum[:, 0:2048], op=aop.add),
            reads=(c.bps[0], c.bps[1], c.bps[2], c.bps[3], c.bx[ot]),
            writes=(c.bx[ot],))
    post_ln(c)


EMIT["l3a"] = emit_l3a









def load_x_tmajor(c):
    k = c.k
    tscr = c.rowsr[:, 0:1024]
    c.brow[0] = take_span(c, "rows", 0, 4096, 1)[0]
    for tt in range(T // 128):
        dtok = k.dma("y", tscr, c.x_in[tt * 128:(tt + 1) * 128, :],
                     writes=(c.brow[0],))
        for kt in range(DKT):
            b = kt % 4
            k.op("t", lambda eng, kt=kt, b=b: eng.transpose(
                out=pbank(c, b)[:, 0:128], in_=tscr[:, kt * 128:(kt + 1) * 128],
                identity=c.ident_f[:, :]),
                reads=(c.brow[0], c.bconst), writes=(c.bps[b],))
            k.op("v", lambda eng, kt=kt, tt=tt, b=b: eng.tensor_copy(
                out=xt(c, kt)[:, tt * 128:tt * 128 + 128], in_=pbank(c, b)[:, 0:128]),
                reads=(c.bps[b],), writes=(c.bx[kt],))
    cast_x_bf(c)


def store_x_tmajor(c):
    k = c.k
    tscr = c.rowsr[:, 0:1024]
    c.brow[0] = take_span(c, "rows", 0, 4096, 1)[0]
    for tt in range(T // 128):
        for kt in range(DKT):
            b = 4 + kt % 4
            k.op("t", lambda eng, kt=kt, tt=tt, b=b: eng.transpose(
                out=pbank(c, b)[:, 0:128], in_=xt(c, kt)[:, tt * 128:tt * 128 + 128],
                identity=c.ident_f[:, :]),
                reads=(c.bx[kt], c.bconst), writes=(c.bps[b],))
            k.op("v", lambda eng, kt=kt, b=b: eng.tensor_copy(
                out=tscr[:, kt * 128:(kt + 1) * 128], in_=pbank(c, b)[:, 0:128]),
                reads=(c.bps[b],), writes=(c.brow[0],))
        k.dma("y", c.x_out[tt * 128:(tt + 1) * 128, :], tscr,
              reads=(c.brow[0],))


WNEEDS = {
    "l0a": ["l0a_wgT", "l0a_winT", "l0a_woT", "l0a_bg", "l0a_bin", "l0a_bo",
            "l0a_alog", "l0a_bmat", "l0a_c0", "l0a_dv"],
    "l0f": ["l0f_w1", "l0f_b1", "l0f_w2", "l0f_b2"],
    "l1a": ["l1a_wq", "l1a_wk", "l1a_wv", "l1a_wo", "l1a_bq", "l1a_bk",
            "l1a_bv", "l1a_bo"],
    "l1f": ["l1f_wgT", "l1f_wuT", "l1f_wdT", "l1f_wrT", "l1f_rb"],
    "l2a": ["l2a_tsg", "l2a_lnw", "l2a_lnb", "l2a_kwT", "l2a_vwT", "l2a_owT",
            "l2a_td", "l2a_tf"],
    "l2f": ["l2f_wg", "l2f_w1", "l2f_w2", "l2f_bg", "l2f_b1", "l2f_b2",
            "l2f_centT"],
    "l3a": ["l3a_normw", "l3a_winT", "l3a_cw", "l3a_cb", "l3a_xpT",
            "l3a_dtwT", "l3a_dtb", "l3a_alog", "l3a_dv", "l3a_woT"],
    "l3f": ["l3f_w1T", "l3f_w2T", "l3f_h1T", "l3f_h2T", "l3f_w3T",
            "l3f_s1", "l3f_s2", "l3f_h1s", "l3f_h2s", "l3f_s3"],
}
CONST_W = ["ident_b", "ident_f", "ones_b", "ones_f"]


def build_program(layers, wspecs, first, last):
    nc = bass.Bass("TRN2", target_bir_lowering=False)
    es = contextlib.ExitStack()
    with es:
        c = build_ctx(nc, es, wspecs, first, last)
        k = c.k
        if first:
            load_x_tmajor(c)
        else:
            for kt in range(DKT):
                k.dma("y", xt(c, kt), c.x_in[kt * 128:(kt + 1) * 128, :],
                      writes=(c.bx[kt],))
            cast_x_bf(c)
        for name in layers:
            EMIT[name](c)
        if last:
            store_x_tmajor(c)
        else:
            for kt in range(DKT):
                k.dma("y", c.x_out[kt * 128:(kt + 1) * 128, :], xt(c, kt),
                      reads=(c.bx[kt],))
        toks = k.barrier_tokens()
        k.op("y", lambda eng: eng.nop(), extra=toks)
        k.finalize()
    return nc


def kernel(**inputs):
    w = prep_weights(inputs)
    wspecs = {name: (arr.shape, BF16 if arr.dtype == BF else F32)
              for name, arr in w.items()}
    xs = [np.ascontiguousarray(np.asarray(inputs["L0_attn_x"])[i].astype(np.float32))
          for i in range(NCORES)]
    if FUSE:
        groups = [list(LAYERS)]
    else:
        groups = [[l] for l in LAYERS]
    for gi, group in enumerate(groups):
        first = gi == 0
        last = gi == len(groups) - 1
        names = list(CONST_W)
        for l in group:
            names += WNEEDS[l]
        gw = {n: w[n] for n in names}
        gs = {n: wspecs[n] for n in names}
        nc = build_program(group, gs, first, last)
        in_maps = [dict(gw, x_in=xs[i]) for i in range(NCORES)]
        res = run_bass_kernel_spmd(nc, in_maps, core_ids=list(range(NCORES)))
        xs = [np.asarray(res.results[i]["x_out"]) for i in range(NCORES)]
    out = np.stack(xs, 0).astype(np.float32)
    return out
